# version 01
# baseline (speedup 1.0000x reference)
import numpy as np
import ml_dtypes
from contextlib import ExitStack
import concourse.bass as bass
import concourse.mybir as mybir
from concourse.bass_utils import run_bass_kernel_spmd

F32 = mybir.dt.float32
BF16 = mybir.dt.bfloat16
I32 = mybir.dt.int32
AF = mybir.ActivationFunctionType
ALU = mybir.AluOpType
AX = mybir.AxisListType
BF = ml_dtypes.bfloat16

D = 4096
S = 2048
B = 4
NCORE = 8
TC = 1024
EPS = 1e-5

_prog_cache = {}
RACE_CHECK = True


def _mknc():
    return bass.Bass("TRN2", target_bir_lowering=False,
                     detect_race_conditions=RACE_CHECK)


def cached(name, builder):
    if name not in _prog_cache:
        _prog_cache[name] = builder()
    return _prog_cache[name]


def sincos_tables():
    rd = 64
    inv_freq = (1.0 / 10000.0 ** (np.arange(0, rd, 2, dtype=np.float32) / rd)).astype(np.float32)
    ang = np.arange(S, dtype=np.float32)[:, None] * inv_freq[None, :]
    sin = np.repeat(np.sin(ang), 2, axis=-1).astype(np.float32)
    cos = np.repeat(np.cos(ang), 2, axis=-1).astype(np.float32)
    sin_s = sin.copy()
    sin_s[:, 0::2] *= -1.0
    return np.concatenate([cos, sin_s], axis=1)


def swap_view(ap):
    dims = [list(d) for d in ap.ap]
    assert dims[-1][0] == 1 and dims[-1][1] % 2 == 0
    nd = dims[:-1] + [[2, dims[-1][1] // 2], [-1, 2]]
    return bass.AP(ap.tensor, ap.offset + 1, nd)


def strided(ap, start, step, count):
    dims = [list(d) for d in ap.ap]
    assert dims[-1][0] == 1
    nd = dims[:-1] + [[step, count]]
    return bass.AP(ap.tensor, ap.offset + start, nd)


class SelfSync:

    def __init__(self, sem):
        self.sem = sem
        self.n = 0

    def barrier(self, e, inst):
        inst.then_inc(self.sem, 1)
        self.n += 1
        e.wait_ge(self.sem, self.n)


def emit_ln(e, stat, xv, out_f32, ssync, eps=EPS):
    dim = xv.shape[-1]
    e.activation(out=out_f32, in_=xv, func=AF.Square, accum_out=stat[:, 1:2])
    e.activation(out=out_f32, in_=xv, func=AF.Copy, accum_out=stat[:, 0:1])
    e.mul(stat[:, 2:3], stat[:, 0:1], 1.0 / dim)
    e.mul(stat[:, 3:4], stat[:, 1:2], 1.0 / dim)
    e.activation(out=stat[:, 4:5], in_=stat[:, 2:3], func=AF.Square)
    ins = e.mul(stat[:, 4:5], stat[:, 4:5], -1.0)
    ssync.barrier(e, ins)
    e.activation(out=stat[:, 3:4], in_=stat[:, 3:4], func=AF.Identity, bias=stat[:, 4:5])
    e.activation(out=stat[:, 3:4], in_=stat[:, 3:4], func=AF.Copy, bias=eps)
    e.activation(out=stat[:, 6:7], in_=stat[:, 3:4], func=AF.Ln)
    ins = e.activation(out=stat[:, 6:7], in_=stat[:, 6:7], func=AF.Exp, scale=-0.5)
    ssync.barrier(e, ins)
    e.mul(stat[:, 7:8], stat[:, 2:3], stat[:, 6:7])
    ins = e.mul(stat[:, 7:8], stat[:, 7:8], -1.0)
    ssync.barrier(e, ins)
    return e.activation(out=out_f32, in_=xv, func=AF.Identity,
                        scale=stat[:, 6:7], bias=stat[:, 7:8])


def build_l1_qkv():
    nc = _mknc()
    x = nc.dram_tensor("x", [TC, D], F32, kind="ExternalInput").ap()
    wT = nc.dram_tensor("wT", [D, 3 * D], F32, kind="ExternalInput").ap()
    sc = nc.dram_tensor("sc", [TC, 128], F32, kind="ExternalInput").ap()
    qw = nc.dram_tensor("qw", [512, 3072], I32, kind="ExternalInput").ap()
    zp = nc.dram_tensor("zp", [512, 2], F32, kind="ExternalInput").ap()
    iden = nc.dram_tensor("iden", [128, 128], F32, kind="ExternalInput").ap()
    o_qkv = nc.dram_tensor("o_qkv", [TC, 3 * D], BF16, kind="ExternalOutput").ap()
    o_xn = nc.dram_tensor("o_xn", [TC, D], F32, kind="ExternalOutput").ap()
    o_wd = nc.dram_tensor("o_wd", [512, D], BF16, kind="ExternalOutput").ap()

    NT = TC // 128
    NK = D // 128
    NO = 3 * D // 512
    NG = NO * NK

    es = ExitStack()
    sbuf = lambda n, s, d=F32: es.enter_context(nc.sbuf_tensor(n, s, d))
    sem = lambda n: es.enter_context(nc.semaphore(n))
    with es:
        xt = sbuf("xt", [128, 2 * D])
        xnf = sbuf("xnf", [128, 2 * D])
        xnT = sbuf("xnT", [128, NK * TC], BF16)
        sctab = sbuf("sctab", [128, NT * 128])
        sid = sbuf("sid", [128, 128])
        wbuf = sbuf("wbuf", [128, 2 * 512])
        wbf = sbuf("wbf", [128, 2 * 512], BF16)
        stat = sbuf("stat", [128, 8])
        otile = sbuf("otile", [128, 4 * 512], BF16)
        rot = sbuf("rot", [128, 2 * 64])
        qwt = es.enter_context(nc.sbuf_tensor("qwt", [128, 2 * 3072], I32))
        uw = es.enter_context(nc.sbuf_tensor("uw", [128, 2048], I32))
        uwb = es.enter_context(nc.sbuf_tensor("uwb", [128, 512], I32))
        uf = sbuf("uf", [128, 2048])
        zps = sbuf("zps", [128, 2 * 4])
        wdt = sbuf("wdt", [128, 2 * D], BF16)
        psm = es.enter_context(nc.psum_tensor("psm", [128, 8 * 512], F32))

        s_xld = [sem("s_xld0"), sem("s_xld1")]
        s_cld = sem("s_cld")
        s_qld = [sem("s_qld0"), sem("s_qld1")]
        s_xn = sem("s_xn")
        s_tr = sem("s_tr")
        s_ev = sem("s_ev")
        s_w = [sem("s_w0"), sem("s_w1")]
        s_wb = sem("s_wb")
        s_mm = sem("s_mm")
        s_cp = sem("s_cp")
        s_rot = sem("s_rot")
        s_out = [sem(f"s_out{j}") for j in range(4)]
        s_xout = [sem("s_xout0"), sem("s_xout1")]
        s_dq = sem("s_dq")
        s_dqo = [sem("s_dqo0"), sem("s_dqo1")]
        s_ss = sem("s_ss")

        with nc.Block() as blk:
            @blk.gpsimd
            def _(e):
                e.dma_start(out=sid[:], in_=iden).then_inc(s_cld, 16)
                e.dma_start(out=sctab[:].rearrange("p (t c) -> p t c", c=128),
                            in_=sc.rearrange("(t p) c -> p t c", p=128)).then_inc(s_cld, 16)
                e.dma_start(out=zps[:].rearrange("p (t c) -> p t c", c=2),
                            in_=zp.rearrange("(t p) c -> p t c", p=128)).then_inc(s_cld, 16)
                e.dma_start(out=xt[:, 0:D], in_=x[0:128, :]).then_inc(s_xld[0], 16)
                e.dma_start(out=xt[:, D:2 * D], in_=x[128:256, :]).then_inc(s_xld[1], 16)
                for m in range(NT):
                    e.wait_ge(s_xn, m + 1)
                    e.dma_start(out=o_xn[m * 128:(m + 1) * 128, :],
                                in_=xnf[:, (m % 2) * D:(m % 2 + 1) * D]
                                ).then_inc(s_xout[m % 2], 16)
                    if m + 2 < NT:
                        e.dma_start(out=xt[:, (m % 2) * D:(m % 2 + 1) * D],
                                    in_=x[(m + 2) * 128:(m + 3) * 128, :]
                                    ).then_inc(s_xld[m % 2], 16)
                for i in range(NO * NT):
                    oc, m = divmod(i, NT)
                    if oc < 16:
                        e.wait_ge(s_rot, i + 1)
                    e.wait_ge(s_cp, i + 1)
                    e.dma_start(out=o_qkv[m * 128:(m + 1) * 128, oc * 512:(oc + 1) * 512],
                                in_=otile[:, (i % 4) * 512:(i % 4 + 1) * 512]
                                ).then_inc(s_out[i % 4], 16)
                e.dma_start(out=qwt[:, 0:3072], in_=qw[0:128, :]).then_inc(s_qld[0], 16)
                e.dma_start(out=qwt[:, 3072:6144], in_=qw[128:256, :]).then_inc(s_qld[1], 16)
                for t in range(4):
                    e.wait_ge(s_dq, (t + 1) * 10)
                    e.dma_start(out=o_wd[t * 128:(t + 1) * 128, :],
                                in_=wdt[:, (t % 2) * D:(t % 2 + 1) * D]
                                ).then_inc(s_dqo[t % 2], 16)
                    if t + 2 < 4:
                        e.dma_start(out=qwt[:, (t % 2) * 3072:(t % 2 + 1) * 3072],
                                    in_=qw[(t + 2) * 128:(t + 3) * 128, :]
                                    ).then_inc(s_qld[t % 2], 16)
                for j in range(4):
                    e.wait_ge(s_out[j], 16 * (NO * NT // 4))
                e.wait_ge(s_dqo[0], 32)
                e.wait_ge(s_dqo[1], 32)
                e.wait_ge(s_xout[0], 16 * 4)
                e.wait_ge(s_xout[1], 16 * 4)

            @blk.sync
            def _(e):
                for g in range(NG):
                    buf = g % 2
                    if g >= 2:
                        e.wait_ge(s_wb, g - 1)
                    oc, k = divmod(g, NK)
                    e.dma_start(out=wbuf[:, buf * 512:(buf + 1) * 512],
                                in_=wT[k * 128:(k + 1) * 128, oc * 512:(oc + 1) * 512]
                                ).then_inc(s_w[buf], 16)

            @blk.scalar
            def _(e):
                ssync = SelfSync(s_ss)
                e.wait_ge(s_cld, 48)
                for m in range(NT):
                    e.wait_ge(s_xld[m % 2], 16 * (m // 2 + 1))
                    if m >= 2:
                        e.wait_ge(s_tr, (m - 1) * NK)
                        e.wait_ge(s_xout[m % 2], 16 * (m // 2))
                    emit_ln(e, stat, xt[:, (m % 2) * D:(m % 2 + 1) * D],
                            xnf[:, (m % 2) * D:(m % 2 + 1) * D], ssync)
                    e.memzero(stat[:, 0:1]).then_inc(s_xn, 1)
                for i in range(NO * NT):
                    oc, m = divmod(i, NT)
                    if m == 0:
                        e.wait_ge(s_mm, (oc + 1) * NK)
                    if i >= 4:
                        e.wait_ge(s_out[i % 4], 16 * ((i - 4) // 4 + 1))
                    ot = otile[:, (i % 4) * 512:(i % 4 + 1) * 512]
                    pv = psm[:, m * 512:(m + 1) * 512]
                    if oc < 16:
                        e.activation(out=ot[:, 64:256], in_=pv[:, 64:256], func=AF.Copy)
                        e.activation(out=ot[:, 320:512], in_=pv[:, 320:512], func=AF.Copy)
                    else:
                        e.activation(out=ot, in_=pv, func=AF.Copy)
                    e.memzero(stat[:, 0:1]).then_inc(s_cp, 1)

            @blk.tensor
            def _(e):
                for i in range(NT * NK):
                    m, k = divmod(i, NK)
                    if k == 0:
                        e.wait_ge(s_xn, m + 1)
                    if i >= 8:
                        e.wait_ge(s_ev, i - 7)
                    e.transpose(psm[:, (i % 8) * 512:(i % 8) * 512 + 128],
                                xnf[:, (m % 2) * D + k * 128:(m % 2) * D + k * 128 + 128],
                                sid[:]).then_inc(s_tr, 1)
                e.wait_ge(s_ev, NT * NK + 1)
                for g in range(NG):
                    oc, k = divmod(g, NK)
                    buf = g % 2
                    e.wait_ge(s_wb, g + 1)
                    if k == 0 and oc > 0:
                        e.wait_ge(s_cp, oc * NT)
                        if oc <= 16:
                            e.wait_ge(s_rot, min(oc, 16) * NT)
                    for m in range(NT):
                        e.matmul(psm[:, m * 512:(m + 1) * 512],
                                 xnT[:, k * TC + m * 128: k * TC + m * 128 + 128],
                                 wbf[:, buf * 512:(buf + 1) * 512],
                                 start=(k == 0), stop=(k == NK - 1)).then_inc(
                                     s_mm, 1) if m == NT - 1 else e.matmul(
                            psm[:, m * 512:(m + 1) * 512],
                            xnT[:, k * TC + m * 128: k * TC + m * 128 + 128],
                            wbf[:, buf * 512:(buf + 1) * 512],
                            start=(k == 0), stop=(k == NK - 1))

            def rotary(e, oc):
                e.wait_ge(s_mm, (oc + 1) * NK)
                for m in range(NT):
                    i = oc * NT + m
                    e.wait_ge(s_cp, i + 1)
                    cosv = sctab[:, m * 128: m * 128 + 64]
                    sinv = sctab[:, m * 128 + 64: m * 128 + 128]
                    for r in (0, 256):
                        pv = psm[:, m * 512 + r: m * 512 + r + 64]
                        e.tensor_tensor(rot[:, 0:64], pv, cosv, ALU.mult)
                        e.tensor_tensor(rot[:, 64:128], swap_view(pv), sinv, ALU.mult)
                        e.tensor_tensor(
                            otile[:, (i % 4) * 512 + r:(i % 4) * 512 + r + 64],
                            rot[:, 0:64], rot[:, 64:128], ALU.add)
                    e.memset(rot[:, 0:1], 0.0).then_inc(s_rot, 1)

            @blk.vector
            def _(e):
                for i in range(NT * NK):
                    m, k = divmod(i, NK)
                    e.wait_ge(s_tr, i + 1)
                    e.tensor_copy(xnT[:, k * TC + m * 128: k * TC + m * 128 + 128],
                                  psm[:, (i % 8) * 512:(i % 8) * 512 + 128]).then_inc(s_ev, 1)
                e.memset(rot[:, 0:1], 0.0).then_inc(s_ev, 1)
                for oc in range(NO):
                    if 1 <= oc <= 16:
                        rotary(e, oc - 1)
                    for k in range(NK):
                        g = oc * NK + k
                        buf = g % 2
                        e.wait_ge(s_w[buf], 16 * (g // 2 + 1))
                        if g >= 2:
                            e.wait_ge(s_mm, g - 1)
                        e.tensor_copy(wbf[:, buf * 512:(buf + 1) * 512],
                                      wbuf[:, buf * 512:(buf + 1) * 512])
                        e.memset(rot[:, 0:1], 0.0).then_inc(s_wb, 1)
                for t in range(4):
                    e.wait_ge(s_qld[t % 2], 16 * (t // 2 + 1))
                    if t >= 2:
                        e.wait_ge(s_dqo[t % 2], 16 * (t // 2))
                    qv = qwt[:, (t % 2) * 3072:(t % 2 + 1) * 3072]
                    wv = wdt[:, (t % 2) * D:(t % 2 + 1) * D]
                    for h in range(2):
                        e.tensor_single_scalar(strided(uw[:], 0, 4, 512),
                                               strided(qv, h * 1536 + 0, 3, 512),
                                               63, ALU.bitwise_and)
                        e.tensor_single_scalar(uwb[:],
                                               strided(qv, h * 1536 + 0, 3, 512),
                                               6, ALU.logical_shift_right)
                        e.tensor_single_scalar(strided(uw[:], 1, 4, 512), uwb[:],
                                               63, ALU.bitwise_and)
                        e.tensor_single_scalar(strided(uw[:], 2, 4, 512),
                                               strided(qv, h * 1536 + 1, 3, 512),
                                               63, ALU.bitwise_and)
                        e.tensor_single_scalar(strided(uw[:], 3, 4, 512),
                                               strided(qv, h * 1536 + 2, 3, 512),
                                               63, ALU.bitwise_and)
                        e.tensor_copy(uf[:], uw[:])
                        e.tensor_scalar(wv[:, h * 2048:(h + 1) * 2048], uf[:],
                                        zps[:, 2 * t:2 * t + 1], zps[:, 2 * t + 1:2 * t + 2],
                                        ALU.subtract, ALU.mult).then_inc(s_dq, 4 if h == 0 else 5)
                    e.memset(rot[:, 0:1], 0.0).then_inc(s_dq, 1)
    return nc


def build_l2_attn():
    nc = _mknc()
    qT = nc.dram_tensor("qT", [8, 256, S], BF16, kind="ExternalInput").ap()
    kT = nc.dram_tensor("kT", [8, 256, S], BF16, kind="ExternalInput").ap()
    v = nc.dram_tensor("v", [8, S, 256], BF16, kind="ExternalInput").ap()
    cm = nc.dram_tensor("cm", [4, 128, 512], F32, kind="ExternalInput").ap()
    iden = nc.dram_tensor("iden", [128, 128], BF16, kind="ExternalInput").ap()
    o_ao = nc.dram_tensor("o_ao", [S, 2048], BF16, kind="ExternalOutput").ap()

    NQT = S // 128
    NS = 8 * NQT

    def nk512(qt):
        return qt // 4 + 1

    PT_BASE = [0]
    for s in range(NS):
        PT_BASE.append(PT_BASE[-1] + (s % NQT) + 1)

    es = ExitStack()
    sbuf = lambda n, s_, d=F32: es.enter_context(nc.sbuf_tensor(n, s_, d))
    sem = lambda n: es.enter_context(nc.semaphore(n))
    with es:
        qs = sbuf("qs", [128, 2 * 2 * S], BF16)
        ks = sbuf("ks", [128, 2 * 2 * S], BF16)
        vs = sbuf("vs", [128, 2 * 16 * 256], BF16)
        cmk = sbuf("cmk", [128, 4 * 512])
        sid = sbuf("sid", [128, 128], BF16)
        sc_sb = sbuf("sc_sb", [128, 4 * 512])
        p_sb = sbuf("p_sb", [128, 4 * 512], BF16)
        pT_sb = sbuf("pT_sb", [128, 2 * 128], BF16)
        cols = sbuf("cols", [128, 4])
        ao_sb = sbuf("ao_sb", [128, NQT * 2048], BF16)
        psc = es.enter_context(nc.psum_tensor("psc", [128, 4 * 512], F32))
        ptp = es.enter_context(nc.psum_tensor("ptp", [128, 2 * 1024], BF16))
        pav = es.enter_context(nc.psum_tensor("pav", [128, 2 * 512], F32))

        s_cld = sem("s_cld")
        s_hl = [sem("s_hl0"), sem("s_hl1")]
        s_mm = sem("s_mm")
        s_scv = sem("s_scv")
        s_ex = sem("s_ex")
        s_rc = sem("s_rc")
        s_pt = sem("s_pt")
        s_pte = sem("s_pte")
        s_av = sem("s_av")
        s_ae = sem("s_ae")
        s_out = sem("s_out")

        with nc.Block() as blk:
            @blk.sync
            def _(e):
                e.dma_start(out=sid[:], in_=iden).then_inc(s_cld, 16)
                e.dma_start(out=cmk[:].rearrange("p (r c) -> p r c", c=512),
                            in_=cm.rearrange("r p c -> p r c")).then_inc(s_cld, 16)
                for h in range(8):
                    sl = h % 2
                    if h >= 2:
                        e.wait_ge(s_ae, (h - 1) * NQT)
                    e.dma_start(out=qs[:, sl * 2 * S: (sl + 1) * 2 * S]
                                .rearrange("p (c t) -> p c t", t=S),
                                in_=qT[h].rearrange("(c p) t -> p c t", p=128)
                                ).then_inc(s_hl[sl], 16)
                    e.dma_start(out=ks[:, sl * 2 * S: (sl + 1) * 2 * S]
                                .rearrange("p (c t) -> p c t", t=S),
                                in_=kT[h].rearrange("(c p) t -> p c t", p=128)
                                ).then_inc(s_hl[sl], 16)
                    e.dma_start(out=vs[:, sl * 16 * 256: (sl + 1) * 16 * 256]
                                .rearrange("p (c t) -> p c t", t=256),
                                in_=v[h].rearrange("(c p) t -> p c t", p=128)
                                ).then_inc(s_hl[sl], 16)

            def ptav(e, s):
                h, qt = divmod(s, NQT)
                sl = h % 2
                vv = vs[:, sl * 16 * 256: (sl + 1) * 16 * 256]
                e.wait_ge(s_ex, s + 1)
                if s >= 2:
                    e.wait_ge(s_ae, s - 1)
                for kc in range(qt + 1):
                    npt = PT_BASE[s] + kc
                    bank = npt % 2
                    if npt >= 2:
                        e.wait_ge(s_pte, npt - 1)
                    e.transpose(ptp[:, bank * 1024: bank * 1024 + 128],
                                p_sb[:, kc * 128:(kc + 1) * 128],
                                sid[:]).then_inc(s_pt, 1)
                    e.wait_ge(s_pte, npt + 1)
                    mmi = e.matmul(pav[:, (s % 2) * 512:(s % 2) * 512 + 256],
                                   pT_sb[:, bank * 128:(bank + 1) * 128],
                                   vv[:, kc * 256:(kc + 1) * 256],
                                   start=(kc == 0), stop=(kc == qt))
                mmi.then_inc(s_av, 1)

            @blk.tensor
            def _(e):
                for s in range(NS):
                    h, qt = divmod(s, NQT)
                    sl = h % 2
                    if qt == 0:
                        e.wait_ge(s_hl[sl], 48 * (h // 2 + 1))
                    qv = qs[:, sl * 2 * S: (sl + 1) * 2 * S]
                    kv = ks[:, sl * 2 * S: (sl + 1) * 2 * S]
                    e.wait_ge(s_scv, s)
                    nk = nk512(qt)
                    mmi = None
                    for kc in range(nk):
                        for half in range(2):
                            mmi = e.matmul(
                                psc[:, kc * 512:(kc + 1) * 512],
                                qv[:, half * S + qt * 128: half * S + qt * 128 + 128],
                                kv[:, half * S + kc * 512: half * S + kc * 512 + 512],
                                start=(half == 0), stop=(half == 1))
                    mmi.then_inc(s_mm, 1)
                    if s >= 1:
                        ptav(e, s - 1)
                ptav(e, NS - 1)

            @blk.vector
            def _(e):
                e.wait_ge(s_cld, 32)
                for s in range(NS):
                    h, qt = divmod(s, NQT)
                    nk = nk512(qt)
                    W = nk * 512
                    e.wait_ge(s_mm, s + 1)
                    if s >= 1:
                        e.wait_ge(s_ex, s)
                    for kc in range(nk):
                        dst = sc_sb[:, kc * 512:(kc + 1) * 512]
                        pv = psc[:, kc * 512:(kc + 1) * 512]
                        if kc == nk - 1:
                            e.scalar_tensor_tensor(dst, pv, 1.0 / 16.0,
                                                   cmk[:, (qt % 4) * 512:(qt % 4 + 1) * 512],
                                                   ALU.mult, ALU.add)
                        else:
                            e.tensor_single_scalar(dst, pv, 1.0 / 16.0, ALU.mult)
                    e.reduce_max(cols[:, 0:1], sc_sb[:, 0:W], axis=AX.X, negate=True)
                    if s >= 1:
                        e.reciprocal(cols[:, 2:3], cols[:, 1:2])
                        e.memset(cols[:, 3:4], 0.0).then_inc(s_rc, 1)
                    e.memset(cols[:, 3:4], 0.0).then_inc(s_scv, 1)
                    if s >= 1:
                        h1, qt1 = divmod(s - 1, NQT)
                        for kc in range(qt1 + 1):
                            npt = PT_BASE[s - 1] + kc
                            bank = npt % 2
                            e.wait_ge(s_pt, npt + 1)
                            e.tensor_copy(pT_sb[:, bank * 128:(bank + 1) * 128],
                                          ptp[:, bank * 1024: bank * 1024 + 128])
                            e.memset(cols[:, 3:4], 0.0).then_inc(s_pte, 1)
                e.wait_ge(s_ex, NS)
                e.reciprocal(cols[:, 2:3], cols[:, 1:2])
                e.memset(cols[:, 3:4], 0.0).then_inc(s_rc, 1)
                h1, qt1 = divmod(NS - 1, NQT)
                for kc in range(qt1 + 1):
                    npt = PT_BASE[NS - 1] + kc
                    bank = npt % 2
                    e.wait_ge(s_pt, npt + 1)
                    e.tensor_copy(pT_sb[:, bank * 128:(bank + 1) * 128],
                                  ptp[:, bank * 1024: bank * 1024 + 128])
                    e.memset(cols[:, 3:4], 0.0).then_inc(s_pte, 1)

            @blk.scalar
            def _(e):
                for s in range(NS):
                    h, qt = divmod(s, NQT)
                    W = nk512(qt) * 512
                    e.wait_ge(s_scv, s + 1)
                    if s >= 1:
                        e.wait_ge(s_pt, PT_BASE[s])
                    e.activation(out=p_sb[:, 0:W], in_=sc_sb[:, 0:W], func=AF.Exp,
                                 bias=cols[:, 0:1], accum_out=cols[:, 1:2])
                    e.memzero(cols[:, 3:4]).then_inc(s_ex, 1)
                    if s >= 1:
                        h1, qt1 = divmod(s - 1, NQT)
                        e.wait_ge(s_av, s)
                        e.wait_ge(s_rc, s)
                        e.activation(out=ao_sb[:, qt1 * 2048 + h1 * 256: qt1 * 2048 + h1 * 256 + 256],
                                     in_=pav[:, ((s - 1) % 2) * 512:((s - 1) % 2) * 512 + 256],
                                     func=AF.Identity, scale=cols[:, 2:3])
                        e.memzero(cols[:, 3:4]).then_inc(s_ae, 1)
                h1, qt1 = divmod(NS - 1, NQT)
                e.wait_ge(s_av, NS)
                e.wait_ge(s_rc, NS)
                e.activation(out=ao_sb[:, qt1 * 2048 + h1 * 256: qt1 * 2048 + h1 * 256 + 256],
                             in_=pav[:, ((NS - 1) % 2) * 512:((NS - 1) % 2) * 512 + 256],
                             func=AF.Identity, scale=cols[:, 2:3])
                e.memzero(cols[:, 3:4]).then_inc(s_ae, 1)

            @blk.gpsimd
            def _(e):
                e.wait_ge(s_ae, NS)
                e.dma_start(out=o_ao.rearrange("(t p) c -> p t c", p=128),
                            in_=ao_sb[:].rearrange("p (t c) -> p t c", c=2048)
                            ).then_inc(s_out, 16)
                e.wait_ge(s_out, 16)
    return nc


def run_l2_attn(qkv):
    nc = cached("l2_attn", build_l2_attn)
    q = qkv[:, :D].reshape(B, S, 16, 256)
    k = qkv[:, D:2 * D].reshape(B, S, 16, 256)
    vv = qkv[:, 2 * D:].reshape(B, S, 16, 256)
    cmask = np.zeros((4, 128, 512), np.float32)
    for r in range(4):
        qpos = r * 128 + np.arange(128)[:, None]
        cmask[r] = np.where(qpos >= np.arange(512)[None, :], 0.0, -1e9)
    iden = np.eye(128, dtype=BF)
    in_maps = []
    for c in range(NCORE):
        b, hg = divmod(c, 2)
        hs = slice(hg * 8, hg * 8 + 8)
        in_maps.append({
            "qT": np.ascontiguousarray(q[b, :, hs, :].transpose(1, 2, 0)),
            "kT": np.ascontiguousarray(k[b, :, hs, :].transpose(1, 2, 0)),
            "v": np.ascontiguousarray(vv[b, :, hs, :].transpose(1, 0, 2)),
            "cm": cmask, "iden": iden,
        })
    res = run_bass_kernel_spmd(nc, in_maps, core_ids=list(range(NCORE)))
    ao = np.zeros((B, S, 16, 256), BF)
    for c in range(NCORE):
        b, hg = divmod(c, 2)
        ao[b, :, hg * 8:hg * 8 + 8, :] = np.asarray(
            res.results[c]["o_ao"]).reshape(S, 8, 256)
    return ao.reshape(B * S, D)


def build_l3_mix():
    nc = _mknc()
    aoT = nc.dram_tensor("aoT", [D, TC], BF16, kind="ExternalInput").ap()
    xnT = nc.dram_tensor("xnT", [D, TC], F32, kind="ExternalInput").ap()
    woT = nc.dram_tensor("woT", [D, D], F32, kind="ExternalInput").ap()
    wdT = nc.dram_tensor("wdT", [D, D], BF16, kind="ExternalInput").ap()
    bias = nc.dram_tensor("bias", [1, D], F32, kind="ExternalInput").ap()
    x = nc.dram_tensor("x", [TC, D], F32, kind="ExternalInput").ap()
    o_x1 = nc.dram_tensor("o_x1", [TC, D], F32, kind="ExternalOutput").ap()

    NT = TC // 128
    NK = D // 128
    NO = D // 512

    es = ExitStack()
    sbuf = lambda n, s_, d=F32: es.enter_context(nc.sbuf_tensor(n, s_, d))
    sem = lambda n: es.enter_context(nc.semaphore(n))
    with es:
        aots = sbuf("aots", [128, 2 * TC], BF16)
        xnts = sbuf("xnts", [128, NK * TC], BF16)
        xcht = sbuf("xcht", [128, 2 * TC])
        wob = sbuf("wob", [128, 2 * 512])
        wobf = sbuf("wobf", [128, 2 * 512], BF16)
        wdb = sbuf("wdb", [128, 2 * 512], BF16)
        brow = sbuf("brow", [1, D])
        brow_bf = sbuf("brow_bf", [1, D], BF16)
        onesr = sbuf("onesr", [1, 128], BF16)
        s_sb = sbuf("s_sb", [128, NT * 512])
        g_sb = sbuf("g_sb", [128, NT * 512])
        xtl = sbuf("xtl", [128, 4 * 512])
        x1t = sbuf("x1t", [128, 4 * 512])
        junk = sbuf("junk", [128, 4])
        psm = es.enter_context(nc.psum_tensor("psm", [128, 8 * 512], F32))

        s_cld = sem("s_cld")
        s_wol = [sem("s_wol0"), sem("s_wol1")]
        s_aol = [sem("s_aol0"), sem("s_aol1")]
        s_wdl = [sem("s_wdl0"), sem("s_wdl1")]
        s_xtl_ = [sem("s_xtl0"), sem("s_xtl1")]
        s_xl = [sem(f"s_xl{j}") for j in range(4)]
        s_wc = sem("s_wc")
        s_xc = sem("s_xc")
        s_ms = sem("s_ms")
        s_mg = sem("s_mg")
        s_sev = sem("s_sev")
        s_sil = sem("s_sil")
        s_cmb = sem("s_cmb")
        s_st = [sem(f"s_st{j}") for j in range(4)]

        with nc.Block() as blk:
            @blk.sync
            def _(e):
                e.dma_start(out=brow[:], in_=bias).then_inc(s_cld, 16)

                for c in range(NK):
                    if c >= 2:
                        e.wait_ge(s_xc, c - 1)
                    e.dma_start(out=xcht[:, (c % 2) * TC:(c % 2 + 1) * TC],
                                in_=xnT[c * 128:(c + 1) * 128, :]).then_inc(s_xtl_[c % 2], 16)
                for oc in range(NO):
                    for k in range(NK):
                        g = oc * NK + k
                        if g >= 2:
                            e.wait_ge(s_wc, g - 1)
                        e.dma_start(out=wob[:, (g % 2) * 512:(g % 2 + 1) * 512],
                                    in_=woT[k * 128:(k + 1) * 128, oc * 512:(oc + 1) * 512]
                                    ).then_inc(s_wol[g % 2], 16)
                        if g >= 2:
                            e.wait_ge(s_ms, g - 1)
                        e.dma_start(out=aots[:, (k % 2) * TC:(k % 2 + 1) * TC],
                                    in_=aoT[k * 128:(k + 1) * 128, :]
                                    ).then_inc(s_aol[k % 2], 16)
                    for k in range(NK):
                        g = oc * NK + k
                        if g >= 2:
                            e.wait_ge(s_mg, 33 * ((g - 2) // 32) + min((g - 2) % 32 + 1, 32))
                        e.dma_start(out=wdb[:, (g % 2) * 512:(g % 2 + 1) * 512],
                                    in_=wdT[k * 128:(k + 1) * 128, oc * 512:(oc + 1) * 512]
                                    ).then_inc(s_wdl[g % 2], 16)

            @blk.gpsimd
            def _(e):
                def xload(i):
                    oc, m = divmod(i, NT)
                    e.dma_start(out=xtl[:, (i % 4) * 512:(i % 4 + 1) * 512],
                                in_=x[m * 128:(m + 1) * 128, oc * 512:(oc + 1) * 512]
                                ).then_inc(s_xl[i % 4], 16)
                for i in range(4):
                    xload(i)
                for i in range(NO * NT):
                    oc, m = divmod(i, NT)
                    e.wait_ge(s_cmb, i + 1)
                    e.dma_start(out=o_x1[m * 128:(m + 1) * 128, oc * 512:(oc + 1) * 512],
                                in_=x1t[:, (i % 4) * 512:(i % 4 + 1) * 512]
                                ).then_inc(s_st[i % 4], 16)
                    if i + 4 < NO * NT:
                        xload(i + 4)

            @blk.vector
            def _(e):
                for c in range(NK):
                    e.wait_ge(s_xtl_[c % 2], 16 * (c // 2 + 1))
                    e.tensor_copy(xnts[:, c * TC:(c + 1) * TC],
                                  xcht[:, (c % 2) * TC:(c % 2 + 1) * TC])
                    e.memset(junk[:, 0:1], 0.0).then_inc(s_xc, 1)
                e.wait_ge(s_cld, 16)
                e.tensor_copy(brow_bf[0:1, :], brow[0:1, :])
                e.memset(onesr[0:1, :].bitcast(mybir.dt.uint16), 0x3F80)
                e.memset(junk[:, 0:1], 0.0).then_inc(s_cld, 1)
                for oc in range(NO):
                    for k in range(NK):
                        g = oc * NK + k
                        e.wait_ge(s_wol[g % 2], 16 * (g // 2 + 1))
                        if g >= 2:
                            e.wait_ge(s_ms, g - 1)
                        e.tensor_copy(wobf[:, (g % 2) * 512:(g % 2 + 1) * 512],
                                      wob[:, (g % 2) * 512:(g % 2 + 1) * 512])
                        e.memset(junk[:, 0:1], 0.0).then_inc(s_wc, 1)
                    for m in range(NT):
                        if m == 0:
                            e.wait_ge(s_ms, 32 * (oc + 1))
                        e.tensor_copy(s_sb[:, m * 512:(m + 1) * 512],
                                      psm[:, m * 512:(m + 1) * 512])
                        e.memset(junk[:, 0:1], 0.0).then_inc(s_sev, 1)
                    for m in range(NT):
                        i = oc * NT + m
                        e.wait_ge(s_sil, i + 1)
                        e.wait_ge(s_xl[i % 4], 16 * (i // 4 + 1))
                        if i >= 4:
                            e.wait_ge(s_st[i % 4], 16 * ((i - 4) // 4 + 1))
                        e.tensor_mul(x1t[:, (i % 4) * 512:(i % 4 + 1) * 512],
                                     s_sb[:, m * 512:(m + 1) * 512],
                                     g_sb[:, m * 512:(m + 1) * 512])
                        e.tensor_add(x1t[:, (i % 4) * 512:(i % 4 + 1) * 512],
                                     x1t[:, (i % 4) * 512:(i % 4 + 1) * 512],
                                     xtl[:, (i % 4) * 512:(i % 4 + 1) * 512])
                        e.memset(junk[:, 0:1], 0.0).then_inc(s_cmb, 1)

            @blk.tensor
            def _(e):
                e.wait_ge(s_xc, NK)
                e.wait_ge(s_cld, 1)
                for oc in range(NO):
                    for k in range(NK):
                        g = oc * NK + k
                        e.wait_ge(s_wc, g + 1)
                        e.wait_ge(s_aol[k % 2], 16 * (g // 2 + 1))
                        if k == 0 and oc > 0:
                            e.wait_ge(s_sil, oc * NT)
                        for m in range(NT):
                            mmi = e.matmul(psm[:, m * 512:(m + 1) * 512],
                                           aots[:, (k % 2) * TC + m * 128:(k % 2) * TC + m * 128 + 128],
                                           wobf[:, (g % 2) * 512:(g % 2 + 1) * 512],
                                           start=(k == 0), stop=(k == NK - 1))
                        mmi.then_inc(s_ms, 1)
                    for k in range(NK):
                        g = oc * NK + k
                        e.wait_ge(s_wdl[g % 2], 16 * (g // 2 + 1))
                        if k == 0:
                            e.wait_ge(s_sev, (oc + 1) * NT)
                        for m in range(NT):
                            mmi = e.matmul(psm[:, m * 512:(m + 1) * 512],
                                           xnts[:, k * TC + m * 128: k * TC + m * 128 + 128],
                                           wdb[:, (g % 2) * 512:(g % 2 + 1) * 512],
                                           start=(k == 0), stop=False)
                        mmi.then_inc(s_mg, 1)
                    for m in range(NT):
                        mmi = e.matmul(psm[:, m * 512:(m + 1) * 512],
                                       onesr[0:1, 0:128],
                                       brow_bf[0:1, oc * 512:(oc + 1) * 512],
                                       start=False, stop=True)
                    mmi.then_inc(s_mg, 1)

            @blk.scalar
            def _(e):
                for oc in range(NO):
                    for m in range(NT):
                        i = oc * NT + m
                        if m == 0:
                            e.wait_ge(s_mg, 33 * (oc + 1))
                        e.activation(out=g_sb[:, m * 512:(m + 1) * 512],
                                     in_=psm[:, m * 512:(m + 1) * 512], func=AF.Silu)
                        e.memzero(junk[:, 1:2]).then_inc(s_sil, 1)
    return nc


def run_l3_mix(x_flat, ao, xn, wdeq, wo, bias0):
    nc = cached("l3_mix", build_l3_mix)
    aoT_full = np.ascontiguousarray(ao.T)
    xnT_full = np.ascontiguousarray(xn.T)
    woT = np.ascontiguousarray(wo.T.astype(np.float32))
    wdT = np.ascontiguousarray(wdeq.T)
    brow = np.ascontiguousarray(bias0.reshape(1, D).astype(np.float32))
    in_maps = []
    for c in range(NCORE):
        sl = slice(c * TC, (c + 1) * TC)
        in_maps.append({
            "aoT": np.ascontiguousarray(aoT_full[:, sl]),
            "xnT": np.ascontiguousarray(xnT_full[:, sl]),
            "woT": woT, "wdT": wdT, "bias": brow,
            "x": np.ascontiguousarray(x_flat[sl]),
        })
    res = run_bass_kernel_spmd(nc, in_maps, core_ids=list(range(NCORE)))
    return np.concatenate([np.asarray(r["o_x1"]) for r in res.results], 0)


def build_l4_proj():
    nc = _mknc()
    NTH = 9
    TH = NTH * 128
    NK = D // 128
    NOG = 145
    x1h = nc.dram_tensor("x1h", [TH, D], F32, kind="ExternalInput").ap()
    ipwT = nc.dram_tensor("ipwT", [D, 18560], F32, kind="ExternalInput").ap()
    nwbc = nc.dram_tensor("nwbc", [128, D], F32, kind="ExternalInput").ap()
    cw = nc.dram_tensor("cw", [128, 80 * 4], F32, kind="ExternalInput").ap()
    cb = nc.dram_tensor("cb", [128, 80], F32, kind="ExternalInput").ap()
    dtb = nc.dram_tensor("dtb", [128, 1], F32, kind="ExternalInput").ap()
    iden = nc.dram_tensor("iden", [128, 128], BF16, kind="ExternalInput").ap()
    o_xn1 = nc.dram_tensor("o_xn1", [TC, D], BF16, kind="ExternalOutput").ap()
    o_gate = nc.dram_tensor("o_gate", [8192, TC], BF16, kind="ExternalOutput").ap()
    o_hbc = nc.dram_tensor("o_hbc", [10240, TC], BF16, kind="ExternalOutput").ap()
    o_dt = nc.dram_tensor("o_dt", [128, TC], F32, kind="ExternalOutput").ap()

    es = ExitStack()
    sbuf = lambda n, s_, d=F32: es.enter_context(nc.sbuf_tensor(n, s_, d))
    sem = lambda n: es.enter_context(nc.semaphore(n))
    with es:
        xt = sbuf("xt", [128, 2 * D])
        xnb = sbuf("xnb", [128, 2 * D], BF16)
        hbf = sbuf("hbf", [128, 2 * D], BF16)
        hT = sbuf("hT", [128, NK * TH], BF16)
        nwbb = sbuf("nwbb", [128, D], BF16)
        wip = sbuf("wip", [128, 2 * 2048])
        wipb = sbuf("wipb", [128, 2 * D], BF16)
        stat = sbuf("stat", [128, 8])
        pre = sbuf("pre", [128, 2 * TH])
        acc = sbuf("acc", [128, 2 * TC])
        hbcb = sbuf("hbcb", [128, 2 * TC], BF16)
        gateb = sbuf("gateb", [128, 2 * TC], BF16)
        cwsb = sbuf("cwsb", [128, 80 * 4])
        cbsb = sbuf("cbsb", [128, 80])
        dtbs = sbuf("dtbs", [128, 1])
        sid = sbuf("sid", [128, 128], BF16)
        junk = sbuf("junk", [128, 4])
        psm = es.enter_context(nc.psum_tensor("psm", [128, 6 * 512], F32))
        ptr = es.enter_context(nc.psum_tensor("ptr", [128, 2 * 1024], BF16))

        s_cld = sem("s_cld")
        s_xl = [sem("s_xl0"), sem("s_xl1")]
        s_wl = [sem("s_wl0"), sem("s_wl1")]
        s_nw = sem("s_nw")
        s_ln = sem("s_ln")
        s_hb = sem("s_hb")
        s_tr = sem("s_tr")
        s_te = sem("s_te")
        s_wc = sem("s_wc")
        s_mm = sem("s_mm")
        s_ge = sem("s_ge")
        s_pe = sem("s_pe")
        s_cv = sem("s_cv")
        s_dt = sem("s_dt")
        s_xno = sem("s_xno")
        s_go = [sem("s_go0"), sem("s_go1")]
        s_ho = [sem("s_ho0"), sem("s_ho1")]
        s_do = sem("s_do")
        s_ss = sem("s_ss")

        def bankfree(og):
            if og < 64:
                return (s_ge, og + 1)
            if og < 144:
                return (s_pe, og - 64 + 1)
            return (s_dt, 1)

        with nc.Block() as blk:
            @blk.sync
            def _(e):
                e.dma_start(out=sid[:], in_=iden).then_inc(s_cld, 16)
                e.dma_start(out=cwsb[:], in_=cw).then_inc(s_cld, 16)
                e.dma_start(out=cbsb[:], in_=cb).then_inc(s_cld, 16)
                e.dma_start(out=dtbs[:], in_=dtb).then_inc(s_cld, 16)
                e.dma_start(out=xt[:, 0:D], in_=nwbc).then_inc(s_nw, 16)
                e.wait_ge(s_nw, 17)
                for m in range(NTH):
                    if m >= 2:
                        e.wait_ge(s_ln, m - 1)
                    e.dma_start(out=xt[:, (m % 2) * D:(m % 2 + 1) * D],
                                in_=x1h[m * 128:(m + 1) * 128, :]).then_inc(s_xl[m % 2], 16)
                for og in range(NOG):
                    for hf in range(2):
                        g = og * 2 + hf
                        if g >= 2:
                            e.wait_ge(s_wc, g - 1)
                        e.dma_start(out=wip[:, (g % 2) * 2048:(g % 2 + 1) * 2048]
                                    .rearrange("p (c o) -> p c o", o=128),
                                    in_=ipwT[hf * 2048:(hf + 1) * 2048, og * 128:(og + 1) * 128]
                                    .rearrange("(c p) o -> p c o", p=128)
                                    ).then_inc(s_wl[g % 2], 16)

            @blk.scalar
            def _(e):
                ssync = SelfSync(s_ss)
                e.wait_ge(s_cld, 64)
                for m in range(NTH):
                    e.wait_ge(s_xl[m % 2], 16 * (m // 2 + 1))
                    if m >= 2:
                        e.wait_ge(s_tr, (m - 1) * NK)
                        e.wait_ge(s_hb, m - 1)
                        if m - 2 >= 1:
                            e.wait_ge(s_xno, 16 * (m - 2))
                    xv = xt[:, (m % 2) * D:(m % 2 + 1) * D]
                    xb = xnb[:, (m % 2) * D:(m % 2 + 1) * D]
                    hv = hbf[:, (m % 2) * D:(m % 2 + 1) * D]
                    e.activation(out=xb, in_=xv, func=AF.Square, accum_out=stat[:, 1:2])
                    e.activation(out=xb, in_=xv, func=AF.Copy, accum_out=stat[:, 0:1])
                    e.mul(stat[:, 2:3], stat[:, 0:1], 1.0 / D)
                    e.mul(stat[:, 3:4], stat[:, 1:2], 1.0 / D)
                    e.activation(out=stat[:, 4:5], in_=stat[:, 2:3], func=AF.Square)
                    ins = e.mul(stat[:, 4:5], stat[:, 4:5], -1.0)
                    ssync.barrier(e, ins)
                    e.activation(out=stat[:, 3:4], in_=stat[:, 3:4], func=AF.Identity, bias=stat[:, 4:5])
                    e.activation(out=stat[:, 3:4], in_=stat[:, 3:4], func=AF.Copy, bias=EPS)
                    e.activation(out=stat[:, 6:7], in_=stat[:, 3:4], func=AF.Ln)
                    ins = e.activation(out=stat[:, 6:7], in_=stat[:, 6:7], func=AF.Exp, scale=-0.5)
                    ssync.barrier(e, ins)
                    ins = e.mul(stat[:, 7:8], stat[:, 2:3], stat[:, 6:7])
                    ssync.barrier(e, ins)
                    ins = e.mul(stat[:, 7:8], stat[:, 7:8], -1.0)
                    ssync.barrier(e, ins)
                    e.activation(out=xb, in_=xv, func=AF.Identity,
                                 scale=stat[:, 6:7], bias=stat[:, 7:8])
                    e.activation(out=xv, in_=xb, func=AF.Square, accum_out=stat[:, 0:1])
                    e.activation(out=stat[:, 1:2], in_=stat[:, 0:1], func=AF.Copy,
                                 scale=1.0 / D, bias=EPS)
                    e.activation(out=stat[:, 1:2], in_=stat[:, 1:2], func=AF.Ln)
                    ins = e.activation(out=stat[:, 1:2], in_=stat[:, 1:2], func=AF.Exp, scale=-0.5)
                    ssync.barrier(e, ins)
                    e.activation(out=hv, in_=xb, func=AF.Identity, scale=stat[:, 1:2])
                    e.memzero(junk[:, 0:1]).then_inc(s_ln, 1)
                for og in range(64):
                    e.wait_ge(s_mm, og + 1)
                    if og >= 2:
                        e.wait_ge(s_go[og % 2], 16 * (og // 2))
                    base = (og % 2) * 1536
                    e.activation(out=gateb[:, (og % 2) * TC:(og % 2) * TC + 256],
                                 in_=psm[:, base + 128: base + 384], func=AF.Copy)
                    pv2 = bass.AP(psm[:].tensor, psm[:].offset + base + 512,
                                  [list(psm[:].ap[0]), [512, 2], [1, 384]])
                    e.activation(out=gateb[:, (og % 2) * TC + 256:(og % 2 + 1) * TC]
                                 .rearrange("p (a b) -> p a b", b=384), in_=pv2, func=AF.Copy)
                    e.memzero(junk[:, 0:1]).then_inc(s_ge, 1)
                for j in range(80):
                    e.wait_ge(s_pe, j + 1)
                    if j >= 2:
                        e.wait_ge(s_ho[j % 2], 16 * (j // 2))
                    e.activation(out=hbcb[:, (j % 2) * TC:(j % 2 + 1) * TC],
                                 in_=acc[:, (j % 2) * TC:(j % 2 + 1) * TC],
                                 func=AF.Silu, bias=cbsb[:, j:j + 1])
                    e.memzero(junk[:, 1:2]).then_inc(s_cv, 1)
                e.wait_ge(s_mm, NOG)
                e.wait_ge(s_pe, 80)
                TH_ = 1152
                pv3 = bass.AP(psm[:].tensor, psm[:].offset + (144 % 2) * 1536,
                              [list(psm[:].ap[0]), [512, 3], [1, 384]])
                e.activation(out=pre[:, 0:TH_].rearrange("p (a b) -> p a b", b=384),
                             in_=pv3, func=AF.Copy)
                dv = pre[:, TH_:TH_ + TC]
                lp = pre[:, 0:TC]
                e.activation(out=dv, in_=pre[:, 128:TH_], func=AF.Identity, bias=dtbs[:, 0:1])
                e.activation(out=lp, in_=dv, func=AF.Abs)
                e.activation(out=lp, in_=lp, func=AF.Exp, scale=-1.0)
                e.activation(out=lp, in_=lp, func=AF.Ln, bias=1.0)
                e.activation(out=dv, in_=dv, func=AF.Relu)
                e.memzero(junk[:, 0:1]).then_inc(s_dt, 1)

            @blk.vector
            def _(e):
                e.wait_ge(s_nw, 16)
                e.tensor_copy(nwbb[:], xt[:, 0:D])
                e.memset(junk[:, 2:3], 0.0).then_inc(s_nw, 1)
                def evacs(mm_):
                    for k in range(NK):
                        i = mm_ * NK + k
                        e.wait_ge(s_tr, i + 1)
                        e.tensor_copy(hT[:, k * TH + mm_ * 128: k * TH + mm_ * 128 + 128],
                                      ptr[:, (i % 2) * 1024:(i % 2) * 1024 + 128])
                        e.memset(junk[:, 2:3], 0.0).then_inc(s_te, 1)
                for m in range(NTH):
                    e.wait_ge(s_ln, m + 1)
                    hv = hbf[:, (m % 2) * D:(m % 2 + 1) * D]
                    e.tensor_tensor(hv, hv, nwbb[:], ALU.mult)
                    e.memset(junk[:, 2:3], 0.0).then_inc(s_hb, 1)
                    if m >= 1:
                        evacs(m - 1)
                evacs(NTH - 1)
                e.memset(junk[:, 2:3], 0.0).then_inc(s_te, 1)
                def conv_j(j):
                    og_ = 64 + j
                    e.wait_ge(s_mm, og_ + 1)
                    if j >= 2:
                        e.wait_ge(s_cv, j - 1)
                    pv3 = bass.AP(psm[:].tensor, psm[:].offset + (og_ % 2) * 1536,
                                  [list(psm[:].ap[0]), [512, 3], [1, 384]])
                    prv = pre[:, (j % 2) * TH:(j % 2) * TH + TH]
                    e.tensor_copy(prv.rearrange("p (a b) -> p a b", b=384), pv3)
                    a0 = acc[:, (j % 2) * TC:(j % 2 + 1) * TC]
                    e.tensor_single_scalar(a0, prv[:, 125:125 + TC], cwsb[:, 4 * j:4 * j + 1], ALU.mult)
                    e.scalar_tensor_tensor(a0, prv[:, 126:126 + TC], cwsb[:, 4 * j + 1:4 * j + 2], a0, ALU.mult, ALU.add)
                    e.scalar_tensor_tensor(a0, prv[:, 127:127 + TC], cwsb[:, 4 * j + 2:4 * j + 3], a0, ALU.mult, ALU.add)
                    e.scalar_tensor_tensor(a0, prv[:, 128:128 + TC], cwsb[:, 4 * j + 3:4 * j + 4], a0, ALU.mult, ALU.add)
                    e.memset(junk[:, 2:3], 0.0).then_inc(s_pe, 1)
                for og in range(NOG):
                    for hf in range(2):
                        g = og * 2 + hf
                        e.wait_ge(s_wl[g % 2], 16 * (g // 2 + 1))
                        if hf == 0 and og >= 2:
                            e.wait_ge(s_mm, og - 1)
                        e.tensor_copy(wipb[:, (og % 2) * D + hf * 2048:(og % 2) * D + (hf + 1) * 2048],
                                      wip[:, (g % 2) * 2048:(g % 2 + 1) * 2048])
                        e.memset(junk[:, 2:3], 0.0).then_inc(s_wc, 1)
                    if og >= 66 and og - 66 < 80:
                        conv_j(og - 66)
                for j in range(NOG - 66, 80):
                    conv_j(j)
                e.wait_ge(s_dt, 1)
                e.tensor_add(pre[:, 1152:1152 + TC], pre[:, 1152:1152 + TC], pre[:, 0:TC])
                e.memset(junk[:, 2:3], 0.0).then_inc(s_dt, 1)

            @blk.tensor
            def _(e):
                for i in range(NTH * NK):
                    m, k = divmod(i, NK)
                    if k == 0:
                        e.wait_ge(s_hb, m + 1)
                    if i >= 2:
                        e.wait_ge(s_te, i - 1)
                    e.transpose(ptr[:, (i % 2) * 1024:(i % 2) * 1024 + 128],
                                hbf[:, (m % 2) * D + k * 128:(m % 2) * D + k * 128 + 128],
                                sid[:]).then_inc(s_tr, 1)
                e.wait_ge(s_te, NTH * NK + 1)
                for og in range(NOG):
                    e.wait_ge(s_wc, 2 * (og + 1))
                    if og >= 2:
                        sm, cnt = bankfree(og - 2)
                        e.wait_ge(sm, cnt)
                    base = (og % 2) * 1536
                    mmi = None
                    for k in range(NK):
                        for n in range(3):
                            mmi = e.matmul(psm[:, base + n * 512: base + n * 512 + 384],
                                           wipb[:, (og % 2) * D + k * 128:(og % 2) * D + k * 128 + 128],
                                           hT[:, k * TH + n * 384: k * TH + n * 384 + 384],
                                           start=(k == 0), stop=(k == NK - 1))
                    mmi.then_inc(s_mm, 1)

            @blk.gpsimd
            def _(e):
                for m in range(1, NTH):
                    e.wait_ge(s_ln, m + 1)
                    e.dma_start(out=o_xn1[(m - 1) * 128: m * 128, :],
                                in_=xnb[:, (m % 2) * D:(m % 2 + 1) * D]).then_inc(s_xno, 16)
                for og in range(64):
                    e.wait_ge(s_ge, og + 1)
                    e.dma_start(out=o_gate[og * 128:(og + 1) * 128, :],
                                in_=gateb[:, (og % 2) * TC:(og % 2 + 1) * TC]
                                ).then_inc(s_go[og % 2], 16)
                for j in range(80):
                    e.wait_ge(s_cv, j + 1)
                    e.dma_start(out=o_hbc[j * 128:(j + 1) * 128, :],
                                in_=hbcb[:, (j % 2) * TC:(j % 2 + 1) * TC]
                                ).then_inc(s_ho[j % 2], 16)
                e.wait_ge(s_dt, 2)
                e.dma_start(out=o_dt, in_=pre[:, 1152:1152 + TC]).then_inc(s_do, 16)
                e.wait_ge(s_do, 16)
                e.wait_ge(s_go[0], 16 * 32)
                e.wait_ge(s_go[1], 16 * 32)
                e.wait_ge(s_ho[0], 16 * 40)
                e.wait_ge(s_ho[1], 16 * 40)
                e.wait_ge(s_xno, 16 * 8)
    return nc


def run_l4_proj(x1, in_proj_w, norm_w, conv_w, conv_b, dt_bias):
    nc = cached("l4_proj", build_l4_proj)
    ipwT = np.ascontiguousarray(in_proj_w.T.astype(np.float32))
    nwbc = np.ascontiguousarray(np.broadcast_to(norm_w.astype(np.float32), (128, D)))
    cwp = np.ascontiguousarray(
        conv_w.astype(np.float32).reshape(80, 128, 4).transpose(1, 0, 2).reshape(128, 320))
    cbp = np.ascontiguousarray(
        conv_b.astype(np.float32).reshape(80, 128).T)
    iden = np.eye(128, dtype=BF)
    in_maps = []
    for c in range(NCORE):
        half = c % 2
        if half == 0:
            halo = np.zeros((128, D), np.float32)
        else:
            halo = x1[c * TC - 128: c * TC]
        in_maps.append({
            "x1h": np.ascontiguousarray(np.concatenate([halo, x1[c * TC:(c + 1) * TC]], 0)),
            "ipwT": ipwT, "nwbc": nwbc, "cw": cwp, "cb": cbp,
            "dtb": np.ascontiguousarray(dt_bias.astype(np.float32).reshape(128, 1)),
            "iden": iden,
        })
    res = run_bass_kernel_spmd(nc, in_maps, core_ids=list(range(NCORE)))
    xn1 = np.concatenate([np.asarray(r["o_xn1"]) for r in res.results], 0)
    gate = np.concatenate([np.asarray(r["o_gate"]) for r in res.results], 1)
    hbc = np.concatenate([np.asarray(r["o_hbc"]) for r in res.results], 1)
    dt = np.concatenate([np.asarray(r["o_dt"]) for r in res.results], 1)
    return xn1, gate, hbc, dt


def build_l5_ssd():
    nc = _mknc()
    NCH = 16
    NH = 64
    hid = nc.dram_tensor("hid", [S, 4096], BF16, kind="ExternalInput").ap()
    Bfm = nc.dram_tensor("Bfm", [4, 128, S], BF16, kind="ExternalInput").ap()
    Cfm = nc.dram_tensor("Cfm", [4, 128, S], BF16, kind="ExternalInput").ap()
    Btm = nc.dram_tensor("Btm", [S, 512], BF16, kind="ExternalInput").ap()
    dt = nc.dram_tensor("dt", [64, S], F32, kind="ExternalInput").ap()
    alog = nc.dram_tensor("alog", [64, 1], F32, kind="ExternalInput").ap()
    dpx = nc.dram_tensor("dpx", [128, 4096], F32, kind="ExternalInput").ap()
    idf = nc.dram_tensor("idf", [128, 128], F32, kind="ExternalInput").ap()
    tri4 = nc.dram_tensor("tri4", [128, 512], F32, kind="ExternalInput").ap()
    hsel = nc.dram_tensor("hsel", [4, 512], F32, kind="ExternalInput").ap()
    onef = nc.dram_tensor("onef", [1, 128], F32, kind="ExternalInput").ap()
    o_y = nc.dram_tensor("o_y", [S, 4096], BF16, kind="ExternalOutput").ap()
    acs_dram = nc.dram_tensor("acs_dram", [2, 64, 128], F32, kind="Internal").ap()
    alast_dram = nc.dram_tensor("alast_dram", [64], F32, kind="Internal").ap()

    es = ExitStack()
    sbuf = lambda n, s_, d=F32: es.enter_context(nc.sbuf_tensor(n, s_, d))
    sem = lambda n: es.enter_context(nc.semaphore(n))
    with es:
        hids = sbuf("hids", [128, 2 * 4096], BF16)
        bfs = sbuf("bfs", [128, 2 * 512], BF16)
        cfs = sbuf("cfs", [128, 2 * 512], BF16)
        bts = sbuf("bts", [128, 2 * 512], BF16)
        dts_ = sbuf("dts_", [64, S])
        dpe = sbuf("dpe", [128, 4096])
        dpeb = sbuf("dpeb", [128, 4096], BF16)
        sidf = sbuf("sidf", [128, 128])
        tri = sbuf("tri", [128, 512])
        hse = sbuf("hse", [4, 512])
        onc = sbuf("onc", [1, 128])
        nega = sbuf("nega", [64, 1])
        afc = sbuf("afc", [64, 768])
        acsr = sbuf("acsr", [1, 2 * 512])
        nacs4 = sbuf("nacs4", [4, 16 * 128])
        eatr = sbuf("eatr", [1, 64])
        fac = sbuf("fac", [128, 2 * 64], BF16)
        expa = sbuf("expa", [128, 64])
        eatb = sbuf("eatb", [128, 64])
        hdt = sbuf("hdt", [128, 4096], BF16)
        hdt2 = sbuf("hdt2", [128, 4096], BF16)
        dres = sbuf("dres", [128, 4096])
        cbt = sbuf("cbt", [128, 4 * 128])
        expd = sbuf("expd", [128, 2 * 512])
        smT = sbuf("smT", [128, 64 * 128], BF16)
        stf = sbuf("stf", [128, 4096])
        stb = sbuf("stb", [128, 4096], BF16)
        ysb = sbuf("ysb", [128, 4096])
        ybf = sbuf("ybf", [128, 2 * 4096], BF16)
        junk = sbuf("junk", [128, 4])
        psm = es.enter_context(nc.psum_tensor("psm", [128, 8 * 512], F32))

        s_cld = sem("s_cld")
        s_ld = [sem("s_ld0"), sem("s_ld1")]
        s_dtp = sem("s_dtp")
        s_bo = sem("s_bo")
        s_br = sem("s_br")
        s_ar = [sem("s_ar0"), sem("s_ar1")]
        s_tp = sem("s_tp")
        s_tpe = sem("s_tpe")
        s_cb = sem("s_cb")
        s_cbe = sem("s_cbe")
        s_dm = sem("s_dm")
        s_ex = sem("s_ex")
        s_sm = sem("s_sm")
        s_hd = sem("s_hd")
        s_pey = sem("s_pey")
        s_yd = sem("s_yd")
        s_yo = sem("s_yo")
        s_up = sem("s_up")
        s_dr = sem("s_dr")
        s_st = [sem("s_st0"), sem("s_st1")]
        s_ss = sem("s_ss")
        s_vs = sem("s_vs")

        BK = lambda b, w=512: psm[:, b * 512: b * 512 + w]

        with nc.Block() as blk:
            @blk.sync
            def _(e):
                e.dma_start(out=sidf[:], in_=idf).then_inc(s_cld, 16)
                e.dma_start(out=tri[:], in_=tri4).then_inc(s_cld, 16)
                e.dma_start(out=hse[:], in_=hsel).then_inc(s_cld, 16)
                e.dma_start(out=onc[:], in_=onef).then_inc(s_cld, 16)
                e.dma_start(out=dts_[:], in_=dt).then_inc(s_cld, 16)
                e.dma_start(out=dpe[:], in_=dpx).then_inc(s_cld, 16)
                e.dma_start(out=nega[:], in_=alog).then_inc(s_cld, 16)
                for c in range(NCH):
                    sl = c % 2
                    if c >= 2:
                        e.wait_ge(s_dr, c - 1)
                    e.dma_start(out=hids[:, sl * 4096:(sl + 1) * 4096],
                                in_=hid[c * 128:(c + 1) * 128, :]).then_inc(s_ld[sl], 16)
                    e.dma_start(out=bfs[:, sl * 512:(sl + 1) * 512]
                                .rearrange("p (g l) -> p g l", l=128),
                                in_=Bfm[:, :, c * 128:(c + 1) * 128]
                                .rearrange("g p l -> p g l")).then_inc(s_ld[sl], 16)
                    e.dma_start(out=cfs[:, sl * 512:(sl + 1) * 512]
                                .rearrange("p (g l) -> p g l", l=128),
                                in_=Cfm[:, :, c * 128:(c + 1) * 128]
                                .rearrange("g p l -> p g l")).then_inc(s_ld[sl], 16)
                    e.dma_start(out=bts[:, sl * 512:(sl + 1) * 512],
                                in_=Btm[c * 128:(c + 1) * 128, :]).then_inc(s_ld[sl], 16)

            @blk.scalar
            def _(e):
                ssync = SelfSync(s_ss)
                e.wait_ge(s_cld, 112)
                ins = e.activation(out=nega[:, 0:1], in_=nega[:, 0:1], func=AF.Exp)
                ins = e.mul(nega[:, 0:1], nega[:, 0:1], -1.0)
                ssync.barrier(e, ins)
                e.memzero(junk[0:64, 0:1]).then_inc(s_dtp, 1)
                for c in range(NCH):
                    af = afc[0:64, 0:128]
                    acs = afc[0:64, 128:256]
                    nac = afc[0:64, 256:384]
                    dsv = afc[0:64, 384:512]
                    e.wait_ge(s_dtp, 4 * c + 1)
                    if c >= 1:
                        e.wait_ge(s_tp, 4 * c)
                        e.wait_ge(s_bo, 48 * c)
                    e.mul(af, dts_[0:64, c * 128:(c + 1) * 128], nega[0:64, 0:1])
                    e.memzero(junk[0:64, 0:1]).then_inc(s_dtp, 1)
                    e.wait_ge(s_dtp, 4 * c + 3)
                    e.activation(out=dsv, in_=acs, func=AF.Identity,
                                 bias=afc[0:64, 512:513])
                    e.activation(out=dsv, in_=dsv, func=AF.Exp, scale=-1.0)
                    e.mul(nac, acs, -1.0)
                    e.memzero(junk[0:64, 0:1]).then_inc(s_dtp, 1)
                    e.wait_ge(s_br, 32 * c + 32)
                    e.activation(out=eatr[0:1, :], in_=eatr[0:1, :], func=AF.Exp, scale=-1.0)
                    e.memzero(junk[0:1, 0:1]).then_inc(s_dtp, 1)
                    e.wait_ge(s_tp, 4 * (c + 1))
                    e.wait_ge(s_tpe, 3 * c + 2)
                    e.activation(out=expa[:], in_=psm[:, 3 * 512 + 128:3 * 512 + 192],
                                 func=AF.Exp)
                    e.memzero(junk[0:1, 0:1]).then_inc(s_tpe, 1)
                    for gg in range(16):
                        e.wait_ge(s_dm, 16 * c + gg + 1)
                        if 16 * c + gg >= 2:
                            e.wait_ge(s_sm, 16 * c + gg - 1)
                        ev = expd[:, (gg % 2) * 512:(gg % 2 + 1) * 512]
                        e.activation(out=ev, in_=BK(1 + gg % 2), func=AF.Relu, scale=-1.0)
                        e.activation(out=ev, in_=ev, func=AF.Exp, scale=-1.0)
                        e.memzero(junk[0:1, 0:1]).then_inc(s_ex, 1)
                    for h in range(NH):
                        if h == 0 and c >= 1:
                            e.wait_ge(s_dr, c)
                        e.wait_ge(s_pey, NH * c + h + 1)
                        e.activation(out=ysb[:, h * 64:(h + 1) * 64],
                                     in_=BK(4 + h % 2, 64), func=AF.Copy)
                        e.memzero(junk[0:1, 0:1]).then_inc(s_yd, 1)

            _scan_n = [0]

            @blk.vector
            def _(e):
                e.wait_ge(s_cld, 112)
                e.tensor_copy(dpeb[:], dpe[:])
                e.memset(afc[0:64, 640:768], 1.0)
                e.memset(stf[:], 0.0)
                e.tensor_copy(stb[:], stf[:])
                e.memset(junk[:, 2:3], 0.0).then_inc(s_up, NH)
                for c in range(NCH):
                    sl = c % 2
                    af = afc[0:64, 0:128]
                    acs = afc[0:64, 128:256]
                    e.wait_ge(s_dtp, 4 * c + 2)
                    if c >= 1:
                        e.wait_ge(s_bo, 48 * c)
                        e.wait_ge(s_tp, 4 * c)
                    e.tensor_tensor_scan(acs, afc[0:64, 640:768], af, 0.0,
                                         ALU.mult, ALU.add)
                    e.memset(junk[:, 2:3], 0.0).then_inc(s_vs, 1)
                    _scan_n[0] += 1
                    e.wait_ge(s_vs, _scan_n[0])
                    e.tensor_scalar(afc[0:64, 512:513], acs[0:64, 127:128], -1.0,
                                    None, ALU.mult)
                    e.memset(junk[:, 2:3], 0.0).then_inc(s_dtp, 1)
                    e.wait_ge(s_tp, 4 * (c + 1))
                    e.tensor_copy(fac[:, 0:64], psm[:, 3 * 512:3 * 512 + 64])
                    e.tensor_copy(fac[:, 64:128], psm[:, 3 * 512 + 64:3 * 512 + 128])
                    e.memset(junk[:, 2:3], 0.0).then_inc(s_tpe, 1)
                    e.tensor_copy(eatb[:], psm[:, 3 * 512 + 192:3 * 512 + 256])
                    e.memset(junk[:, 2:3], 0.0).then_inc(s_tpe, 1)
                    e.wait_ge(s_ld[sl], 64 * (c // 2 + 1))
                    hv = hids[:, sl * 4096:(sl + 1) * 4096]
                    dtv = bass.AP(fac[:].tensor, fac[:].offset,
                                  [list(fac[:].ap[0]), [1, 64], [0, 64]])
                    dsvv = bass.AP(fac[:].tensor, fac[:].offset + 64,
                                   [list(fac[:].ap[0]), [1, 64], [0, 64]])
                    e.tensor_tensor(hdt[:].rearrange("p (h q) -> p h q", q=64),
                                    hv.rearrange("p (h q) -> p h q", q=64), dtv, ALU.mult)
                    e.tensor_tensor(hdt2[:].rearrange("p (h q) -> p h q", q=64),
                                    hdt[:].rearrange("p (h q) -> p h q", q=64), dsvv, ALU.mult)
                    e.memset(junk[:, 2:3], 0.0).then_inc(s_hd, 1)
                    for g in range(4):
                        e.wait_ge(s_cb, 4 * c + g + 1)
                        e.tensor_copy(cbt[:, g * 128:(g + 1) * 128],
                                      psm[:, g * 128:(g + 1) * 128])
                        e.memset(junk[:, 2:3], 0.0).then_inc(s_cbe, 1)
                    for gg in range(16):
                        e.wait_ge(s_ex, 16 * c + gg + 1)
                        ev = expd[:, (gg % 2) * 512:(gg % 2 + 1) * 512]
                        e.tensor_tensor(ev, ev, tri[:], ALU.mult)
                        g = gg // 4
                        cbv = bass.AP(cbt[:].tensor, cbt[:].offset + g * 128,
                                      [list(cbt[:].ap[0]), [0, 4], [1, 128]])
                        e.tensor_tensor(smT[:, gg * 512:(gg + 1) * 512]
                                        .rearrange("p (a b) -> p a b", b=128),
                                        ev.rearrange("p (a b) -> p a b", b=128),
                                        cbv, ALU.mult)
                        e.memset(junk[:, 2:3], 0.0).then_inc(s_sm, 1)
                    for h in range(NH):
                        e.wait_ge(s_yd, NH * c + h + 1)
                        e.scalar_tensor_tensor(ysb[:, h * 64:(h + 1) * 64],
                                               BK(6 + h % 2, 64), expa[:, h:h + 1],
                                               ysb[:, h * 64:(h + 1) * 64],
                                               ALU.mult, ALU.add)
                        e.memset(junk[:, 2:3], 0.0).then_inc(s_yo, 1)
                        e.scalar_tensor_tensor(stf[:, h * 64:(h + 1) * 64],
                                               stf[:, h * 64:(h + 1) * 64],
                                               eatb[:, h:h + 1],
                                               BK(0 if h % 2 == 0 else 1, 64),
                                               ALU.mult, ALU.add)
                        e.tensor_copy(stb[:, h * 64:(h + 1) * 64],
                                      stf[:, h * 64:(h + 1) * 64])
                        e.memset(junk[:, 3:4], 0.0).then_inc(s_up, 1)
                    e.wait_ge(s_yo, NH * (c + 1))
                    e.tensor_tensor(dres[:], hv, dpeb[:], ALU.mult)
                    e.tensor_add(ysb[:], ysb[:], dres[:])
                    if c >= 2:
                        e.wait_ge(s_st[sl], 16 * (c // 2))
                    e.tensor_copy(ybf[:, sl * 4096:(sl + 1) * 4096], ysb[:])
                    e.memset(junk[:, 2:3], 0.0).then_inc(s_dr, 1)

            @blk.tensor
            def _(e):
                for c in range(NCH):
                    sl = c % 2
                    id64 = sidf[0:64, 0:64]
                    e.wait_ge(s_dtp, 4 * c + 4)
                    e.wait_ge(s_tpe, 3 * c)
                    e.transpose(psm[:, 3 * 512:3 * 512 + 64],
                                dts_[0:64, c * 128:(c + 1) * 128], id64).then_inc(s_tp, 1)
                    e.transpose(psm[:, 3 * 512 + 64:3 * 512 + 128],
                                afc[0:64, 384:512], id64).then_inc(s_tp, 1)
                    e.transpose(psm[:, 3 * 512 + 128:3 * 512 + 192],
                                afc[0:64, 128:256], id64).then_inc(s_tp, 1)
                    e.wait_ge(s_dtp, 4 * c + 5)
                    e.matmul(psm[:, 3 * 512 + 192:3 * 512 + 256], onc[0:1, :],
                             eatr[0:1, 0:64], start=True, stop=True).then_inc(s_tp, 1)
                    e.wait_ge(s_ld[sl], 64 * (c // 2 + 1))
                    if c >= 1:
                        e.wait_ge(s_up, NH * c + NH)
                        e.wait_ge(s_cbe, 4 * c)
                    for g in range(4):
                        e.matmul(psm[:, g * 128:(g + 1) * 128],
                                 bfs[:, sl * 512 + g * 128: sl * 512 + (g + 1) * 128],
                                 cfs[:, sl * 512 + g * 128: sl * 512 + (g + 1) * 128],
                                 start=True, stop=True).then_inc(s_cb, 1)
                    for gg in range(16):
                        i = 16 * c + gg
                        e.wait_ge(s_ar[gg % 2], 16 * (i // 2 + 1))
                        e.wait_ge(s_br, 32 * c + 16)
                        if i >= 2:
                            e.wait_ge(s_ex, i - 1)
                        if c >= 1 and gg == 0:
                            e.wait_ge(s_up, NH * c + NH)
                        bank = 1 + gg % 2
                        e.matmul(BK(bank), onc[0:1, :],
                                 acsr[0:1, (gg % 2) * 512:(gg % 2 + 1) * 512],
                                 start=True, stop=False)
                        e.matmul(BK(bank), nacs4[0:4, gg * 128:(gg + 1) * 128],
                                 hse[0:4, :], start=False, stop=True).then_inc(s_dm, 1)
                    e.wait_ge(s_hd, c + 1)
                    e.wait_ge(s_cbe, 4 * (c + 1))
                    e.wait_ge(s_ex, 16 * (c + 1))
                    for h in range(NH):
                        gg = h // 4
                        g = h // 16
                        e.wait_ge(s_sm, 16 * c + gg + 1)
                        if NH * c + h >= 2:
                            e.wait_ge(s_yd, NH * c + h - 1)
                            e.wait_ge(s_yo, NH * c + h - 1)
                        e.wait_ge(s_up, NH * c + h - 2 + 1 + NH if NH * c + h >= 2 else NH)
                        e.matmul(BK(4 + h % 2, 64), smT[:, h * 128:(h + 1) * 128],
                                 hdt[:, h * 64:(h + 1) * 64],
                                 start=True, stop=True)
                        e.wait_ge(s_up, NH * c + h + NH)
                        e.matmul(BK(6 + h % 2, 64),
                                 cfs[:, sl * 512 + g * 128: sl * 512 + (g + 1) * 128],
                                 stb[:, h * 64:(h + 1) * 64],
                                 start=True, stop=True)
                        e.matmul(BK(0 if h % 2 == 0 else 1, 64),
                                 bts[:, sl * 512 + g * 128: sl * 512 + (g + 1) * 128],
                                 hdt2[:, h * 64:(h + 1) * 64],
                                 start=True, stop=True).then_inc(s_pey, 1)

            @blk.gpsimd
            def _(e):
                for c in range(NCH):
                    sl = c % 2
                    e.wait_ge(s_dtp, 4 * c + 4)
                    e.dma_start(out=acs_dram[0], in_=afc[0:64, 128:256]).then_inc(s_bo, 16)
                    e.dma_start(out=acs_dram[1], in_=afc[0:64, 256:384]).then_inc(s_bo, 16)
                    e.dma_start(out=alast_dram, in_=afc[0:64, 512:513]).then_inc(s_bo, 16)
                    e.wait_ge(s_bo, 48 * (c + 1))
                    e.dma_start(out=nacs4[0:4, :].rearrange("p (g l) -> p g l", l=128),
                                in_=bass.AP(acs_dram.tensor, acs_dram.offset + 64 * 128,
                                            [[128, 4], [512, 16], [1, 128]])
                                ).then_inc(s_br, 16)
                    e.dma_start(out=eatr[0:1, :],
                                in_=alast_dram.rearrange("(o k) -> o k", o=1)
                                ).then_inc(s_br, 16)
                    for gg in range(16):
                        i = 16 * c + gg
                        if i >= 2:
                            e.wait_ge(s_dm, i - 1)
                        e.dma_start(out=acsr[0:1, (gg % 2) * 512:(gg % 2 + 1) * 512],
                                    in_=bass.AP(acs_dram.tensor, acs_dram.offset + gg * 512,
                                                [[0, 1], [1, 512]])).then_inc(s_ar[gg % 2], 16)
                    e.wait_ge(s_dr, c + 1)
                    e.dma_start(out=o_y[c * 128:(c + 1) * 128, :],
                                in_=ybf[:, sl * 4096:(sl + 1) * 4096]).then_inc(s_st[sl], 16)
                e.wait_ge(s_st[0], 16 * 8)
                e.wait_ge(s_st[1], 16 * 8)
    return nc


def run_l5_ssd(hbc, dtf, A_log, Dp):
    nc = cached("l5_ssd", build_l5_ssd)
    hid_f = hbc[:8192]
    Bf = hbc[8192:9216]
    Cf = hbc[9216:10240]
    iden_f = np.eye(128, dtype=np.float32)
    tri4 = np.zeros((128, 512), np.float32)
    for k in range(4):
        tri4[:, k * 128:(k + 1) * 128] = np.triu(np.ones((128, 128), np.float32))
    hsel = np.zeros((4, 512), np.float32)
    for k in range(4):
        hsel[k, k * 128:(k + 1) * 128] = 1.0
    onef = np.ones((1, 128), np.float32)
    in_maps = []
    for core in range(NCORE):
        b, hg = divmod(core, 2)
        tok = slice(b * S, (b + 1) * S)
        hs = slice(hg * 4096, (hg + 1) * 4096)
        gs = slice(hg * 512, (hg + 1) * 512)
        Bfm = np.ascontiguousarray(Bf[gs, tok].reshape(4, 128, S))
        Cfm = np.ascontiguousarray(Cf[gs, tok].reshape(4, 128, S))
        Btm = np.ascontiguousarray(Bf[gs, tok].T)
        in_maps.append({
            "hid": np.ascontiguousarray(hid_f[hs, tok].T),
            "Bfm": Bfm, "Cfm": Cfm, "Btm": Btm,
            "dt": np.ascontiguousarray(dtf[hg * 64:(hg + 1) * 64, tok]),
            "alog": np.ascontiguousarray(A_log[hg * 64:(hg + 1) * 64].astype(np.float32).reshape(64, 1)),
            "dpx": np.ascontiguousarray(np.broadcast_to(
                np.repeat(Dp[hg * 64:(hg + 1) * 64].astype(np.float32), 64), (128, 4096))),
            "idf": iden_f, "tri4": tri4, "hsel": hsel, "onef": onef,
        })
    res = run_bass_kernel_spmd(nc, in_maps, core_ids=list(range(NCORE)))
    y = np.zeros((B, S, 8192), BF)
    for core in range(NCORE):
        b, hg = divmod(core, 2)
        y[b, :, hg * 4096:(hg + 1) * 4096] = np.asarray(res.results[core]["o_y"])
    return y.reshape(B * S, 8192)


def build_l6a_gate():
    nc = _mknc()
    y = nc.dram_tensor("y", [TC, 8192], BF16, kind="ExternalInput").ap()
    gt = nc.dram_tensor("gt", [TC, 8192], BF16, kind="ExternalInput").ap()
    gnw = nc.dram_tensor("gnw", [128, 8192], F32, kind="ExternalInput").ap()
    o_gn = nc.dram_tensor("o_gn", [TC, 8192], BF16, kind="ExternalOutput").ap()
    NT = TC // 128
    es = ExitStack()
    sbuf = lambda n, s_, d=F32: es.enter_context(nc.sbuf_tensor(n, s_, d))
    sem = lambda n: es.enter_context(nc.semaphore(n))
    with es:
        ys = sbuf("ys", [128, 2 * 8192], BF16)
        gs = sbuf("gs", [128, 8192], BF16)
        gn = sbuf("gn", [128, 8192])
        tg = sbuf("tg", [128, 8192], BF16)
        yg = sbuf("yg", [128, 8192])
        sq = sbuf("sq", [128, 8192])
        ob = sbuf("ob", [128, 2 * 8192], BF16)
        ms = sbuf("ms", [128, 16])
        junk = sbuf("junk", [128, 4])
        s_cld = sem("s_cld")
        s_ld = [sem("s_ld0"), sem("s_ld1")]
        s_sg = sem("s_sg")
        s_yg = sem("s_yg")
        s_ms = sem("s_ms")
        s_gn = sem("s_gn")
        s_st = [sem("s_st0"), sem("s_st1")]
        s_ss = sem("s_ss")
        with nc.Block() as blk:
            @blk.sync
            def _(e):
                e.dma_start(out=gn[:], in_=gnw).then_inc(s_cld, 16)
                for m in range(NT):
                    sl = m % 2
                    if m >= 2:
                        e.wait_ge(s_gn, m - 1)
                    e.dma_start(out=ys[:, sl * 8192:(sl + 1) * 8192],
                                in_=y[m * 128:(m + 1) * 128, :]).then_inc(s_ld[sl], 16)
                    if m >= 1:
                        e.wait_ge(s_sg, m)
                    e.dma_start(out=gs[:],
                                in_=gt[m * 128:(m + 1) * 128, :]).then_inc(s_ld[sl], 16)

            @blk.scalar
            def _(e):
                ssync = SelfSync(s_ss)
                for m in range(NT):
                    sl = m % 2
                    e.wait_ge(s_ld[sl], 32 * (m // 2 + 1))
                    if m >= 1:
                        e.wait_ge(s_yg, m)
                    e.activation(out=tg[:], in_=gs[:], func=AF.Silu)
                    e.memzero(junk[:, 0:1]).then_inc(s_sg, 1)
                    e.wait_ge(s_yg, m + 1)
                    e.activation(out=ms[:, 8:16], in_=ms[:, 0:8], func=AF.Copy,
                                 scale=1.0 / 1024.0, bias=EPS)
                    e.activation(out=ms[:, 8:16], in_=ms[:, 8:16], func=AF.Ln)
                    ins = e.activation(out=ms[:, 8:16], in_=ms[:, 8:16], func=AF.Exp, scale=-0.5)
                    ssync.barrier(e, ins)
                    e.memzero(junk[:, 0:1]).then_inc(s_ms, 1)

            @blk.vector
            def _(e):
                e.wait_ge(s_cld, 16)
                for m in range(NT):
                    sl = m % 2
                    e.wait_ge(s_sg, m + 1)
                    e.tensor_tensor(yg[:], ys[:, sl * 8192:(sl + 1) * 8192], tg[:], ALU.mult)
                    e.tensor_tensor(sq[:], yg[:], yg[:], ALU.mult)
                    e.reduce_sum(ms[:, 0:8], sq[:].rearrange("p (g n) -> p g n", g=8),
                                 axis=AX.X)
                    e.memset(junk[:, 2:3], 0.0).then_inc(s_yg, 1)
                    e.wait_ge(s_ms, m + 1)
                    if m >= 2:
                        e.wait_ge(s_st[sl], 16 * (m // 2))
                    for g in range(8):
                        e.scalar_tensor_tensor(
                            ob[:, sl * 8192 + g * 1024: sl * 8192 + (g + 1) * 1024],
                            yg[:, g * 1024:(g + 1) * 1024], ms[:, 8 + g:9 + g],
                            gn[:, g * 1024:(g + 1) * 1024], ALU.mult, ALU.mult)
                    e.memset(junk[:, 2:3], 0.0).then_inc(s_gn, 1)

            @blk.gpsimd
            def _(e):
                for m in range(NT):
                    sl = m % 2
                    e.wait_ge(s_gn, m + 1)
                    e.dma_start(out=o_gn[m * 128:(m + 1) * 128, :],
                                in_=ob[:, sl * 8192:(sl + 1) * 8192]).then_inc(s_st[sl], 16)
                e.wait_ge(s_st[0], 64)
                e.wait_ge(s_st[1], 64)
    return nc


def run_l6a_gate(y, gate_fm, gnorm_w):
    nc = cached("l6a_gate", build_l6a_gate)
    gnwb = np.ascontiguousarray(np.broadcast_to(gnorm_w.astype(np.float32), (128, 8192)))
    in_maps = []
    for c in range(NCORE):
        sl = slice(c * TC, (c + 1) * TC)
        in_maps.append({
            "y": np.ascontiguousarray(y[sl]),
            "gt": np.ascontiguousarray(gate_fm[:, sl].T),
            "gnw": gnwb,
        })
    res = run_bass_kernel_spmd(nc, in_maps, core_ids=list(range(NCORE)))
    return np.concatenate([np.asarray(r["o_gn"]) for r in res.results], 0)


def build_l6b_oproj():
    nc = _mknc()
    gnT = nc.dram_tensor("gnT", [8192, TC], BF16, kind="ExternalInput").ap()
    opwT = nc.dram_tensor("opwT", [8192, D], F32, kind="ExternalInput").ap()
    xn1 = nc.dram_tensor("xn1", [TC, D], BF16, kind="ExternalInput").ap()
    xn1T = nc.dram_tensor("xn1T", [D, TC], BF16, kind="ExternalInput").ap()
    w1T = nc.dram_tensor("w1T", [D, D], F32, kind="ExternalInput").ap()
    b1 = nc.dram_tensor("b1", [1, D], F32, kind="ExternalInput").ap()
    a1 = nc.dram_tensor("a1", [D, 32], F32, kind="ExternalInput").ap()
    liT = nc.dram_tensor("liT", [32, 8], F32, kind="ExternalInput").ap()
    parf = nc.dram_tensor("parf", [8, TC], F32, kind="ExternalInput").ap()
    o_s = nc.dram_tensor("o_s", [TC, D], F32, kind="ExternalOutput").ap()
    o_H = nc.dram_tensor("o_H", [TC, D], BF16, kind="ExternalOutput").ap()
    o_zfm = nc.dram_tensor("o_zfm", [32, TC], F32, kind="ExternalOutput").ap()
    o_lm = nc.dram_tensor("o_lm", [1, 2], F32, kind="ExternalOutput").ap()
    lm_dram = nc.dram_tensor("lm_dram", [16], F32, kind="Internal").ap()

    NT = TC // 128
    NO = D // 512
    NKG = 8192 // 128
    NKH = D // 128
    NGW = NO * NKG + NO * NKH

    es = ExitStack()
    sbuf = lambda n, s_, d=F32: es.enter_context(nc.sbuf_tensor(n, s_, d))
    sem = lambda n: es.enter_context(nc.semaphore(n))
    with es:
        gts = sbuf("gts", [128, 2 * TC], BF16)
        x1ts = sbuf("x1ts", [128, NKH * TC], BF16)
        x1s = sbuf("x1s", [128, 4 * 512], BF16)
        wb = sbuf("wb", [128, 2 * 512])
        wbb = sbuf("wbb", [128, 2 * 512], BF16)
        a1b = sbuf("a1b", [128, NKH * 32], BF16)
        w8b = sbuf("w8b", [128, NKH * 8], BF16)
        brow = sbuf("brow", [1, D])
        browb = sbuf("browb", [1, D], BF16)
        b8f = sbuf("b8f", [1, 8])
        b8r = sbuf("b8r", [1, 8], BF16)
        onesw = sbuf("onesw", [1, 512], BF16)
        liTf = sbuf("liTf", [32, 8])
        onesr = sbuf("onesr", [1, 128], BF16)
        liTs = sbuf("liTs", [32, 8], BF16)
        pars = sbuf("pars", [8, TC])
        otile = sbuf("otile", [128, 4 * 512])
        htile = sbuf("htile", [128, 4 * 512], BF16)
        zfm = sbuf("zfm", [32, TC], BF16)
        zff = sbuf("zff", [32, TC])
        hsf = sbuf("hsf", [8, TC])
        dsf = sbuf("dsf", [8, TC])
        lmx = sbuf("lmx", [8, 4])
        lrow = sbuf("lrow", [1, 16])
        junk = sbuf("junk", [128, 4])
        psm = es.enter_context(nc.psum_tensor("psm", [128, 8 * 512], F32))

        s_cld = sem("s_cld")
        s_cldg = sem("s_cldg")
        s_gl = [sem("s_gl0"), sem("s_gl1")]
        s_wl = [sem("s_wl0"), sem("s_wl1")]
        s_x1l = [sem(f"s_x1l{j}") for j in range(4)]
        s_wc = sem("s_wc")
        s_pre = sem("s_pre")
        s_pe2 = sem("s_pe2")
        s_lm = sem("s_lm")
        s_mm = sem("s_mm")
        s_ev = sem("s_ev")
        s_he = sem("s_he")
        s_st = [sem(f"s_st{j}") for j in range(4)]
        s_sth = [sem(f"s_sth{j}") for j in range(4)]
        s_zo = sem("s_zo")
        s_bb = sem("s_bb")

        with nc.Block() as blk:
            @blk.sync
            def _(e):
                e.dma_start(out=brow[:], in_=b1).then_inc(s_cld, 16)
                e.dma_start(out=b8f[:], in_=b1[0:1, 0:8]).then_inc(s_cld, 16)
                e.dma_start(out=liTf[:], in_=liT).then_inc(s_cld, 16)
                e.dma_start(out=pars[:], in_=parf).then_inc(s_cld, 16)
                e.dma_start(out=x1ts[:].rearrange("p (c t) -> p c t", t=TC),
                            in_=xn1T.rearrange("(c p) t -> p c t", p=128)).then_inc(s_cld, 16)
                for g in range(NGW):
                    oc_, k_ = (g // NKG, g % NKG) if g < NO * NKG else \
                        ((g - NO * NKG) // NKH, (g - NO * NKG) % NKH)
                    srcw = opwT if g < NO * NKG else w1T
                    if g >= 2:
                        e.wait_ge(s_wc, g - 1)
                    e.dma_start(out=wb[:, (g % 2) * 512:(g % 2 + 1) * 512],
                                in_=srcw[k_ * 128:(k_ + 1) * 128, oc_ * 512:(oc_ + 1) * 512]
                                ).then_inc(s_wl[g % 2], 16)
                    if g < NO * NKG:
                        if g >= 2:
                            e.wait_ge(s_mm, g - 1)
                        e.dma_start(out=gts[:, (k_ % 2) * TC:(k_ % 2 + 1) * TC],
                                    in_=gnT[k_ * 128:(k_ + 1) * 128, :]).then_inc(s_gl[k_ % 2], 16)

            @blk.vector
            def _(e):
                e.wait_ge(s_cld, 80)
                e.wait_ge(s_cldg, 32)
                e.tensor_copy(browb[0:1, :], brow[0:1, :])
                e.memset(onesr[0:1, :].bitcast(mybir.dt.uint16), 0x3F80)
                e.memset(onesw[0:1, :].bitcast(mybir.dt.uint16), 0x3F80)
                e.tensor_copy(liTs[:], liTf[:])
                e.tensor_copy(b8r[0:1, :], b8f[0:1, :])
                e.memset(junk[:, 2:3], 0.0).then_inc(s_cld, 1)
                e.wait_ge(s_pre, 1)
                e.tensor_copy(zff[:, 0:512], psm[0:32, 0:512])
                e.tensor_copy(zff[:, 512:1024], psm[0:32, 512:1024])
                e.tensor_copy(zfm[:, 0:512], psm[0:32, 0:512])
                e.tensor_copy(zfm[:, 512:1024], psm[0:32, 512:1024])
                e.memset(junk[:, 2:3], 0.0).then_inc(s_pe2, 1)
                e.wait_ge(s_pre, 2)
                e.tensor_copy(hsf[:, 0:512], psm[0:8, 2 * 512:3 * 512])
                e.tensor_copy(hsf[:, 512:1024], psm[0:8, 3 * 512:4 * 512])
                e.memset(junk[:, 2:3], 0.0).then_inc(s_pe2, 1)
                e.wait_ge(s_pre, 3)
                e.tensor_copy(dsf[:, 0:512], psm[0:8, 4 * 512:5 * 512])
                e.tensor_copy(dsf[:, 512:1024], psm[0:8, 5 * 512:6 * 512])
                e.memset(junk[:, 2:3], 0.0).then_inc(s_pe2, 1)
                e.tensor_tensor(hsf[0:8, :], hsf[0:8, :], pars[0:8, :], ALU.mult)
                e.tensor_tensor(dsf[0:8, :], dsf[0:8, :], pars[0:8, :], ALU.mult)
                e.reduce_max(lmx[0:8, 0:1], hsf[0:8, :], axis=AX.X, apply_absolute_value=True)
                e.reduce_max(lmx[0:8, 1:2], dsf[0:8, :], axis=AX.X, apply_absolute_value=True)
                e.memset(junk[:, 2:3], 0.0).then_inc(s_lm, 1)
                e.wait_ge(s_bb, 32)
                e.reduce_max(o_lm_s := lrow[0:1, 0:1], strided(lrow[0:1, 0:16], 0, 2, 8),
                             axis=AX.X, apply_absolute_value=True)
                e.reduce_max(lrow[0:1, 1:2], strided(lrow[0:1, 0:16], 1, 2, 8),
                             axis=AX.X, apply_absolute_value=True)
                e.memset(junk[:, 2:3], 0.0).then_inc(s_lm, 1)
                def mmcount(j):
                    if j < NO * NKG:
                        return j + 1
                    oc2, k2 = (j - NO * NKG) // NKH, (j - NO * NKG) % NKH
                    return NO * NKG + 33 * oc2 + k2 + 1
                for g in range(NGW):
                    e.wait_ge(s_wl[g % 2], 16 * (g // 2 + 1))
                    if g >= 2:
                        e.wait_ge(s_mm, mmcount(g - 2))
                    e.tensor_copy(wbb[:, (g % 2) * 512:(g % 2 + 1) * 512],
                                  wb[:, (g % 2) * 512:(g % 2 + 1) * 512])
                    e.memset(junk[:, 2:3], 0.0).then_inc(s_wc, 1)
                    if g < NO * NKG and g % NKG == NKG - 1:
                        oc = g // NKG
                        e.wait_ge(s_mm, (oc + 1) * NKG)
                        for m in range(NT):
                            i = oc * NT + m
                            e.wait_ge(s_x1l[i % 4], 16 * (i // 4 + 1))
                            if i >= 4:
                                e.wait_ge(s_st[i % 4], 16 * ((i - 4) // 4 + 1))
                            e.tensor_tensor(otile[:, (i % 4) * 512:(i % 4 + 1) * 512],
                                            psm[:, m * 512:(m + 1) * 512],
                                            x1s[:, (i % 4) * 512:(i % 4 + 1) * 512],
                                            ALU.add)
                            e.memset(junk[:, 2:3], 0.0).then_inc(s_ev, 1)
                    elif g >= NO * NKG and (g - NO * NKG) % NKH == NKH - 1:
                        oc = (g - NO * NKG) // NKH
                        e.wait_ge(s_mm, NO * NKG + (oc + 1) * NKH + (oc + 1))
                        for m in range(NT):
                            i = oc * NT + m
                            if i >= 4:
                                e.wait_ge(s_sth[i % 4], 16 * ((i - 4) // 4 + 1))
                            e.tensor_copy(htile[:, (i % 4) * 512:(i % 4 + 1) * 512],
                                          psm[:, m * 512:(m + 1) * 512])
                            e.memset(junk[:, 2:3], 0.0).then_inc(s_he, 1)

            @blk.tensor
            def _(e):
                e.wait_ge(s_cld, 81)
                e.wait_ge(s_cldg, 32)
                for n in range(2):
                    for k in range(NKH):
                        mmi = e.matmul(psm[0:32, n * 512:(n + 1) * 512],
                                       a1b[:, k * 32:(k + 1) * 32],
                                       x1ts[:, k * TC + n * 512: k * TC + (n + 1) * 512],
                                       start=(k == 0), stop=(k == NKH - 1))
                mmi.then_inc(s_pre, 1)
                for n in range(2):
                    for k in range(NKH):
                        e.matmul(psm[0:8, (2 + n) * 512:(3 + n) * 512],
                                 w8b[:, k * 8:(k + 1) * 8],
                                 x1ts[:, k * TC + n * 512: k * TC + (n + 1) * 512],
                                 start=(k == 0), stop=False)
                    mmi = e.matmul(psm[0:8, (2 + n) * 512:(3 + n) * 512],
                                   b8r[0:1, 0:8], onesw[0:1, 0:512],
                                   start=False, stop=True)
                mmi.then_inc(s_pre, 1)
                e.wait_ge(s_pe2, 1)
                for n in range(2):
                    mmi = e.matmul(psm[0:8, (4 + n) * 512:(5 + n) * 512],
                                   liTs[0:32, 0:8],
                                   zfm[0:32, n * 512:(n + 1) * 512],
                                   start=True, stop=True)
                mmi.then_inc(s_pre, 1)
                for g in range(NGW):
                    e.wait_ge(s_wc, g + 1)
                    if g < NO * NKG:
                        oc, k = g // NKG, g % NKG
                        e.wait_ge(s_gl[k % 2], 16 * ((oc * NKG + k) // 2 + 1))
                        if k == 0:
                            if oc == 0:
                                e.wait_ge(s_pe2, 3)
                                e.wait_ge(s_lm, 1)
                            else:
                                e.wait_ge(s_ev, oc * NT)
                        for m in range(NT):
                            mmi = e.matmul(psm[:, m * 512:(m + 1) * 512],
                                           gts[:, (k % 2) * TC + m * 128:(k % 2) * TC + m * 128 + 128],
                                           wbb[:, (g % 2) * 512:(g % 2 + 1) * 512],
                                           start=(k == 0), stop=(k == NKG - 1))
                        mmi.then_inc(s_mm, 1)
                    else:
                        oc, k = (g - NO * NKG) // NKH, (g - NO * NKG) % NKH
                        if k == 0:
                            if oc == 0:
                                e.wait_ge(s_ev, NO * NT)
                            else:
                                e.wait_ge(s_he, oc * NT)
                        for m in range(NT):
                            mmi = e.matmul(psm[:, m * 512:(m + 1) * 512],
                                           x1ts[:, k * TC + m * 128: k * TC + m * 128 + 128],
                                           wbb[:, (g % 2) * 512:(g % 2 + 1) * 512],
                                           start=(k == 0), stop=False)
                        mmi.then_inc(s_mm, 1)
                        if k == NKH - 1:
                            for m in range(NT):
                                mmi = e.matmul(psm[:, m * 512:(m + 1) * 512],
                                               onesr[0:1, 0:128],
                                               browb[0:1, oc * 512:(oc + 1) * 512],
                                               start=False, stop=True)
                            mmi.then_inc(s_mm, 1)

            @blk.gpsimd
            def _(e):
                e.dma_start(out=a1b[:].rearrange("p (c o) -> p c o", o=32),
                            in_=a1.rearrange("(c p) o -> p c o", p=128)).then_inc(s_cldg, 16)
                e.dma_start(out=w8b[:].rearrange("p (c o) -> p c o", o=8),
                            in_=w1T.rearrange("(c p) o -> p c o", p=128)[:, :, 0:8]
                            ).then_inc(s_cldg, 16)
                def x1load(i):
                    oc, m = divmod(i, NT)
                    e.dma_start(out=x1s[:, (i % 4) * 512:(i % 4 + 1) * 512],
                                in_=xn1[m * 128:(m + 1) * 128, oc * 512:(oc + 1) * 512]
                                ).then_inc(s_x1l[i % 4], 16)
                for i in range(4):
                    x1load(i)
                e.wait_ge(s_lm, 1)
                e.dma_start(out=lm_dram.rearrange("(a b) -> a b", a=8), in_=lmx[0:8, 0:2]).then_inc(s_bb, 16)
                e.wait_ge(s_bb, 16)
                e.dma_start(out=lrow[0:1, :], in_=lm_dram.rearrange("(o k) -> o k", o=1)).then_inc(s_bb, 16)
                e.wait_ge(s_lm, 2)
                e.dma_start(out=o_lm, in_=lrow[0:1, 0:2]).then_inc(s_zo, 16)
                e.dma_start(out=o_zfm, in_=zff[0:32, :]).then_inc(s_zo, 16)
                for i in range(NO * NT):
                    oc, m = divmod(i, NT)
                    e.wait_ge(s_ev, i + 1)
                    e.dma_start(out=o_s[m * 128:(m + 1) * 128, oc * 512:(oc + 1) * 512],
                                in_=otile[:, (i % 4) * 512:(i % 4 + 1) * 512]
                                ).then_inc(s_st[i % 4], 16)
                    if i + 4 < NO * NT:
                        x1load(i + 4)
                for i in range(NO * NT):
                    oc, m = divmod(i, NT)
                    e.wait_ge(s_he, i + 1)
                    e.dma_start(out=o_H[m * 128:(m + 1) * 128, oc * 512:(oc + 1) * 512],
                                in_=htile[:, (i % 4) * 512:(i % 4 + 1) * 512]
                                ).then_inc(s_sth[i % 4], 16)
                for j in range(4):
                    e.wait_ge(s_st[j], 16 * 16)
                    e.wait_ge(s_sth[j], 16 * 16)
                e.wait_ge(s_zo, 32)
    return nc


def run_l6b_oproj(gn, out_proj_w, xn1, W1, b1v, A1, LiMEs):
    nc = cached("l6b_oproj", build_l6b_oproj)
    opwT = np.ascontiguousarray(out_proj_w.T.astype(np.float32))
    w1T = np.ascontiguousarray(W1.T.astype(np.float32))
    liT = np.ascontiguousarray(LiMEs.T.astype(np.float32))
    browv = np.ascontiguousarray(b1v.astype(np.float32).reshape(1, D))
    xn1T_full = np.ascontiguousarray(xn1.T)
    gnT_full = np.ascontiguousarray(gn.T)
    in_maps = []
    for c in range(NCORE):
        sl = slice(c * TC, (c + 1) * TC)
        par = np.zeros((8, TC), np.float32)
        par[:, 1::2] = 1.0
        in_maps.append({
            "gnT": np.ascontiguousarray(gnT_full[:, sl]),
            "opwT": opwT,
            "xn1": np.ascontiguousarray(xn1[sl]),
            "xn1T": np.ascontiguousarray(xn1T_full[:, sl]),
            "w1T": w1T, "b1": browv, "a1": A1.astype(np.float32),
            "liT": liT, "parf": par,
        })
    res = run_bass_kernel_spmd(nc, in_maps, core_ids=list(range(NCORE)))
    s = np.concatenate([np.asarray(r["o_s"]) for r in res.results], 0)
    H = np.concatenate([np.asarray(r["o_H"]) for r in res.results], 0)
    zfm = np.concatenate([np.asarray(r["o_zfm"]) for r in res.results], 1)
    lm = np.stack([np.asarray(r["o_lm"]).reshape(2) for r in res.results], 0)
    return s, H, zfm, lm


def build_l7_final():
    nc = _mknc()
    sdr = nc.dram_tensor("sdr", [TC, D], F32, kind="ExternalInput").ap()
    H = nc.dram_tensor("H", [TC, D], BF16, kind="ExternalInput").ap()
    zfm = nc.dram_tensor("zfm", [32, TC], F32, kind="ExternalInput").ap()
    lm16 = nc.dram_tensor("lm16", [1, 16], F32, kind="ExternalInput").ap()
    liT = nc.dram_tensor("liT", [32, 8], F32, kind="ExternalInput").ap()
    lim = nc.dram_tensor("lim", [8, 32], F32, kind="ExternalInput").ap()
    bmat = nc.dram_tensor("bmat", [32, D], F32, kind="ExternalInput").ap()
    x1 = nc.dram_tensor("x1", [TC, D], F32, kind="ExternalInput").ap()
    onef = nc.dram_tensor("onef", [1, 128], F32, kind="ExternalInput").ap()
    idb = nc.dram_tensor("idb", [128, 128], BF16, kind="ExternalInput").ap()
    o_out = nc.dram_tensor("o_out", [TC, D], F32, kind="ExternalOutput").ap()
    wn_dram = nc.dram_tensor("wn_dram", [TC, 8], F32, kind="Internal").ap()

    NT = TC // 128
    es = ExitStack()
    sbuf = lambda n, s_, d=F32: es.enter_context(nc.sbuf_tensor(n, s_, d))
    sem = lambda n: es.enter_context(nc.semaphore(n))
    with es:
        ss_ = sbuf("ss_", [128, 2 * D])
        x1s = sbuf("x1s", [128, 2 * D])
        hs = sbuf("hs", [128, 2 * D], BF16)
        hf = sbuf("hf", [128, D])
        zs = sbuf("zs", [32, TC])
        zbt = sbuf("zbt", [32, 128], BF16)
        liTb = sbuf("liTb", [32, 8], BF16)
        liTf = sbuf("liTf", [32, 8])
        limb = sbuf("limb", [8, 32], BF16)
        limf = sbuf("limf", [8, 32])
        bmb = sbuf("bmb", [32, D], BF16)
        lrow = sbuf("lrow", [1, 16])
        sclc = sbuf("sclc", [128, 8])
        onc = sbuf("onc", [1, 128])
        sid = sbuf("sid", [128, 128], BF16)
        rt = sbuf("rt", [128, 8 * 8])
        wnt = sbuf("wnt", [128, 8])
        wxp = sbuf("wxp", [128, 8], BF16)
        wfT = sbuf("wfT", [8, 128], BF16)
        pmx = sbuf("pmx", [32, 128])
        gsl = sbuf("gsl", [128, D])
        ot = sbuf("ot", [128, 2 * D])
        junk = sbuf("junk", [128, 4])
        psm = es.enter_context(nc.psum_tensor("psm", [128, 7 * 512], F32))
        psb = es.enter_context(nc.psum_tensor("psb", [128, 1024], BF16))

        s_cld = sem("s_cld")
        s_ld = [sem("s_ld0"), sem("s_ld1")]
        s_scl = sem("s_scl")
        s_dsm = sem("s_dsm")
        s_rt = sem("s_rt")
        s_rs = sem("s_rs")
        s_vb = sem("s_vb")
        s_bw = sem("s_bw")
        s_tw = sem("s_tw")
        s_twe = sem("s_twe")
        s_pm = sem("s_pm")
        s_pme = sem("s_pme")
        s_dl = sem("s_dl")
        s_vd = sem("s_vd")
        s_ge = sem("s_ge")
        s_hf = sem("s_hf")
        s_st = [sem("s_st0"), sem("s_st1")]

        with nc.Block() as blk:
            @blk.sync
            def _(e):
                e.dma_start(out=lrow[0:1, :], in_=lm16).then_inc(s_cld, 16)
                e.dma_start(out=liTf[:], in_=liT).then_inc(s_cld, 16)
                e.dma_start(out=limf[:], in_=lim).then_inc(s_cld, 16)
                e.dma_start(out=onc[:], in_=onef).then_inc(s_cld, 16)
                e.dma_start(out=sid[:], in_=idb).then_inc(s_cld, 16)
                e.dma_start(out=zs[:], in_=zfm).then_inc(s_cld, 16)
                for m in range(NT):
                    sl = m % 2
                    if m >= 2:
                        e.wait_ge(s_ge, 2 * (m - 1))
                        e.wait_ge(s_hf, m - 1)
                    e.dma_start(out=ss_[:, sl * D:(sl + 1) * D],
                                in_=sdr[m * 128:(m + 1) * 128, :]).then_inc(s_ld[sl], 16)
                    e.dma_start(out=x1s[:, sl * D:(sl + 1) * D],
                                in_=x1[m * 128:(m + 1) * 128, :]).then_inc(s_ld[sl], 16)
                    e.dma_start(out=hs[:, sl * D:(sl + 1) * D],
                                in_=H[m * 128:(m + 1) * 128, :]).then_inc(s_ld[sl], 16)

            @blk.gpsimd
            def _(e):
                e.dma_start(out=bmb[:], in_=bmat).then_inc(s_cld, 16)
                for m in range(NT):
                    e.wait_ge(s_rt, 5 * m + 5)
                    e.dma_start(out=wn_dram[m * 128:(m + 1) * 128, :],
                                in_=wnt[:]).then_inc(s_bw, 16)
                    e.wait_ge(s_bw, 32 * m + 16)
                    if m >= 1:
                        e.wait_ge(s_tw, m)
                    e.dma_start(out=wxp[:],
                                in_=bass.AP(wn_dram.tensor, wn_dram.offset + m * 1024 + 8,
                                            [[16, 64], [0, 2], [1, 8]])).then_inc(s_bw, 16)
                    e.wait_ge(s_ge, 2 * (m + 1))
                    e.dma_start(out=o_out[m * 128:(m + 1) * 128, :],
                                in_=ot[:, (m % 2) * D:(m % 2 + 1) * D]).then_inc(s_st[m % 2], 16)
                e.wait_ge(s_st[0], 64)
                e.wait_ge(s_st[1], 64)

            @blk.vector
            def _(e):
                vb = [0]

                def vbar():
                    e.memset(junk[:, 2:3], 0.0).then_inc(s_vb, 1)
                    vb[0] += 1
                    e.wait_ge(s_vb, vb[0])
                e.wait_ge(s_cld, 112)
                e.tensor_copy(liTb[:], liTf[:])
                e.tensor_copy(limb[:], limf[:])
                e.reduce_max(lrow[0:1, 2:3], strided(lrow[0:1, 0:16], 0, 2, 8), axis=AX.X)
                e.reduce_max(lrow[0:1, 3:4], strided(lrow[0:1, 0:16], 1, 2, 8), axis=AX.X)
                e.tensor_scalar(lrow[0:1, 2:4], lrow[0:1, 2:4], 1e-6, None, ALU.max)
                e.memset(junk[0:1, 0:1], 0.0).then_inc(s_scl, 1)
                e.wait_ge(s_scl, 2)
                e.reciprocal(sclc[:, 0:2], psm[:, 6 * 512:6 * 512 + 2])
                e.tensor_scalar(sclc[:, 0:2], sclc[:, 0:2], 0.5, None, ALU.mult)
                vbar()
                e.memset(junk[0:1, 0:1], 0.0).then_inc(s_scl, 1)
                for m in range(NT):
                    e.wait_ge(s_hf, m + 1)
                    e.wait_ge(s_dsm, m + 1)
                    e.tensor_scalar(rt[:, 0:8], hf[:, 0:8], sclc[:, 0:1], None, ALU.mult)
                    e.scalar_tensor_tensor(rt[:, 0:8], psm[:, 0:8], sclc[:, 1:2],
                                           rt[:, 0:8], ALU.mult, ALU.add)
                    e.reduce_max(rt[:, 8:9], rt[:, 0:8], axis=AX.X, negate=True)
                    e.memset(junk[:, 2:3], 0.0).then_inc(s_rt, 1)
                    e.wait_ge(s_rs, 4 * m + 1)
                    e.reciprocal(rt[:, 25:26], rt[:, 24:25])
                    vbar()
                    e.tensor_scalar(rt[:, 16:24], rt[:, 16:24], rt[:, 25:26], None, ALU.mult)
                    e.reduce_max(rt[:, 26:27], rt[:, 16:24], axis=AX.X)
                    vbar()
                    e.tensor_scalar(rt[:, 32:40], rt[:, 16:24], rt[:, 26:27], None, ALU.is_equal)
                    e.scalar_tensor_tensor(rt[:, 32:40], rt[:, 32:40], -1e9,
                                           rt[:, 16:24], ALU.mult, ALU.add)
                    e.reduce_max(rt[:, 27:28], rt[:, 32:40], axis=AX.X)
                    vbar()
                    e.tensor_scalar(rt[:, 28:29], rt[:, 27:28], -2.0, None, ALU.mult)
                    e.memset(junk[:, 2:3], 0.0).then_inc(s_rt, 1)
                    e.wait_ge(s_rs, 4 * m + 2)
                    if m >= 1:
                        e.wait_ge(s_bw, 32 * (m - 1) + 16)
                    e.tensor_tensor(rt[:, 40:48], rt[:, 16:24], rt[:, 48:56], ALU.mult)
                    e.reduce_sum(rt[:, 29:30], rt[:, 40:48], axis=AX.X)
                    vbar()
                    e.tensor_scalar(rt[:, 29:30], rt[:, 29:30], 1e-9, None, ALU.add)
                    e.reciprocal(rt[:, 30:31], rt[:, 29:30])
                    vbar()
                    e.tensor_scalar(wnt[:], rt[:, 40:48], rt[:, 30:31], None, ALU.mult)
                    e.memset(junk[:, 2:3], 0.0).then_inc(s_rt, 3)
                    e.wait_ge(s_tw, m + 1)
                    e.tensor_copy(wfT[0:8, :], psb[0:8, 0:128])
                    e.memset(junk[:, 2:3], 0.0).then_inc(s_twe, 1)
                    e.wait_ge(s_pm, m + 1)
                    e.tensor_tensor(zbt[0:32, :], psm[0:32, 512:512 + 128],
                                    zs[0:32, m * 128:(m + 1) * 128], ALU.mult)
                    e.memset(junk[:, 2:3], 0.0).then_inc(s_pme, 1)
                    for hh in range(2):
                        e.wait_ge(s_dl, 2 * m + hh + 1)
                        for nt in range(4):
                            e.scalar_tensor_tensor(
                                gsl[:, (hh * 4 + nt) * 512:(hh * 4 + nt + 1) * 512],
                                psm[:, (2 + nt) * 512:(3 + nt) * 512], 0.5,
                                hf[:, (hh * 4 + nt) * 512:(hh * 4 + nt + 1) * 512],
                                ALU.mult, ALU.add)
                        e.memset(junk[:, 2:3], 0.0).then_inc(s_vd, 1)
                        e.wait_ge(s_rs, 4 * m + 3 + hh)
                        if m >= 2 and hh == 0:
                            e.wait_ge(s_st[m % 2], 16 * (m // 2))
                        sl = m % 2
                        hv = gsl[:, hh * 2048:(hh + 1) * 2048]
                        e.tensor_tensor(ot[:, sl * D + hh * 2048: sl * D + (hh + 1) * 2048],
                                        ss_[:, sl * D + hh * 2048: sl * D + (hh + 1) * 2048],
                                        hv, ALU.mult)
                        e.tensor_add(ot[:, sl * D + hh * 2048: sl * D + (hh + 1) * 2048],
                                     ot[:, sl * D + hh * 2048: sl * D + (hh + 1) * 2048],
                                     x1s[:, sl * D + hh * 2048: sl * D + (hh + 1) * 2048])
                        e.memset(junk[:, 2:3], 0.0).then_inc(s_ge, 1)

            @blk.scalar
            def _(e):
                for m in range(NT):
                    sl = m % 2
                    e.wait_ge(s_ld[sl], 48 * (m // 2 + 1))
                    if m >= 1:
                        e.wait_ge(s_vd, 2 * m)
                    e.activation(out=hf[:], in_=hs[:, sl * D:(sl + 1) * D], func=AF.Copy)
                    e.memzero(junk[:, 1:2]).then_inc(s_hf, 1)
                    e.wait_ge(s_rt, 5 * m + 1)
                    e.activation(out=rt[:, 16:24], in_=rt[:, 0:8], func=AF.Exp,
                                 bias=rt[:, 8:9], accum_out=rt[:, 24:25])
                    e.memzero(junk[:, 1:2]).then_inc(s_rs, 1)
                    e.wait_ge(s_rt, 5 * m + 2)
                    e.activation(out=rt[:, 48:56], in_=rt[:, 16:24], func=AF.Sigmoid,
                                 scale=2.0, bias=rt[:, 28:29])
                    e.memzero(junk[:, 1:2]).then_inc(s_rs, 1)
                    for hh in range(2):
                        e.wait_ge(s_vd, 2 * m + hh + 1)
                        e.activation(out=gsl[:, hh * 2048:(hh + 1) * 2048],
                                     in_=gsl[:, hh * 2048:(hh + 1) * 2048], func=AF.Silu)
                        e.memzero(junk[:, 1:2]).then_inc(s_rs, 1)

            @blk.tensor
            def _(e):
                e.wait_ge(s_cld, 112)
                e.wait_ge(s_scl, 1)
                e.matmul(psm[:, 6 * 512:6 * 512 + 2], onc[0:1, :], lrow[0:1, 2:4],
                         start=True, stop=True).then_inc(s_scl, 1)
                for m in range(NT):
                    e.wait_ge(s_scl, 3)
                    if m >= 1:
                        e.wait_ge(s_rt, 5 * (m - 1) + 1)
                    e.matmul(psm[:, 0:8], zs[0:32, m * 128:(m + 1) * 128],
                             liTf[0:32, :], start=True, stop=True).then_inc(s_dsm, 1)
                    e.wait_ge(s_bw, 32 * m + 32)
                    if m >= 1:
                        e.wait_ge(s_twe, m)
                    e.transpose(psb[0:8, 0:128], wxp[:], sid[:]).then_inc(s_tw, 1)
                    e.wait_ge(s_twe, m + 1)
                    if m >= 1:
                        e.wait_ge(s_pme, m)
                    e.matmul(psm[0:32, 512:512 + 128], limb[0:8, :], wfT[0:8, :],
                             start=True, stop=True).then_inc(s_pm, 1)
                    e.wait_ge(s_pme, m + 1)
                    for hh in range(2):
                        if 2 * m + hh >= 1:
                            e.wait_ge(s_vd, 2 * m + hh)
                        mmi = None
                        for nt in range(4):
                            mmi = e.matmul(psm[:, (2 + nt) * 512:(3 + nt) * 512],
                                           zbt[0:32, :],
                                           bmb[0:32, (hh * 4 + nt) * 512:(hh * 4 + nt + 1) * 512],
                                           start=True, stop=True)
                        mmi.then_inc(s_dl, 1)
    return nc


def run_l7_final(s, H, zfm, lm, x1, LiMEs, Bmat):
    nc = cached("l7_final", build_l7_final)
    lm16 = np.ascontiguousarray(lm.reshape(1, 16).astype(np.float32))
    liT = np.ascontiguousarray(LiMEs.T.astype(np.float32))
    limf = np.ascontiguousarray(LiMEs.astype(np.float32))
    bm = np.ascontiguousarray(Bmat.astype(np.float32))
    onef = np.ones((1, 128), np.float32)
    idb = np.eye(128, dtype=BF)
    in_maps = []
    for c in range(NCORE):
        sl = slice(c * TC, (c + 1) * TC)
        in_maps.append({
            "sdr": np.ascontiguousarray(s[sl]),
            "H": np.ascontiguousarray(H[sl]),
            "zfm": np.ascontiguousarray(zfm[:, sl]),
            "lm16": lm16, "liT": liT, "lim": limf, "bmat": bm,
            "x1": np.ascontiguousarray(x1[sl]),
            "onef": onef, "idb": idb,
        })
    res = run_bass_kernel_spmd(nc, in_maps, core_ids=list(range(NCORE)))
    return np.concatenate([np.asarray(r["o_out"]) for r in res.results], 0)


def run_l1_qkv(x_flat, wq, wk, wv, qweight, wscale, wzp):
    nc = cached("l1_qkv", build_l1_qkv)
    wT = np.ascontiguousarray(np.concatenate([wq, wk, wv], 0).T)
    sctab = sincos_tables()
    zp2 = np.ascontiguousarray(np.concatenate([wzp, wscale], 1).astype(np.float32))
    iden = np.eye(128, dtype=np.float32)
    in_maps = []
    for c in range(NCORE):
        half = c % 2
        in_maps.append({
            "x": np.ascontiguousarray(x_flat[c * TC:(c + 1) * TC]),
            "wT": wT,
            "sc": np.ascontiguousarray(sctab[half * TC:(half + 1) * TC]),
            "qw": np.ascontiguousarray(qweight[c * 512:(c + 1) * 512]),
            "zp": np.ascontiguousarray(zp2[c * 512:(c + 1) * 512]),
            "iden": iden,
        })
    res = run_bass_kernel_spmd(nc, in_maps, core_ids=list(range(NCORE)))
    qkv = np.concatenate([np.asarray(r["o_qkv"]) for r in res.results], 0)
    xn = np.concatenate([np.asarray(r["o_xn"]) for r in res.results], 0)
    wdeq = np.concatenate([np.asarray(r["o_wd"]) for r in res.results], 0)
    return qkv, xn, wdeq


def kernel(**inputs):
    x = np.asarray(inputs["L0_attn_hidden_states"], np.float32)
    x_flat = np.ascontiguousarray(x.reshape(B * S, D))
    qkv, xn, wdeq = run_l1_qkv(
        x_flat, inputs["L0_attn_wq"], inputs["L0_attn_wk"], inputs["L0_attn_wv"],
        inputs["L0_ffn_qweight"], inputs["L0_ffn_weight_scale"],
        inputs["L0_ffn_weight_zero_point"])
    ao = run_l2_attn(qkv)
    x1 = run_l3_mix(x_flat, ao, xn, wdeq, inputs["L0_attn_wo"], inputs["L0_ffn_bias"])
    xn1, gate, hbc, dt = run_l4_proj(
        x1, inputs["L1_attn_in_proj_w"], inputs["L1_attn_norm_w"],
        inputs["L1_attn_conv_w"], inputs["L1_attn_conv_b"], inputs["L1_attn_dt_bias"])
    y = run_l5_ssd(hbc, dt, inputs["L1_attn_A_log"], inputs["L1_attn_Dp"])
    gn = run_l6a_gate(y, gate, inputs["L1_attn_gnorm_w"])
    s, H, zfm, lm = run_l6b_oproj(gn, inputs["L1_attn_out_proj_w"], xn1,
                                  inputs["L1_ffn_W"], inputs["L1_ffn_b"],
                                  inputs["L1_ffn_A"], inputs["L1_ffn_LiMEs"])
    out = run_l7_final(s, H, zfm, lm, x1, inputs["L1_ffn_LiMEs"], inputs["L1_ffn_Bmat"])
    return out.reshape(B, S, D).astype(np.float32)


# version 02 (final)
# speedup vs baseline: 1.1059x; 1.1059x over previous
import numpy as np
import ml_dtypes
from contextlib import ExitStack
import concourse.bass as bass
import concourse.mybir as mybir
from concourse.bass_utils import run_bass_kernel_spmd

F32 = mybir.dt.float32
BF16 = mybir.dt.bfloat16
I32 = mybir.dt.int32
AF = mybir.ActivationFunctionType
ALU = mybir.AluOpType
AX = mybir.AxisListType
BF = ml_dtypes.bfloat16

D = 4096
S = 2048
B = 4
NCORE = 8
TC = 1024
EPS = 1e-5

_prog_cache = {}
RACE_CHECK = True


def _mknc():
    return bass.Bass("TRN2", target_bir_lowering=False,
                     detect_race_conditions=RACE_CHECK)


def cached(name, builder):
    if name not in _prog_cache:
        _prog_cache[name] = builder()
    return _prog_cache[name]


def sincos_tables():
    rd = 64
    inv_freq = (1.0 / 10000.0 ** (np.arange(0, rd, 2, dtype=np.float32) / rd)).astype(np.float32)
    ang = np.arange(S, dtype=np.float32)[:, None] * inv_freq[None, :]
    sin = np.repeat(np.sin(ang), 2, axis=-1).astype(np.float32)
    cos = np.repeat(np.cos(ang), 2, axis=-1).astype(np.float32)
    sin_s = sin.copy()
    sin_s[:, 0::2] *= -1.0
    return np.concatenate([cos, sin_s], axis=1)


def swap_view(ap):
    dims = [list(d) for d in ap.ap]
    assert dims[-1][0] == 1 and dims[-1][1] % 2 == 0
    nd = dims[:-1] + [[2, dims[-1][1] // 2], [-1, 2]]
    return bass.AP(ap.tensor, ap.offset + 1, nd)


def strided(ap, start, step, count):
    dims = [list(d) for d in ap.ap]
    assert dims[-1][0] == 1
    nd = dims[:-1] + [[step, count]]
    return bass.AP(ap.tensor, ap.offset + start, nd)


class SelfSync:

    def __init__(self, sem):
        self.sem = sem
        self.n = 0

    def barrier(self, e, inst):
        inst.then_inc(self.sem, 1)
        self.n += 1
        e.wait_ge(self.sem, self.n)


def emit_ln(e, stat, xv, out_f32, ssync, eps=EPS):
    dim = xv.shape[-1]
    e.activation(out=out_f32, in_=xv, func=AF.Square, accum_out=stat[:, 1:2])
    e.activation(out=out_f32, in_=xv, func=AF.Copy, accum_out=stat[:, 0:1])
    e.mul(stat[:, 2:3], stat[:, 0:1], 1.0 / dim)
    e.mul(stat[:, 3:4], stat[:, 1:2], 1.0 / dim)
    e.activation(out=stat[:, 4:5], in_=stat[:, 2:3], func=AF.Square)
    ins = e.mul(stat[:, 4:5], stat[:, 4:5], -1.0)
    ssync.barrier(e, ins)
    e.activation(out=stat[:, 3:4], in_=stat[:, 3:4], func=AF.Identity, bias=stat[:, 4:5])
    e.activation(out=stat[:, 3:4], in_=stat[:, 3:4], func=AF.Copy, bias=eps)
    e.activation(out=stat[:, 6:7], in_=stat[:, 3:4], func=AF.Ln)
    ins = e.activation(out=stat[:, 6:7], in_=stat[:, 6:7], func=AF.Exp, scale=-0.5)
    ssync.barrier(e, ins)
    e.mul(stat[:, 7:8], stat[:, 2:3], stat[:, 6:7])
    ins = e.mul(stat[:, 7:8], stat[:, 7:8], -1.0)
    ssync.barrier(e, ins)
    return e.activation(out=out_f32, in_=xv, func=AF.Identity,
                        scale=stat[:, 6:7], bias=stat[:, 7:8])


def build_l1_qkv():
    nc = _mknc()
    x = nc.dram_tensor("x", [TC, D], F32, kind="ExternalInput").ap()
    wT = nc.dram_tensor("wT", [D, 3 * D], F32, kind="ExternalInput").ap()
    sc = nc.dram_tensor("sc", [TC, 128], F32, kind="ExternalInput").ap()
    qw = nc.dram_tensor("qw", [512, 3072], I32, kind="ExternalInput").ap()
    zp = nc.dram_tensor("zp", [512, 2], F32, kind="ExternalInput").ap()
    iden = nc.dram_tensor("iden", [128, 128], F32, kind="ExternalInput").ap()
    o_qkv = nc.dram_tensor("o_qkv", [TC, 3 * D], BF16, kind="ExternalOutput").ap()
    o_xn = nc.dram_tensor("o_xn", [TC, D], F32, kind="ExternalOutput").ap()
    o_wd = nc.dram_tensor("o_wd", [512, D], BF16, kind="ExternalOutput").ap()

    NT = TC // 128
    NK = D // 128
    NO = 3 * D // 512
    NG = NO * NK

    es = ExitStack()
    sbuf = lambda n, s, d=F32: es.enter_context(nc.sbuf_tensor(n, s, d))
    sem = lambda n: es.enter_context(nc.semaphore(n))
    with es:
        xt = sbuf("xt", [128, 2 * D])
        xnf = sbuf("xnf", [128, 2 * D])
        xnT = sbuf("xnT", [128, NK * TC], BF16)
        sctab = sbuf("sctab", [128, NT * 128])
        sid = sbuf("sid", [128, 128])
        wbuf = sbuf("wbuf", [128, 2 * 512])
        wbf = sbuf("wbf", [128, 2 * 512], BF16)
        stat = sbuf("stat", [128, 8])
        otile = sbuf("otile", [128, 4 * 512], BF16)
        rot = sbuf("rot", [128, 2 * 64])
        qwt = es.enter_context(nc.sbuf_tensor("qwt", [128, 2 * 3072], I32))
        uw = es.enter_context(nc.sbuf_tensor("uw", [128, 2048], I32))
        uwb = es.enter_context(nc.sbuf_tensor("uwb", [128, 512], I32))
        uf = sbuf("uf", [128, 2048])
        zps = sbuf("zps", [128, 2 * 4])
        wdt = sbuf("wdt", [128, 2 * D], BF16)
        psm = es.enter_context(nc.psum_tensor("psm", [128, 8 * 512], F32))

        s_xld = [sem("s_xld0"), sem("s_xld1")]
        s_cld = sem("s_cld")
        s_qld = [sem("s_qld0"), sem("s_qld1")]
        s_xn = sem("s_xn")
        s_tr = sem("s_tr")
        s_ev = sem("s_ev")
        s_w = [sem("s_w0"), sem("s_w1")]
        s_wb = sem("s_wb")
        s_mm = sem("s_mm")
        s_cp = sem("s_cp")
        s_rot = sem("s_rot")
        s_out = [sem(f"s_out{j}") for j in range(4)]
        s_xout = [sem("s_xout0"), sem("s_xout1")]
        s_dq = sem("s_dq")
        s_dqo = [sem("s_dqo0"), sem("s_dqo1")]
        s_ss = sem("s_ss")

        with nc.Block() as blk:
            @blk.gpsimd
            def _(e):
                e.dma_start(out=sid[:], in_=iden).then_inc(s_cld, 16)
                e.dma_start(out=sctab[:].rearrange("p (t c) -> p t c", c=128),
                            in_=sc.rearrange("(t p) c -> p t c", p=128)).then_inc(s_cld, 16)
                e.dma_start(out=zps[:].rearrange("p (t c) -> p t c", c=2),
                            in_=zp.rearrange("(t p) c -> p t c", p=128)).then_inc(s_cld, 16)
                e.dma_start(out=xt[:, 0:D], in_=x[0:128, :]).then_inc(s_xld[0], 16)
                e.dma_start(out=xt[:, D:2 * D], in_=x[128:256, :]).then_inc(s_xld[1], 16)
                for m in range(NT):
                    e.wait_ge(s_xn, m + 1)
                    e.dma_start(out=o_xn[m * 128:(m + 1) * 128, :],
                                in_=xnf[:, (m % 2) * D:(m % 2 + 1) * D]
                                ).then_inc(s_xout[m % 2], 16)
                    if m + 2 < NT:
                        e.dma_start(out=xt[:, (m % 2) * D:(m % 2 + 1) * D],
                                    in_=x[(m + 2) * 128:(m + 3) * 128, :]
                                    ).then_inc(s_xld[m % 2], 16)
                for i in range(NO * NT):
                    oc, m = divmod(i, NT)
                    if oc < 16:
                        e.wait_ge(s_rot, i + 1)
                    e.wait_ge(s_cp, i + 1)
                    e.dma_start(out=o_qkv[m * 128:(m + 1) * 128, oc * 512:(oc + 1) * 512],
                                in_=otile[:, (i % 4) * 512:(i % 4 + 1) * 512]
                                ).then_inc(s_out[i % 4], 16)
                e.dma_start(out=qwt[:, 0:3072], in_=qw[0:128, :]).then_inc(s_qld[0], 16)
                e.dma_start(out=qwt[:, 3072:6144], in_=qw[128:256, :]).then_inc(s_qld[1], 16)
                for t in range(4):
                    e.wait_ge(s_dq, (t + 1) * 10)
                    e.dma_start(out=o_wd[t * 128:(t + 1) * 128, :],
                                in_=wdt[:, (t % 2) * D:(t % 2 + 1) * D]
                                ).then_inc(s_dqo[t % 2], 16)
                    if t + 2 < 4:
                        e.dma_start(out=qwt[:, (t % 2) * 3072:(t % 2 + 1) * 3072],
                                    in_=qw[(t + 2) * 128:(t + 3) * 128, :]
                                    ).then_inc(s_qld[t % 2], 16)
                for j in range(4):
                    e.wait_ge(s_out[j], 16 * (NO * NT // 4))
                e.wait_ge(s_dqo[0], 32)
                e.wait_ge(s_dqo[1], 32)
                e.wait_ge(s_xout[0], 16 * 4)
                e.wait_ge(s_xout[1], 16 * 4)

            @blk.sync
            def _(e):
                for g in range(NG):
                    buf = g % 2
                    if g >= 2:
                        e.wait_ge(s_wb, g - 1)
                    oc, k = divmod(g, NK)
                    e.dma_start(out=wbuf[:, buf * 512:(buf + 1) * 512],
                                in_=wT[k * 128:(k + 1) * 128, oc * 512:(oc + 1) * 512]
                                ).then_inc(s_w[buf], 16)

            @blk.scalar
            def _(e):
                ssync = SelfSync(s_ss)
                e.wait_ge(s_cld, 48)
                for m in range(NT):
                    e.wait_ge(s_xld[m % 2], 16 * (m // 2 + 1))
                    if m >= 2:
                        e.wait_ge(s_tr, (m - 1) * NK)
                        e.wait_ge(s_xout[m % 2], 16 * (m // 2))
                    emit_ln(e, stat, xt[:, (m % 2) * D:(m % 2 + 1) * D],
                            xnf[:, (m % 2) * D:(m % 2 + 1) * D], ssync)
                    e.memzero(stat[:, 0:1]).then_inc(s_xn, 1)
                for i in range(NO * NT):
                    oc, m = divmod(i, NT)
                    if m == 0:
                        e.wait_ge(s_mm, (oc + 1) * NK)
                    if i >= 4:
                        e.wait_ge(s_out[i % 4], 16 * ((i - 4) // 4 + 1))
                    ot = otile[:, (i % 4) * 512:(i % 4 + 1) * 512]
                    pv = psm[:, m * 512:(m + 1) * 512]
                    if oc < 16:
                        e.activation(out=ot[:, 64:256], in_=pv[:, 64:256], func=AF.Copy)
                        e.activation(out=ot[:, 320:512], in_=pv[:, 320:512], func=AF.Copy)
                    else:
                        e.activation(out=ot, in_=pv, func=AF.Copy)
                    e.memzero(stat[:, 0:1]).then_inc(s_cp, 1)

            @blk.tensor
            def _(e):
                for i in range(NT * NK):
                    m, k = divmod(i, NK)
                    if k == 0:
                        e.wait_ge(s_xn, m + 1)
                    if i >= 8:
                        e.wait_ge(s_ev, i - 7)
                    e.transpose(psm[:, (i % 8) * 512:(i % 8) * 512 + 128],
                                xnf[:, (m % 2) * D + k * 128:(m % 2) * D + k * 128 + 128],
                                sid[:]).then_inc(s_tr, 1)
                e.wait_ge(s_ev, NT * NK + 1)
                for g in range(NG):
                    oc, k = divmod(g, NK)
                    buf = g % 2
                    e.wait_ge(s_wb, g + 1)
                    if k == 0 and oc > 0:
                        e.wait_ge(s_cp, oc * NT)
                        if oc <= 16:
                            e.wait_ge(s_rot, min(oc, 16) * NT)
                    for m in range(NT):
                        e.matmul(psm[:, m * 512:(m + 1) * 512],
                                 xnT[:, k * TC + m * 128: k * TC + m * 128 + 128],
                                 wbf[:, buf * 512:(buf + 1) * 512],
                                 start=(k == 0), stop=(k == NK - 1)).then_inc(
                                     s_mm, 1) if m == NT - 1 else e.matmul(
                            psm[:, m * 512:(m + 1) * 512],
                            xnT[:, k * TC + m * 128: k * TC + m * 128 + 128],
                            wbf[:, buf * 512:(buf + 1) * 512],
                            start=(k == 0), stop=(k == NK - 1))

            def rotary(e, oc):
                e.wait_ge(s_mm, (oc + 1) * NK)
                for m in range(NT):
                    i = oc * NT + m
                    e.wait_ge(s_cp, i + 1)
                    cosv = sctab[:, m * 128: m * 128 + 64]
                    sinv = sctab[:, m * 128 + 64: m * 128 + 128]
                    for r in (0, 256):
                        pv = psm[:, m * 512 + r: m * 512 + r + 64]
                        e.tensor_tensor(rot[:, 0:64], pv, cosv, ALU.mult)
                        e.tensor_tensor(rot[:, 64:128], swap_view(pv), sinv, ALU.mult)
                        e.tensor_tensor(
                            otile[:, (i % 4) * 512 + r:(i % 4) * 512 + r + 64],
                            rot[:, 0:64], rot[:, 64:128], ALU.add)
                    e.memset(rot[:, 0:1], 0.0).then_inc(s_rot, 1)

            @blk.vector
            def _(e):
                for i in range(NT * NK):
                    m, k = divmod(i, NK)
                    e.wait_ge(s_tr, i + 1)
                    e.tensor_copy(xnT[:, k * TC + m * 128: k * TC + m * 128 + 128],
                                  psm[:, (i % 8) * 512:(i % 8) * 512 + 128]).then_inc(s_ev, 1)
                e.memset(rot[:, 0:1], 0.0).then_inc(s_ev, 1)
                for oc in range(NO):
                    if 1 <= oc <= 16:
                        rotary(e, oc - 1)
                    for k in range(NK):
                        g = oc * NK + k
                        buf = g % 2
                        e.wait_ge(s_w[buf], 16 * (g // 2 + 1))
                        if g >= 2:
                            e.wait_ge(s_mm, g - 1)
                        e.tensor_copy(wbf[:, buf * 512:(buf + 1) * 512],
                                      wbuf[:, buf * 512:(buf + 1) * 512])
                        e.memset(rot[:, 0:1], 0.0).then_inc(s_wb, 1)
                for t in range(4):
                    e.wait_ge(s_qld[t % 2], 16 * (t // 2 + 1))
                    if t >= 2:
                        e.wait_ge(s_dqo[t % 2], 16 * (t // 2))
                    qv = qwt[:, (t % 2) * 3072:(t % 2 + 1) * 3072]
                    wv = wdt[:, (t % 2) * D:(t % 2 + 1) * D]
                    for h in range(2):
                        e.tensor_single_scalar(strided(uw[:], 0, 4, 512),
                                               strided(qv, h * 1536 + 0, 3, 512),
                                               63, ALU.bitwise_and)
                        e.tensor_single_scalar(uwb[:],
                                               strided(qv, h * 1536 + 0, 3, 512),
                                               6, ALU.logical_shift_right)
                        e.tensor_single_scalar(strided(uw[:], 1, 4, 512), uwb[:],
                                               63, ALU.bitwise_and)
                        e.tensor_single_scalar(strided(uw[:], 2, 4, 512),
                                               strided(qv, h * 1536 + 1, 3, 512),
                                               63, ALU.bitwise_and)
                        e.tensor_single_scalar(strided(uw[:], 3, 4, 512),
                                               strided(qv, h * 1536 + 2, 3, 512),
                                               63, ALU.bitwise_and)
                        e.tensor_copy(uf[:], uw[:])
                        e.tensor_scalar(wv[:, h * 2048:(h + 1) * 2048], uf[:],
                                        zps[:, 2 * t:2 * t + 1], zps[:, 2 * t + 1:2 * t + 2],
                                        ALU.subtract, ALU.mult).then_inc(s_dq, 4 if h == 0 else 5)
                    e.memset(rot[:, 0:1], 0.0).then_inc(s_dq, 1)
    return nc


def build_l2_attn():
    nc = _mknc()
    qT = nc.dram_tensor("qT", [8, 256, S], BF16, kind="ExternalInput").ap()
    kT = nc.dram_tensor("kT", [8, 256, S], BF16, kind="ExternalInput").ap()
    v = nc.dram_tensor("v", [8, S, 256], BF16, kind="ExternalInput").ap()
    cm = nc.dram_tensor("cm", [4, 128, 512], F32, kind="ExternalInput").ap()
    iden = nc.dram_tensor("iden", [128, 128], BF16, kind="ExternalInput").ap()
    o_ao = nc.dram_tensor("o_ao", [S, 2048], BF16, kind="ExternalOutput").ap()

    NQT = S // 128
    NS = 8 * NQT

    def nk512(qt):
        return qt // 4 + 1

    PT_BASE = [0]
    for s in range(NS):
        PT_BASE.append(PT_BASE[-1] + (s % NQT) + 1)

    es = ExitStack()
    sbuf = lambda n, s_, d=F32: es.enter_context(nc.sbuf_tensor(n, s_, d))
    sem = lambda n: es.enter_context(nc.semaphore(n))
    with es:
        qs = sbuf("qs", [128, 2 * 2 * S], BF16)
        ks = sbuf("ks", [128, 2 * 2 * S], BF16)
        vs = sbuf("vs", [128, 2 * 16 * 256], BF16)
        cmk = sbuf("cmk", [128, 4 * 512])
        sid = sbuf("sid", [128, 128], BF16)
        sc_sb = sbuf("sc_sb", [128, 4 * 512])
        p_sb = sbuf("p_sb", [128, 4 * 512], BF16)
        pT_sb = sbuf("pT_sb", [128, 2 * 128], BF16)
        cols = sbuf("cols", [128, 4])
        ao_sb = sbuf("ao_sb", [128, NQT * 2048], BF16)
        psc = es.enter_context(nc.psum_tensor("psc", [128, 4 * 512], F32))
        ptp = es.enter_context(nc.psum_tensor("ptp", [128, 2 * 1024], BF16))
        pav = es.enter_context(nc.psum_tensor("pav", [128, 2 * 512], F32))

        s_cld = sem("s_cld")
        s_hl = [sem("s_hl0"), sem("s_hl1")]
        s_mm = sem("s_mm")
        s_scv = sem("s_scv")
        s_ex = sem("s_ex")
        s_rc = sem("s_rc")
        s_pt = sem("s_pt")
        s_pte = sem("s_pte")
        s_av = sem("s_av")
        s_ae = sem("s_ae")
        s_out = sem("s_out")

        with nc.Block() as blk:
            @blk.sync
            def _(e):
                e.dma_start(out=sid[:], in_=iden).then_inc(s_cld, 16)
                e.dma_start(out=cmk[:].rearrange("p (r c) -> p r c", c=512),
                            in_=cm.rearrange("r p c -> p r c")).then_inc(s_cld, 16)
                for h in range(8):
                    sl = h % 2
                    if h >= 2:
                        e.wait_ge(s_ae, (h - 1) * NQT)
                    e.dma_start(out=qs[:, sl * 2 * S: (sl + 1) * 2 * S]
                                .rearrange("p (c t) -> p c t", t=S),
                                in_=qT[h].rearrange("(c p) t -> p c t", p=128)
                                ).then_inc(s_hl[sl], 16)
                    e.dma_start(out=ks[:, sl * 2 * S: (sl + 1) * 2 * S]
                                .rearrange("p (c t) -> p c t", t=S),
                                in_=kT[h].rearrange("(c p) t -> p c t", p=128)
                                ).then_inc(s_hl[sl], 16)
                    e.dma_start(out=vs[:, sl * 16 * 256: (sl + 1) * 16 * 256]
                                .rearrange("p (c t) -> p c t", t=256),
                                in_=v[h].rearrange("(c p) t -> p c t", p=128)
                                ).then_inc(s_hl[sl], 16)

            def ptav(e, s):
                h, qt = divmod(s, NQT)
                sl = h % 2
                vv = vs[:, sl * 16 * 256: (sl + 1) * 16 * 256]
                e.wait_ge(s_ex, s + 1)
                if s >= 2:
                    e.wait_ge(s_ae, s - 1)
                for kc in range(qt + 1):
                    npt = PT_BASE[s] + kc
                    bank = npt % 2
                    if npt >= 2:
                        e.wait_ge(s_pte, npt - 1)
                    e.transpose(ptp[:, bank * 1024: bank * 1024 + 128],
                                p_sb[:, kc * 128:(kc + 1) * 128],
                                sid[:]).then_inc(s_pt, 1)
                    e.wait_ge(s_pte, npt + 1)
                    mmi = e.matmul(pav[:, (s % 2) * 512:(s % 2) * 512 + 256],
                                   pT_sb[:, bank * 128:(bank + 1) * 128],
                                   vv[:, kc * 256:(kc + 1) * 256],
                                   start=(kc == 0), stop=(kc == qt))
                mmi.then_inc(s_av, 1)

            @blk.tensor
            def _(e):
                for s in range(NS):
                    h, qt = divmod(s, NQT)
                    sl = h % 2
                    if qt == 0:
                        e.wait_ge(s_hl[sl], 48 * (h // 2 + 1))
                    qv = qs[:, sl * 2 * S: (sl + 1) * 2 * S]
                    kv = ks[:, sl * 2 * S: (sl + 1) * 2 * S]
                    e.wait_ge(s_scv, s)
                    nk = nk512(qt)
                    mmi = None
                    for kc in range(nk):
                        for half in range(2):
                            mmi = e.matmul(
                                psc[:, kc * 512:(kc + 1) * 512],
                                qv[:, half * S + qt * 128: half * S + qt * 128 + 128],
                                kv[:, half * S + kc * 512: half * S + kc * 512 + 512],
                                start=(half == 0), stop=(half == 1))
                    mmi.then_inc(s_mm, 1)
                    if s >= 1:
                        ptav(e, s - 1)
                ptav(e, NS - 1)

            @blk.vector
            def _(e):
                e.wait_ge(s_cld, 32)
                for s in range(NS):
                    h, qt = divmod(s, NQT)
                    nk = nk512(qt)
                    W = nk * 512
                    e.wait_ge(s_mm, s + 1)
                    if s >= 1:
                        e.wait_ge(s_ex, s)
                    for kc in range(nk):
                        dst = sc_sb[:, kc * 512:(kc + 1) * 512]
                        pv = psc[:, kc * 512:(kc + 1) * 512]
                        if kc == nk - 1:
                            e.scalar_tensor_tensor(dst, pv, 1.0 / 16.0,
                                                   cmk[:, (qt % 4) * 512:(qt % 4 + 1) * 512],
                                                   ALU.mult, ALU.add)
                        else:
                            e.tensor_single_scalar(dst, pv, 1.0 / 16.0, ALU.mult)
                    e.reduce_max(cols[:, 0:1], sc_sb[:, 0:W], axis=AX.X, negate=True)
                    if s >= 1:
                        e.reciprocal(cols[:, 2:3], cols[:, 1:2])
                        e.memset(cols[:, 3:4], 0.0).then_inc(s_rc, 1)
                    e.memset(cols[:, 3:4], 0.0).then_inc(s_scv, 1)
                    if s >= 1:
                        h1, qt1 = divmod(s - 1, NQT)
                        for kc in range(qt1 + 1):
                            npt = PT_BASE[s - 1] + kc
                            bank = npt % 2
                            e.wait_ge(s_pt, npt + 1)
                            e.tensor_copy(pT_sb[:, bank * 128:(bank + 1) * 128],
                                          ptp[:, bank * 1024: bank * 1024 + 128])
                            e.memset(cols[:, 3:4], 0.0).then_inc(s_pte, 1)
                e.wait_ge(s_ex, NS)
                e.reciprocal(cols[:, 2:3], cols[:, 1:2])
                e.memset(cols[:, 3:4], 0.0).then_inc(s_rc, 1)
                h1, qt1 = divmod(NS - 1, NQT)
                for kc in range(qt1 + 1):
                    npt = PT_BASE[NS - 1] + kc
                    bank = npt % 2
                    e.wait_ge(s_pt, npt + 1)
                    e.tensor_copy(pT_sb[:, bank * 128:(bank + 1) * 128],
                                  ptp[:, bank * 1024: bank * 1024 + 128])
                    e.memset(cols[:, 3:4], 0.0).then_inc(s_pte, 1)

            @blk.scalar
            def _(e):
                for s in range(NS):
                    h, qt = divmod(s, NQT)
                    W = nk512(qt) * 512
                    e.wait_ge(s_scv, s + 1)
                    if s >= 1:
                        e.wait_ge(s_pt, PT_BASE[s])
                    e.activation(out=p_sb[:, 0:W], in_=sc_sb[:, 0:W], func=AF.Exp,
                                 bias=cols[:, 0:1], accum_out=cols[:, 1:2])
                    e.memzero(cols[:, 3:4]).then_inc(s_ex, 1)
                    if s >= 1:
                        h1, qt1 = divmod(s - 1, NQT)
                        e.wait_ge(s_av, s)
                        e.wait_ge(s_rc, s)
                        e.activation(out=ao_sb[:, qt1 * 2048 + h1 * 256: qt1 * 2048 + h1 * 256 + 256],
                                     in_=pav[:, ((s - 1) % 2) * 512:((s - 1) % 2) * 512 + 256],
                                     func=AF.Identity, scale=cols[:, 2:3])
                        e.memzero(cols[:, 3:4]).then_inc(s_ae, 1)
                h1, qt1 = divmod(NS - 1, NQT)
                e.wait_ge(s_av, NS)
                e.wait_ge(s_rc, NS)
                e.activation(out=ao_sb[:, qt1 * 2048 + h1 * 256: qt1 * 2048 + h1 * 256 + 256],
                             in_=pav[:, ((NS - 1) % 2) * 512:((NS - 1) % 2) * 512 + 256],
                             func=AF.Identity, scale=cols[:, 2:3])
                e.memzero(cols[:, 3:4]).then_inc(s_ae, 1)

            @blk.gpsimd
            def _(e):
                e.wait_ge(s_ae, NS)
                e.dma_start(out=o_ao.rearrange("(t p) c -> p t c", p=128),
                            in_=ao_sb[:].rearrange("p (t c) -> p t c", c=2048)
                            ).then_inc(s_out, 16)
                e.wait_ge(s_out, 16)
    return nc


def run_l2_attn(qkv):
    nc = cached("l2_attn", build_l2_attn)
    q = qkv[:, :D].reshape(B, S, 16, 256)
    k = qkv[:, D:2 * D].reshape(B, S, 16, 256)
    vv = qkv[:, 2 * D:].reshape(B, S, 16, 256)
    cmask = np.zeros((4, 128, 512), np.float32)
    for r in range(4):
        qpos = r * 128 + np.arange(128)[:, None]
        cmask[r] = np.where(qpos >= np.arange(512)[None, :], 0.0, -1e9)
    iden = np.eye(128, dtype=BF)
    in_maps = []
    for c in range(NCORE):
        b, hg = divmod(c, 2)
        hs = slice(hg * 8, hg * 8 + 8)
        in_maps.append({
            "qT": np.ascontiguousarray(q[b, :, hs, :].transpose(1, 2, 0)),
            "kT": np.ascontiguousarray(k[b, :, hs, :].transpose(1, 2, 0)),
            "v": np.ascontiguousarray(vv[b, :, hs, :].transpose(1, 0, 2)),
            "cm": cmask, "iden": iden,
        })
    res = run_bass_kernel_spmd(nc, in_maps, core_ids=list(range(NCORE)))
    ao = np.zeros((B, S, 16, 256), BF)
    for c in range(NCORE):
        b, hg = divmod(c, 2)
        ao[b, :, hg * 8:hg * 8 + 8, :] = np.asarray(
            res.results[c]["o_ao"]).reshape(S, 8, 256)
    return ao.reshape(B * S, D)


def build_l3_mix():
    nc = _mknc()
    aoT = nc.dram_tensor("aoT", [D, TC], BF16, kind="ExternalInput").ap()
    xnT = nc.dram_tensor("xnT", [D, TC], F32, kind="ExternalInput").ap()
    woT = nc.dram_tensor("woT", [D, D], F32, kind="ExternalInput").ap()
    wdT = nc.dram_tensor("wdT", [D, D], BF16, kind="ExternalInput").ap()
    bias = nc.dram_tensor("bias", [1, D], F32, kind="ExternalInput").ap()
    x = nc.dram_tensor("x", [TC, D], F32, kind="ExternalInput").ap()
    o_x1 = nc.dram_tensor("o_x1", [TC, D], F32, kind="ExternalOutput").ap()

    NT = TC // 128
    NK = D // 128
    NO = D // 512

    es = ExitStack()
    sbuf = lambda n, s_, d=F32: es.enter_context(nc.sbuf_tensor(n, s_, d))
    sem = lambda n: es.enter_context(nc.semaphore(n))
    with es:
        aots = sbuf("aots", [128, 2 * TC], BF16)
        xnts = sbuf("xnts", [128, NK * TC], BF16)
        xcht = sbuf("xcht", [128, 2 * TC])
        wob = sbuf("wob", [128, 2 * 512])
        wobf = sbuf("wobf", [128, 2 * 512], BF16)
        wdb = sbuf("wdb", [128, 2 * 512], BF16)
        brow = sbuf("brow", [1, D])
        brow_bf = sbuf("brow_bf", [1, D], BF16)
        onesr = sbuf("onesr", [1, 128], BF16)
        s_sb = sbuf("s_sb", [128, NT * 512])
        g_sb = sbuf("g_sb", [128, NT * 512])
        xtl = sbuf("xtl", [128, 4 * 512])
        x1t = sbuf("x1t", [128, 4 * 512])
        junk = sbuf("junk", [128, 4])
        psm = es.enter_context(nc.psum_tensor("psm", [128, 8 * 512], F32))

        s_cld = sem("s_cld")
        s_wol = [sem("s_wol0"), sem("s_wol1")]
        s_aol = [sem("s_aol0"), sem("s_aol1")]
        s_wdl = [sem("s_wdl0"), sem("s_wdl1")]
        s_xtl_ = [sem("s_xtl0"), sem("s_xtl1")]
        s_xl = [sem(f"s_xl{j}") for j in range(4)]
        s_wc = sem("s_wc")
        s_xc = sem("s_xc")
        s_ms = sem("s_ms")
        s_mg = sem("s_mg")
        s_sev = sem("s_sev")
        s_sil = sem("s_sil")
        s_cmb = sem("s_cmb")
        s_st = [sem(f"s_st{j}") for j in range(4)]

        with nc.Block() as blk:
            @blk.sync
            def _(e):
                e.dma_start(out=brow[:], in_=bias).then_inc(s_cld, 16)

                for c in range(NK):
                    if c >= 2:
                        e.wait_ge(s_xc, c - 1)
                    e.dma_start(out=xcht[:, (c % 2) * TC:(c % 2 + 1) * TC],
                                in_=xnT[c * 128:(c + 1) * 128, :]).then_inc(s_xtl_[c % 2], 16)
                for oc in range(NO):
                    for k in range(NK):
                        g = oc * NK + k
                        if g >= 2:
                            e.wait_ge(s_wc, g - 1)
                        e.dma_start(out=wob[:, (g % 2) * 512:(g % 2 + 1) * 512],
                                    in_=woT[k * 128:(k + 1) * 128, oc * 512:(oc + 1) * 512]
                                    ).then_inc(s_wol[g % 2], 16)
                        if g >= 2:
                            e.wait_ge(s_ms, g - 1)
                        e.dma_start(out=aots[:, (k % 2) * TC:(k % 2 + 1) * TC],
                                    in_=aoT[k * 128:(k + 1) * 128, :]
                                    ).then_inc(s_aol[k % 2], 16)
                    for k in range(NK):
                        g = oc * NK + k
                        if g >= 2:
                            e.wait_ge(s_mg, 33 * ((g - 2) // 32) + min((g - 2) % 32 + 1, 32))
                        e.dma_start(out=wdb[:, (g % 2) * 512:(g % 2 + 1) * 512],
                                    in_=wdT[k * 128:(k + 1) * 128, oc * 512:(oc + 1) * 512]
                                    ).then_inc(s_wdl[g % 2], 16)

            @blk.gpsimd
            def _(e):
                def xload(i):
                    oc, m = divmod(i, NT)
                    e.dma_start(out=xtl[:, (i % 4) * 512:(i % 4 + 1) * 512],
                                in_=x[m * 128:(m + 1) * 128, oc * 512:(oc + 1) * 512]
                                ).then_inc(s_xl[i % 4], 16)
                for i in range(4):
                    xload(i)
                for i in range(NO * NT):
                    oc, m = divmod(i, NT)
                    e.wait_ge(s_cmb, i + 1)
                    e.dma_start(out=o_x1[m * 128:(m + 1) * 128, oc * 512:(oc + 1) * 512],
                                in_=x1t[:, (i % 4) * 512:(i % 4 + 1) * 512]
                                ).then_inc(s_st[i % 4], 16)
                    if i + 4 < NO * NT:
                        xload(i + 4)

            @blk.vector
            def _(e):
                for c in range(NK):
                    e.wait_ge(s_xtl_[c % 2], 16 * (c // 2 + 1))
                    e.tensor_copy(xnts[:, c * TC:(c + 1) * TC],
                                  xcht[:, (c % 2) * TC:(c % 2 + 1) * TC])
                    e.memset(junk[:, 0:1], 0.0).then_inc(s_xc, 1)
                e.wait_ge(s_cld, 16)
                e.tensor_copy(brow_bf[0:1, :], brow[0:1, :])
                e.memset(onesr[0:1, :].bitcast(mybir.dt.uint16), 0x3F80)
                e.memset(junk[:, 0:1], 0.0).then_inc(s_cld, 1)
                for oc in range(NO):
                    for k in range(NK):
                        g = oc * NK + k
                        e.wait_ge(s_wol[g % 2], 16 * (g // 2 + 1))
                        if g >= 2:
                            e.wait_ge(s_ms, g - 1)
                        e.tensor_copy(wobf[:, (g % 2) * 512:(g % 2 + 1) * 512],
                                      wob[:, (g % 2) * 512:(g % 2 + 1) * 512])
                        e.memset(junk[:, 0:1], 0.0).then_inc(s_wc, 1)
                    for m in range(NT):
                        if m == 0:
                            e.wait_ge(s_ms, 32 * (oc + 1))
                        e.tensor_copy(s_sb[:, m * 512:(m + 1) * 512],
                                      psm[:, m * 512:(m + 1) * 512])
                        e.memset(junk[:, 0:1], 0.0).then_inc(s_sev, 1)
                    for m in range(NT):
                        i = oc * NT + m
                        e.wait_ge(s_sil, i + 1)
                        e.wait_ge(s_xl[i % 4], 16 * (i // 4 + 1))
                        if i >= 4:
                            e.wait_ge(s_st[i % 4], 16 * ((i - 4) // 4 + 1))
                        e.tensor_mul(x1t[:, (i % 4) * 512:(i % 4 + 1) * 512],
                                     s_sb[:, m * 512:(m + 1) * 512],
                                     g_sb[:, m * 512:(m + 1) * 512])
                        e.tensor_add(x1t[:, (i % 4) * 512:(i % 4 + 1) * 512],
                                     x1t[:, (i % 4) * 512:(i % 4 + 1) * 512],
                                     xtl[:, (i % 4) * 512:(i % 4 + 1) * 512])
                        e.memset(junk[:, 0:1], 0.0).then_inc(s_cmb, 1)

            @blk.tensor
            def _(e):
                e.wait_ge(s_xc, NK)
                e.wait_ge(s_cld, 1)
                for oc in range(NO):
                    for k in range(NK):
                        g = oc * NK + k
                        e.wait_ge(s_wc, g + 1)
                        e.wait_ge(s_aol[k % 2], 16 * (g // 2 + 1))
                        if k == 0 and oc > 0:
                            e.wait_ge(s_sil, oc * NT)
                        for m in range(NT):
                            mmi = e.matmul(psm[:, m * 512:(m + 1) * 512],
                                           aots[:, (k % 2) * TC + m * 128:(k % 2) * TC + m * 128 + 128],
                                           wobf[:, (g % 2) * 512:(g % 2 + 1) * 512],
                                           start=(k == 0), stop=(k == NK - 1))
                        mmi.then_inc(s_ms, 1)
                    for k in range(NK):
                        g = oc * NK + k
                        e.wait_ge(s_wdl[g % 2], 16 * (g // 2 + 1))
                        if k == 0:
                            e.wait_ge(s_sev, (oc + 1) * NT)
                        for m in range(NT):
                            mmi = e.matmul(psm[:, m * 512:(m + 1) * 512],
                                           xnts[:, k * TC + m * 128: k * TC + m * 128 + 128],
                                           wdb[:, (g % 2) * 512:(g % 2 + 1) * 512],
                                           start=(k == 0), stop=False)
                        mmi.then_inc(s_mg, 1)
                    for m in range(NT):
                        mmi = e.matmul(psm[:, m * 512:(m + 1) * 512],
                                       onesr[0:1, 0:128],
                                       brow_bf[0:1, oc * 512:(oc + 1) * 512],
                                       start=False, stop=True)
                    mmi.then_inc(s_mg, 1)

            @blk.scalar
            def _(e):
                for oc in range(NO):
                    for m in range(NT):
                        i = oc * NT + m
                        if m == 0:
                            e.wait_ge(s_mg, 33 * (oc + 1))
                        e.activation(out=g_sb[:, m * 512:(m + 1) * 512],
                                     in_=psm[:, m * 512:(m + 1) * 512], func=AF.Silu)
                        e.memzero(junk[:, 1:2]).then_inc(s_sil, 1)
    return nc


def run_l3_mix(x_flat, ao, xn, wdeq, wo, bias0):
    nc = cached("l3_mix", build_l3_mix)
    aoT_full = np.ascontiguousarray(ao.T)
    xnT_full = np.ascontiguousarray(xn.T)
    woT = np.ascontiguousarray(wo.T.astype(np.float32))
    wdT = np.ascontiguousarray(wdeq.T)
    brow = np.ascontiguousarray(bias0.reshape(1, D).astype(np.float32))
    in_maps = []
    for c in range(NCORE):
        sl = slice(c * TC, (c + 1) * TC)
        in_maps.append({
            "aoT": np.ascontiguousarray(aoT_full[:, sl]),
            "xnT": np.ascontiguousarray(xnT_full[:, sl]),
            "woT": woT, "wdT": wdT, "bias": brow,
            "x": np.ascontiguousarray(x_flat[sl]),
        })
    res = run_bass_kernel_spmd(nc, in_maps, core_ids=list(range(NCORE)))
    return np.concatenate([np.asarray(r["o_x1"]) for r in res.results], 0)


def build_l4_proj():
    nc = _mknc()
    NTH = 9
    TH = NTH * 128
    NK = D // 128
    NOG = 145
    x1h = nc.dram_tensor("x1h", [TH, D], F32, kind="ExternalInput").ap()
    ipwT = nc.dram_tensor("ipwT", [D, 18560], F32, kind="ExternalInput").ap()
    nwbc = nc.dram_tensor("nwbc", [128, D], F32, kind="ExternalInput").ap()
    cw = nc.dram_tensor("cw", [128, 80 * 4], F32, kind="ExternalInput").ap()
    cb = nc.dram_tensor("cb", [128, 80], F32, kind="ExternalInput").ap()
    dtb = nc.dram_tensor("dtb", [128, 1], F32, kind="ExternalInput").ap()
    iden = nc.dram_tensor("iden", [128, 128], BF16, kind="ExternalInput").ap()
    o_xn1 = nc.dram_tensor("o_xn1", [TC, D], BF16, kind="ExternalOutput").ap()
    o_gate = nc.dram_tensor("o_gate", [8192, TC], BF16, kind="ExternalOutput").ap()
    o_hbc = nc.dram_tensor("o_hbc", [10240, TC], BF16, kind="ExternalOutput").ap()
    o_dt = nc.dram_tensor("o_dt", [128, TC], F32, kind="ExternalOutput").ap()

    es = ExitStack()
    sbuf = lambda n, s_, d=F32: es.enter_context(nc.sbuf_tensor(n, s_, d))
    sem = lambda n: es.enter_context(nc.semaphore(n))
    with es:
        xt = sbuf("xt", [128, 2 * D])
        xnb = sbuf("xnb", [128, 2 * D], BF16)
        hbf = sbuf("hbf", [128, 2 * D], BF16)
        hT = sbuf("hT", [128, NK * TH], BF16)
        nwbb = sbuf("nwbb", [128, D], BF16)
        wip = sbuf("wip", [128, 2 * 2048])
        wipb = sbuf("wipb", [128, 2 * D], BF16)
        stat = sbuf("stat", [128, 8])
        pre = sbuf("pre", [128, 2 * TH])
        acc = sbuf("acc", [128, 2 * TC])
        hbcb = sbuf("hbcb", [128, 2 * TC], BF16)
        gateb = sbuf("gateb", [128, 2 * TC], BF16)
        cwsb = sbuf("cwsb", [128, 80 * 4])
        cbsb = sbuf("cbsb", [128, 80])
        dtbs = sbuf("dtbs", [128, 1])
        sid = sbuf("sid", [128, 128], BF16)
        junk = sbuf("junk", [128, 4])
        psm = es.enter_context(nc.psum_tensor("psm", [128, 6 * 512], F32))
        ptr = es.enter_context(nc.psum_tensor("ptr", [128, 2 * 1024], BF16))

        s_cld = sem("s_cld")
        s_xl = [sem("s_xl0"), sem("s_xl1")]
        s_wl = [sem("s_wl0"), sem("s_wl1")]
        s_nw = sem("s_nw")
        s_ln = sem("s_ln")
        s_hb = sem("s_hb")
        s_tr = sem("s_tr")
        s_te = sem("s_te")
        s_wc = sem("s_wc")
        s_mm = sem("s_mm")
        s_ge = sem("s_ge")
        s_pe = sem("s_pe")
        s_cv = sem("s_cv")
        s_dt = sem("s_dt")
        s_xno = sem("s_xno")
        s_go = [sem("s_go0"), sem("s_go1")]
        s_ho = [sem("s_ho0"), sem("s_ho1")]
        s_do = sem("s_do")
        s_ss = sem("s_ss")

        def bankfree(og):
            if og < 64:
                return (s_ge, og + 1)
            if og < 144:
                return (s_pe, og - 64 + 1)
            return (s_dt, 1)

        with nc.Block() as blk:
            @blk.sync
            def _(e):
                e.dma_start(out=sid[:], in_=iden).then_inc(s_cld, 16)
                e.dma_start(out=cwsb[:], in_=cw).then_inc(s_cld, 16)
                e.dma_start(out=cbsb[:], in_=cb).then_inc(s_cld, 16)
                e.dma_start(out=dtbs[:], in_=dtb).then_inc(s_cld, 16)
                e.dma_start(out=xt[:, 0:D], in_=nwbc).then_inc(s_nw, 16)
                e.wait_ge(s_nw, 17)
                for m in range(NTH):
                    if m >= 2:
                        e.wait_ge(s_ln, m - 1)
                    e.dma_start(out=xt[:, (m % 2) * D:(m % 2 + 1) * D],
                                in_=x1h[m * 128:(m + 1) * 128, :]).then_inc(s_xl[m % 2], 16)
                for og in range(NOG):
                    for hf in range(2):
                        g = og * 2 + hf
                        if g >= 2:
                            e.wait_ge(s_wc, g - 1)
                        e.dma_start(out=wip[:, (g % 2) * 2048:(g % 2 + 1) * 2048]
                                    .rearrange("p (c o) -> p c o", o=128),
                                    in_=ipwT[hf * 2048:(hf + 1) * 2048, og * 128:(og + 1) * 128]
                                    .rearrange("(c p) o -> p c o", p=128)
                                    ).then_inc(s_wl[g % 2], 16)

            @blk.scalar
            def _(e):
                ssync = SelfSync(s_ss)
                e.wait_ge(s_cld, 64)
                for m in range(NTH):
                    e.wait_ge(s_xl[m % 2], 16 * (m // 2 + 1))
                    if m >= 2:
                        e.wait_ge(s_tr, (m - 1) * NK)
                        e.wait_ge(s_hb, m - 1)
                        if m - 2 >= 1:
                            e.wait_ge(s_xno, 16 * (m - 2))
                    xv = xt[:, (m % 2) * D:(m % 2 + 1) * D]
                    xb = xnb[:, (m % 2) * D:(m % 2 + 1) * D]
                    hv = hbf[:, (m % 2) * D:(m % 2 + 1) * D]
                    e.activation(out=xb, in_=xv, func=AF.Square, accum_out=stat[:, 1:2])
                    e.activation(out=xb, in_=xv, func=AF.Copy, accum_out=stat[:, 0:1])
                    e.mul(stat[:, 2:3], stat[:, 0:1], 1.0 / D)
                    e.mul(stat[:, 3:4], stat[:, 1:2], 1.0 / D)
                    e.activation(out=stat[:, 4:5], in_=stat[:, 2:3], func=AF.Square)
                    ins = e.mul(stat[:, 4:5], stat[:, 4:5], -1.0)
                    ssync.barrier(e, ins)
                    e.activation(out=stat[:, 3:4], in_=stat[:, 3:4], func=AF.Identity, bias=stat[:, 4:5])
                    e.activation(out=stat[:, 3:4], in_=stat[:, 3:4], func=AF.Copy, bias=EPS)
                    e.activation(out=stat[:, 6:7], in_=stat[:, 3:4], func=AF.Ln)
                    ins = e.activation(out=stat[:, 6:7], in_=stat[:, 6:7], func=AF.Exp, scale=-0.5)
                    ssync.barrier(e, ins)
                    ins = e.mul(stat[:, 7:8], stat[:, 2:3], stat[:, 6:7])
                    ssync.barrier(e, ins)
                    ins = e.mul(stat[:, 7:8], stat[:, 7:8], -1.0)
                    ssync.barrier(e, ins)
                    e.activation(out=xb, in_=xv, func=AF.Identity,
                                 scale=stat[:, 6:7], bias=stat[:, 7:8])
                    e.activation(out=xv, in_=xb, func=AF.Square, accum_out=stat[:, 0:1])
                    e.activation(out=stat[:, 1:2], in_=stat[:, 0:1], func=AF.Copy,
                                 scale=1.0 / D, bias=EPS)
                    e.activation(out=stat[:, 1:2], in_=stat[:, 1:2], func=AF.Ln)
                    ins = e.activation(out=stat[:, 1:2], in_=stat[:, 1:2], func=AF.Exp, scale=-0.5)
                    ssync.barrier(e, ins)
                    e.activation(out=hv, in_=xb, func=AF.Identity, scale=stat[:, 1:2])
                    e.memzero(junk[:, 0:1]).then_inc(s_ln, 1)
                for og in range(64):
                    e.wait_ge(s_mm, og + 1)
                    if og >= 2:
                        e.wait_ge(s_go[og % 2], 16 * (og // 2))
                    base = (og % 2) * 1536
                    e.activation(out=gateb[:, (og % 2) * TC:(og % 2) * TC + 256],
                                 in_=psm[:, base + 128: base + 384], func=AF.Copy)
                    pv2 = bass.AP(psm[:].tensor, psm[:].offset + base + 512,
                                  [list(psm[:].ap[0]), [512, 2], [1, 384]])
                    e.activation(out=gateb[:, (og % 2) * TC + 256:(og % 2 + 1) * TC]
                                 .rearrange("p (a b) -> p a b", b=384), in_=pv2, func=AF.Copy)
                    e.memzero(junk[:, 0:1]).then_inc(s_ge, 1)
                for j in range(80):
                    e.wait_ge(s_pe, j + 1)
                    if j >= 2:
                        e.wait_ge(s_ho[j % 2], 16 * (j // 2))
                    e.activation(out=hbcb[:, (j % 2) * TC:(j % 2 + 1) * TC],
                                 in_=acc[:, (j % 2) * TC:(j % 2 + 1) * TC],
                                 func=AF.Silu, bias=cbsb[:, j:j + 1])
                    e.memzero(junk[:, 1:2]).then_inc(s_cv, 1)
                e.wait_ge(s_mm, NOG)
                e.wait_ge(s_pe, 80)
                TH_ = 1152
                pv3 = bass.AP(psm[:].tensor, psm[:].offset + (144 % 2) * 1536,
                              [list(psm[:].ap[0]), [512, 3], [1, 384]])
                e.activation(out=pre[:, 0:TH_].rearrange("p (a b) -> p a b", b=384),
                             in_=pv3, func=AF.Copy)
                dv = pre[:, TH_:TH_ + TC]
                lp = pre[:, 0:TC]
                e.activation(out=dv, in_=pre[:, 128:TH_], func=AF.Identity, bias=dtbs[:, 0:1])
                e.activation(out=lp, in_=dv, func=AF.Abs)
                e.activation(out=lp, in_=lp, func=AF.Exp, scale=-1.0)
                e.activation(out=lp, in_=lp, func=AF.Ln, bias=1.0)
                e.activation(out=dv, in_=dv, func=AF.Relu)
                e.memzero(junk[:, 0:1]).then_inc(s_dt, 1)

            @blk.vector
            def _(e):
                e.wait_ge(s_nw, 16)
                e.tensor_copy(nwbb[:], xt[:, 0:D])
                e.memset(junk[:, 2:3], 0.0).then_inc(s_nw, 1)
                def evacs(mm_):
                    for k in range(NK):
                        i = mm_ * NK + k
                        e.wait_ge(s_tr, i + 1)
                        e.tensor_copy(hT[:, k * TH + mm_ * 128: k * TH + mm_ * 128 + 128],
                                      ptr[:, (i % 2) * 1024:(i % 2) * 1024 + 128])
                        e.memset(junk[:, 2:3], 0.0).then_inc(s_te, 1)
                for m in range(NTH):
                    e.wait_ge(s_ln, m + 1)
                    hv = hbf[:, (m % 2) * D:(m % 2 + 1) * D]
                    e.tensor_tensor(hv, hv, nwbb[:], ALU.mult)
                    e.memset(junk[:, 2:3], 0.0).then_inc(s_hb, 1)
                    if m >= 1:
                        evacs(m - 1)
                evacs(NTH - 1)
                e.memset(junk[:, 2:3], 0.0).then_inc(s_te, 1)
                def conv_j(j):
                    og_ = 64 + j
                    e.wait_ge(s_mm, og_ + 1)
                    if j >= 2:
                        e.wait_ge(s_cv, j - 1)
                    pv3 = bass.AP(psm[:].tensor, psm[:].offset + (og_ % 2) * 1536,
                                  [list(psm[:].ap[0]), [512, 3], [1, 384]])
                    prv = pre[:, (j % 2) * TH:(j % 2) * TH + TH]
                    e.tensor_copy(prv.rearrange("p (a b) -> p a b", b=384), pv3)
                    a0 = acc[:, (j % 2) * TC:(j % 2 + 1) * TC]
                    e.tensor_single_scalar(a0, prv[:, 125:125 + TC], cwsb[:, 4 * j:4 * j + 1], ALU.mult)
                    e.scalar_tensor_tensor(a0, prv[:, 126:126 + TC], cwsb[:, 4 * j + 1:4 * j + 2], a0, ALU.mult, ALU.add)
                    e.scalar_tensor_tensor(a0, prv[:, 127:127 + TC], cwsb[:, 4 * j + 2:4 * j + 3], a0, ALU.mult, ALU.add)
                    e.scalar_tensor_tensor(a0, prv[:, 128:128 + TC], cwsb[:, 4 * j + 3:4 * j + 4], a0, ALU.mult, ALU.add)
                    e.memset(junk[:, 2:3], 0.0).then_inc(s_pe, 1)
                for og in range(NOG):
                    for hf in range(2):
                        g = og * 2 + hf
                        e.wait_ge(s_wl[g % 2], 16 * (g // 2 + 1))
                        if hf == 0 and og >= 2:
                            e.wait_ge(s_mm, og - 1)
                        e.tensor_copy(wipb[:, (og % 2) * D + hf * 2048:(og % 2) * D + (hf + 1) * 2048],
                                      wip[:, (g % 2) * 2048:(g % 2 + 1) * 2048])
                        e.memset(junk[:, 2:3], 0.0).then_inc(s_wc, 1)
                    if og >= 66 and og - 66 < 80:
                        conv_j(og - 66)
                for j in range(NOG - 66, 80):
                    conv_j(j)
                e.wait_ge(s_dt, 1)
                e.tensor_add(pre[:, 1152:1152 + TC], pre[:, 1152:1152 + TC], pre[:, 0:TC])
                e.memset(junk[:, 2:3], 0.0).then_inc(s_dt, 1)

            @blk.tensor
            def _(e):
                for i in range(NTH * NK):
                    m, k = divmod(i, NK)
                    if k == 0:
                        e.wait_ge(s_hb, m + 1)
                    if i >= 2:
                        e.wait_ge(s_te, i - 1)
                    e.transpose(ptr[:, (i % 2) * 1024:(i % 2) * 1024 + 128],
                                hbf[:, (m % 2) * D + k * 128:(m % 2) * D + k * 128 + 128],
                                sid[:]).then_inc(s_tr, 1)
                e.wait_ge(s_te, NTH * NK + 1)
                for og in range(NOG):
                    e.wait_ge(s_wc, 2 * (og + 1))
                    if og >= 2:
                        sm, cnt = bankfree(og - 2)
                        e.wait_ge(sm, cnt)
                    base = (og % 2) * 1536
                    mmi = None
                    for k in range(NK):
                        for n in range(3):
                            mmi = e.matmul(psm[:, base + n * 512: base + n * 512 + 384],
                                           wipb[:, (og % 2) * D + k * 128:(og % 2) * D + k * 128 + 128],
                                           hT[:, k * TH + n * 384: k * TH + n * 384 + 384],
                                           start=(k == 0), stop=(k == NK - 1))
                    mmi.then_inc(s_mm, 1)

            @blk.gpsimd
            def _(e):
                for m in range(1, NTH):
                    e.wait_ge(s_ln, m + 1)
                    e.dma_start(out=o_xn1[(m - 1) * 128: m * 128, :],
                                in_=xnb[:, (m % 2) * D:(m % 2 + 1) * D]).then_inc(s_xno, 16)
                for og in range(64):
                    e.wait_ge(s_ge, og + 1)
                    e.dma_start(out=o_gate[og * 128:(og + 1) * 128, :],
                                in_=gateb[:, (og % 2) * TC:(og % 2 + 1) * TC]
                                ).then_inc(s_go[og % 2], 16)
                for j in range(80):
                    e.wait_ge(s_cv, j + 1)
                    e.dma_start(out=o_hbc[j * 128:(j + 1) * 128, :],
                                in_=hbcb[:, (j % 2) * TC:(j % 2 + 1) * TC]
                                ).then_inc(s_ho[j % 2], 16)
                e.wait_ge(s_dt, 2)
                e.dma_start(out=o_dt, in_=pre[:, 1152:1152 + TC]).then_inc(s_do, 16)
                e.wait_ge(s_do, 16)
                e.wait_ge(s_go[0], 16 * 32)
                e.wait_ge(s_go[1], 16 * 32)
                e.wait_ge(s_ho[0], 16 * 40)
                e.wait_ge(s_ho[1], 16 * 40)
                e.wait_ge(s_xno, 16 * 8)
    return nc


def run_l4_proj(x1, in_proj_w, norm_w, conv_w, conv_b, dt_bias):
    nc = cached("l4_proj", build_l4_proj)
    ipwT = np.ascontiguousarray(in_proj_w.T.astype(np.float32))
    nwbc = np.ascontiguousarray(np.broadcast_to(norm_w.astype(np.float32), (128, D)))
    cwp = np.ascontiguousarray(
        conv_w.astype(np.float32).reshape(80, 128, 4).transpose(1, 0, 2).reshape(128, 320))
    cbp = np.ascontiguousarray(
        conv_b.astype(np.float32).reshape(80, 128).T)
    iden = np.eye(128, dtype=BF)
    in_maps = []
    for c in range(NCORE):
        half = c % 2
        if half == 0:
            halo = np.zeros((128, D), np.float32)
        else:
            halo = x1[c * TC - 128: c * TC]
        in_maps.append({
            "x1h": np.ascontiguousarray(np.concatenate([halo, x1[c * TC:(c + 1) * TC]], 0)),
            "ipwT": ipwT, "nwbc": nwbc, "cw": cwp, "cb": cbp,
            "dtb": np.ascontiguousarray(dt_bias.astype(np.float32).reshape(128, 1)),
            "iden": iden,
        })
    res = run_bass_kernel_spmd(nc, in_maps, core_ids=list(range(NCORE)))
    xn1 = np.concatenate([np.asarray(r["o_xn1"]) for r in res.results], 0)
    gate = np.concatenate([np.asarray(r["o_gate"]) for r in res.results], 1)
    hbc = np.concatenate([np.asarray(r["o_hbc"]) for r in res.results], 1)
    dt = np.concatenate([np.asarray(r["o_dt"]) for r in res.results], 1)
    return xn1, gate, hbc, dt


def build_l5_ssd():
    nc = _mknc()
    NCH = 16
    NH = 64
    NG = 8
    hid = nc.dram_tensor("hid", [S, 4096], BF16, kind="ExternalInput").ap()
    Bfm = nc.dram_tensor("Bfm", [4, 128, S], BF16, kind="ExternalInput").ap()
    Cfm = nc.dram_tensor("Cfm", [4, 128, S], BF16, kind="ExternalInput").ap()
    Btm = nc.dram_tensor("Btm", [S, 512], BF16, kind="ExternalInput").ap()
    dt = nc.dram_tensor("dt", [64, S], F32, kind="ExternalInput").ap()
    alog = nc.dram_tensor("alog", [64, 1], F32, kind="ExternalInput").ap()
    dpx = nc.dram_tensor("dpx", [128, 4096], F32, kind="ExternalInput").ap()
    idf = nc.dram_tensor("idf", [128, 128], F32, kind="ExternalInput").ap()
    tri4 = nc.dram_tensor("tri4", [128, 512], F32, kind="ExternalInput").ap()
    hsel = nc.dram_tensor("hsel", [4, 512], F32, kind="ExternalInput").ap()
    onef = nc.dram_tensor("onef", [1, 128], F32, kind="ExternalInput").ap()
    o_y = nc.dram_tensor("o_y", [S, 4096], BF16, kind="ExternalOutput").ap()
    acs_dram = nc.dram_tensor("acs_dram", [2, 64, 128], F32, kind="Internal").ap()
    alast_dram = nc.dram_tensor("alast_dram", [64], F32, kind="Internal").ap()

    es = ExitStack()
    sbuf = lambda n, s_, d=F32: es.enter_context(nc.sbuf_tensor(n, s_, d))
    sem = lambda n: es.enter_context(nc.semaphore(n))
    with es:
        hids = sbuf("hids", [128, 2 * 4096], BF16)
        bfs = sbuf("bfs", [128, 2 * 512], BF16)
        cfs = sbuf("cfs", [128, 2 * 512], BF16)
        bts = sbuf("bts", [128, 2 * 512], BF16)
        dts_ = sbuf("dts_", [64, S])
        dpe = sbuf("dpe", [128, 4096])
        dpeb = sbuf("dpeb", [128, 4096], BF16)
        sidf = sbuf("sidf", [128, 128])
        tri = sbuf("tri", [128, 512])
        hse = sbuf("hse", [4, 512])
        onc = sbuf("onc", [1, 128])
        nega = sbuf("nega", [64, 1])
        afc = sbuf("afc", [64, 768])
        acsr = sbuf("acsr", [1, 2 * 512])
        nacs4 = sbuf("nacs4", [4, 16 * 128])
        eatr = sbuf("eatr", [1, 64])
        fac = sbuf("fac", [128, 2 * 64], BF16)
        expa = sbuf("expa", [128, 64])
        eatb = sbuf("eatb", [128, 64])
        hdt = sbuf("hdt", [128, 4096], BF16)
        hdt2 = sbuf("hdt2", [128, 4096], BF16)
        dres = sbuf("dres", [128, 4096])
        cbt = sbuf("cbt", [128, 4 * 128])
        expd = sbuf("expd", [128, 2 * 512])
        smT = sbuf("smT", [128, 64 * 128], BF16)
        stf = sbuf("stf", [128, 4096])
        stb = sbuf("stb", [128, 4096], BF16)
        ysb = sbuf("ysb", [128, 4096])
        ybf = sbuf("ybf", [128, 2 * 4096], BF16)
        tmpg = sbuf("tmpg", [128, 512])
        junk = sbuf("junk", [128, 4])
        psm = es.enter_context(nc.psum_tensor("psm", [128, 8 * 512], F32))

        s_cld = sem("s_cld")
        s_ld = [sem("s_ld0"), sem("s_ld1")]
        s_dtp = sem("s_dtp")
        s_bo = sem("s_bo")
        s_br = sem("s_br")
        s_ar = [sem("s_ar0"), sem("s_ar1")]
        s_tp = sem("s_tp")
        s_tpe = sem("s_tpe")
        s_cb = sem("s_cb")
        s_cbe = sem("s_cbe")
        s_dm = sem("s_dm")
        s_ex = sem("s_ex")
        s_sm = sem("s_sm")
        s_hd = sem("s_hd")
        s_pey = sem("s_pey")
        s_yd = sem("s_yd")
        s_yo = sem("s_yo")
        s_up = sem("s_up")
        s_dr = sem("s_dr")
        s_st = [sem("s_st0"), sem("s_st1")]
        s_ss = sem("s_ss")
        s_vs = sem("s_vs")

        BK = lambda b, w=512: psm[:, b * 512: b * 512 + w]

        with nc.Block() as blk:
            @blk.sync
            def _(e):
                e.dma_start(out=sidf[:], in_=idf).then_inc(s_cld, 16)
                e.dma_start(out=tri[:], in_=tri4).then_inc(s_cld, 16)
                e.dma_start(out=hse[:], in_=hsel).then_inc(s_cld, 16)
                e.dma_start(out=onc[:], in_=onef).then_inc(s_cld, 16)
                e.dma_start(out=dts_[:], in_=dt).then_inc(s_cld, 16)
                e.dma_start(out=dpe[:], in_=dpx).then_inc(s_cld, 16)
                e.dma_start(out=nega[:], in_=alog).then_inc(s_cld, 16)
                for c in range(NCH):
                    sl = c % 2
                    if c >= 2:
                        e.wait_ge(s_dr, c - 1)
                    e.dma_start(out=hids[:, sl * 4096:(sl + 1) * 4096],
                                in_=hid[c * 128:(c + 1) * 128, :]).then_inc(s_ld[sl], 16)
                    e.dma_start(out=bfs[:, sl * 512:(sl + 1) * 512]
                                .rearrange("p (g l) -> p g l", l=128),
                                in_=Bfm[:, :, c * 128:(c + 1) * 128]
                                .rearrange("g p l -> p g l")).then_inc(s_ld[sl], 16)
                    e.dma_start(out=cfs[:, sl * 512:(sl + 1) * 512]
                                .rearrange("p (g l) -> p g l", l=128),
                                in_=Cfm[:, :, c * 128:(c + 1) * 128]
                                .rearrange("g p l -> p g l")).then_inc(s_ld[sl], 16)
                    e.dma_start(out=bts[:, sl * 512:(sl + 1) * 512],
                                in_=Btm[c * 128:(c + 1) * 128, :]).then_inc(s_ld[sl], 16)

            @blk.scalar
            def _(e):
                ssync = SelfSync(s_ss)
                e.wait_ge(s_cld, 112)
                ins = e.activation(out=nega[:, 0:1], in_=nega[:, 0:1], func=AF.Exp)
                ins = e.mul(nega[:, 0:1], nega[:, 0:1], -1.0)
                ssync.barrier(e, ins)
                e.memzero(junk[0:64, 0:1]).then_inc(s_dtp, 1)
                for c in range(NCH):
                    af = afc[0:64, 0:128]
                    acs = afc[0:64, 128:256]
                    nac = afc[0:64, 256:384]
                    dsv = afc[0:64, 384:512]
                    e.wait_ge(s_dtp, 4 * c + 1)
                    if c >= 1:
                        e.wait_ge(s_tp, 4 * c)
                        e.wait_ge(s_bo, 48 * c)
                    e.mul(af, dts_[0:64, c * 128:(c + 1) * 128], nega[0:64, 0:1])
                    e.memzero(junk[0:64, 0:1]).then_inc(s_dtp, 1)
                    e.wait_ge(s_dtp, 4 * c + 3)
                    e.activation(out=dsv, in_=acs, func=AF.Identity,
                                 bias=afc[0:64, 512:513])
                    e.activation(out=dsv, in_=dsv, func=AF.Exp, scale=-1.0)
                    e.mul(nac, acs, -1.0)
                    e.memzero(junk[0:64, 0:1]).then_inc(s_dtp, 1)
                    e.wait_ge(s_br, 32 * c + 32)
                    e.activation(out=eatr[0:1, :], in_=eatr[0:1, :], func=AF.Exp, scale=-1.0)
                    e.memzero(junk[0:1, 0:1]).then_inc(s_dtp, 1)
                    e.wait_ge(s_tp, 4 * (c + 1))
                    e.wait_ge(s_tpe, 3 * c + 2)
                    e.activation(out=expa[:], in_=psm[:, 3 * 512 + 128:3 * 512 + 192],
                                 func=AF.Exp)
                    e.memzero(junk[0:1, 0:1]).then_inc(s_tpe, 1)
                    for gg in range(16):
                        e.wait_ge(s_dm, 16 * c + gg + 1)
                        if 16 * c + gg >= 2:
                            e.wait_ge(s_sm, 16 * c + gg - 1)
                        ev = expd[:, (gg % 2) * 512:(gg % 2 + 1) * 512]
                        e.activation(out=ev, in_=BK(1 + gg % 2), func=AF.Relu, scale=-1.0)
                        e.activation(out=ev, in_=ev, func=AF.Exp, scale=-1.0)
                        e.memzero(junk[0:1, 0:1]).then_inc(s_ex, 1)
                    for j in range(NG):
                        if j == 0 and c >= 1:
                            e.wait_ge(s_dr, c)
                        e.wait_ge(s_pey, NG * c + j + 1)
                        e.activation(out=ysb[:, j * 512:(j + 1) * 512],
                                     in_=BK(4 + j % 2), func=AF.Copy)
                        e.memzero(junk[0:1, 0:1]).then_inc(s_yd, 1)

            _scan_n = [0]

            @blk.vector
            def _(e):
                e.wait_ge(s_cld, 112)
                e.tensor_copy(dpeb[:], dpe[:])
                e.memset(afc[0:64, 640:768], 1.0)
                e.memset(stf[:], 0.0)
                e.tensor_copy(stb[:], stf[:])
                e.memset(junk[:, 2:3], 0.0).then_inc(s_up, NG)
                for c in range(NCH):
                    sl = c % 2
                    af = afc[0:64, 0:128]
                    acs = afc[0:64, 128:256]
                    e.wait_ge(s_dtp, 4 * c + 2)
                    if c >= 1:
                        e.wait_ge(s_bo, 48 * c)
                        e.wait_ge(s_tp, 4 * c)
                    e.tensor_tensor_scan(acs, afc[0:64, 640:768], af, 0.0,
                                         ALU.mult, ALU.add)
                    e.memset(junk[:, 2:3], 0.0).then_inc(s_vs, 1)
                    _scan_n[0] += 1
                    e.wait_ge(s_vs, _scan_n[0])
                    e.tensor_scalar(afc[0:64, 512:513], acs[0:64, 127:128], -1.0,
                                    None, ALU.mult)
                    e.memset(junk[:, 2:3], 0.0).then_inc(s_dtp, 1)
                    e.wait_ge(s_tp, 4 * (c + 1))
                    e.tensor_copy(fac[:, 0:64], psm[:, 3 * 512:3 * 512 + 64])
                    e.tensor_copy(fac[:, 64:128], psm[:, 3 * 512 + 64:3 * 512 + 128])
                    e.memset(junk[:, 2:3], 0.0).then_inc(s_tpe, 1)
                    e.tensor_copy(eatb[:], psm[:, 3 * 512 + 192:3 * 512 + 256])
                    e.memset(junk[:, 2:3], 0.0).then_inc(s_tpe, 1)
                    e.wait_ge(s_ld[sl], 64 * (c // 2 + 1))
                    hv = hids[:, sl * 4096:(sl + 1) * 4096]
                    dtv = bass.AP(fac[:].tensor, fac[:].offset,
                                  [list(fac[:].ap[0]), [1, 64], [0, 64]])
                    dsvv = bass.AP(fac[:].tensor, fac[:].offset + 64,
                                   [list(fac[:].ap[0]), [1, 64], [0, 64]])
                    e.tensor_tensor(hdt[:].rearrange("p (h q) -> p h q", q=64),
                                    hv.rearrange("p (h q) -> p h q", q=64), dtv, ALU.mult)
                    e.tensor_tensor(hdt2[:].rearrange("p (h q) -> p h q", q=64),
                                    hdt[:].rearrange("p (h q) -> p h q", q=64), dsvv, ALU.mult)
                    e.memset(junk[:, 2:3], 0.0).then_inc(s_hd, 1)
                    for g in range(4):
                        e.wait_ge(s_cb, 4 * c + g + 1)
                        e.tensor_copy(cbt[:, g * 128:(g + 1) * 128],
                                      psm[:, g * 128:(g + 1) * 128])
                        e.memset(junk[:, 2:3], 0.0).then_inc(s_cbe, 1)
                    for gg in range(16):
                        e.wait_ge(s_ex, 16 * c + gg + 1)
                        ev = expd[:, (gg % 2) * 512:(gg % 2 + 1) * 512]
                        e.tensor_tensor(ev, ev, tri[:], ALU.mult)
                        g = gg // 4
                        cbv = bass.AP(cbt[:].tensor, cbt[:].offset + g * 128,
                                      [list(cbt[:].ap[0]), [0, 4], [1, 128]])
                        e.tensor_tensor(smT[:, gg * 512:(gg + 1) * 512]
                                        .rearrange("p (a b) -> p a b", b=128),
                                        ev.rearrange("p (a b) -> p a b", b=128),
                                        cbv, ALU.mult)
                        e.memset(junk[:, 2:3], 0.0).then_inc(s_sm, 1)
                    for j in range(NG):
                        gsl = j % 2
                        e.wait_ge(s_yd, NG * c + j + 1)
                        yv = ysb[:, j * 512:(j + 1) * 512]
                        expav = bass.AP(expa[:].tensor, expa[:].offset + j * 8,
                                        [list(expa[:].ap[0]), [1, 8], [0, 64]])
                        e.tensor_tensor(tmpg[:].rearrange("p (h q) -> p h q", q=64),
                                        BK(6 + gsl).rearrange("p (h q) -> p h q", q=64),
                                        expav, ALU.mult)
                        e.tensor_add(yv, yv, tmpg[:])
                        e.memset(junk[:, 2:3], 0.0).then_inc(s_yo, 1)
                        eatv = bass.AP(eatb[:].tensor, eatb[:].offset + j * 8,
                                       [list(eatb[:].ap[0]), [1, 8], [0, 64]])
                        sv = stf[:, j * 512:(j + 1) * 512]
                        e.tensor_tensor(tmpg[:].rearrange("p (h q) -> p h q", q=64),
                                        sv.rearrange("p (h q) -> p h q", q=64),
                                        eatv, ALU.mult)
                        e.tensor_add(sv, tmpg[:], BK(gsl))
                        e.tensor_copy(stb[:, j * 512:(j + 1) * 512], sv)
                        e.memset(junk[:, 3:4], 0.0).then_inc(s_up, 1)
                    e.wait_ge(s_yo, NG * (c + 1))
                    e.tensor_tensor(dres[:], hv, dpeb[:], ALU.mult)
                    e.tensor_add(ysb[:], ysb[:], dres[:])
                    if c >= 2:
                        e.wait_ge(s_st[sl], 16 * (c // 2))
                    e.tensor_copy(ybf[:, sl * 4096:(sl + 1) * 4096], ysb[:])
                    e.memset(junk[:, 2:3], 0.0).then_inc(s_dr, 1)

            @blk.tensor
            def _(e):
                for c in range(NCH):
                    sl = c % 2
                    id64 = sidf[0:64, 0:64]
                    e.wait_ge(s_dtp, 4 * c + 4)
                    e.wait_ge(s_tpe, 3 * c)
                    e.transpose(psm[:, 3 * 512:3 * 512 + 64],
                                dts_[0:64, c * 128:(c + 1) * 128], id64).then_inc(s_tp, 1)
                    e.transpose(psm[:, 3 * 512 + 64:3 * 512 + 128],
                                afc[0:64, 384:512], id64).then_inc(s_tp, 1)
                    e.transpose(psm[:, 3 * 512 + 128:3 * 512 + 192],
                                afc[0:64, 128:256], id64).then_inc(s_tp, 1)
                    e.wait_ge(s_dtp, 4 * c + 5)
                    e.matmul(psm[:, 3 * 512 + 192:3 * 512 + 256], onc[0:1, :],
                             eatr[0:1, 0:64], start=True, stop=True).then_inc(s_tp, 1)
                    e.wait_ge(s_ld[sl], 64 * (c // 2 + 1))
                    if c >= 1:
                        e.wait_ge(s_up, NG * c + NG)
                        e.wait_ge(s_cbe, 4 * c)
                    for g in range(4):
                        e.matmul(psm[:, g * 128:(g + 1) * 128],
                                 bfs[:, sl * 512 + g * 128: sl * 512 + (g + 1) * 128],
                                 cfs[:, sl * 512 + g * 128: sl * 512 + (g + 1) * 128],
                                 start=True, stop=True).then_inc(s_cb, 1)
                    for gg in range(16):
                        i = 16 * c + gg
                        e.wait_ge(s_ar[gg % 2], 16 * (i // 2 + 1))
                        e.wait_ge(s_br, 32 * c + 16)
                        if i >= 2:
                            e.wait_ge(s_ex, i - 1)
                        if c >= 1 and gg == 0:
                            e.wait_ge(s_up, NG * c + NG)
                        bank = 1 + gg % 2
                        e.matmul(BK(bank), onc[0:1, :],
                                 acsr[0:1, (gg % 2) * 512:(gg % 2 + 1) * 512],
                                 start=True, stop=False)
                        e.matmul(BK(bank), nacs4[0:4, gg * 128:(gg + 1) * 128],
                                 hse[0:4, :], start=False, stop=True).then_inc(s_dm, 1)
                    e.wait_ge(s_hd, c + 1)
                    e.wait_ge(s_cbe, 4 * (c + 1))
                    e.wait_ge(s_ex, 16 * (c + 1))
                    for j in range(NG):
                        G = NG * c + j
                        gsl = j % 2
                        g = j // 2
                        e.wait_ge(s_sm, 16 * c + 2 * j + 2)
                        if G >= 2:
                            e.wait_ge(s_yd, G - 1)
                            e.wait_ge(s_yo, G - 1)
                            e.wait_ge(s_up, NG + G - 1)
                        for hh in range(8):
                            h = j * 8 + hh
                            e.matmul(psm[:, (4 + gsl) * 512 + hh * 64:
                                          (4 + gsl) * 512 + (hh + 1) * 64],
                                     smT[:, h * 128:(h + 1) * 128],
                                     hdt[:, h * 64:(h + 1) * 64],
                                     start=True, stop=True)
                        e.wait_ge(s_up, NG + G)
                        for hh in range(8):
                            h = j * 8 + hh
                            e.matmul(psm[:, (6 + gsl) * 512 + hh * 64:
                                          (6 + gsl) * 512 + (hh + 1) * 64],
                                     cfs[:, sl * 512 + g * 128: sl * 512 + (g + 1) * 128],
                                     stb[:, h * 64:(h + 1) * 64],
                                     start=True, stop=True)
                        ins = None
                        for hh in range(8):
                            h = j * 8 + hh
                            ins = e.matmul(psm[:, gsl * 512 + hh * 64:
                                               gsl * 512 + (hh + 1) * 64],
                                           bts[:, sl * 512 + g * 128: sl * 512 + (g + 1) * 128],
                                           hdt2[:, h * 64:(h + 1) * 64],
                                           start=True, stop=True)
                        ins.then_inc(s_pey, 1)

            @blk.gpsimd
            def _(e):
                for c in range(NCH):
                    sl = c % 2
                    e.wait_ge(s_dtp, 4 * c + 4)
                    e.dma_start(out=acs_dram[0], in_=afc[0:64, 128:256]).then_inc(s_bo, 16)
                    e.dma_start(out=acs_dram[1], in_=afc[0:64, 256:384]).then_inc(s_bo, 16)
                    e.dma_start(out=alast_dram, in_=afc[0:64, 512:513]).then_inc(s_bo, 16)
                    e.wait_ge(s_bo, 48 * (c + 1))
                    e.dma_start(out=nacs4[0:4, :].rearrange("p (g l) -> p g l", l=128),
                                in_=bass.AP(acs_dram.tensor, acs_dram.offset + 64 * 128,
                                            [[128, 4], [512, 16], [1, 128]])
                                ).then_inc(s_br, 16)
                    e.dma_start(out=eatr[0:1, :],
                                in_=alast_dram.rearrange("(o k) -> o k", o=1)
                                ).then_inc(s_br, 16)
                    for gg in range(16):
                        i = 16 * c + gg
                        if i >= 2:
                            e.wait_ge(s_dm, i - 1)
                        e.dma_start(out=acsr[0:1, (gg % 2) * 512:(gg % 2 + 1) * 512],
                                    in_=bass.AP(acs_dram.tensor, acs_dram.offset + gg * 512,
                                                [[0, 1], [1, 512]])).then_inc(s_ar[gg % 2], 16)
                    e.wait_ge(s_dr, c + 1)
                    e.dma_start(out=o_y[c * 128:(c + 1) * 128, :],
                                in_=ybf[:, sl * 4096:(sl + 1) * 4096]).then_inc(s_st[sl], 16)
                e.wait_ge(s_st[0], 16 * 8)
                e.wait_ge(s_st[1], 16 * 8)
    return nc


def run_l5_ssd(hbc, dtf, A_log, Dp):
    nc = cached("l5_ssd", build_l5_ssd)
    hid_f = hbc[:8192]
    Bf = hbc[8192:9216]
    Cf = hbc[9216:10240]
    iden_f = np.eye(128, dtype=np.float32)
    tri4 = np.zeros((128, 512), np.float32)
    for k in range(4):
        tri4[:, k * 128:(k + 1) * 128] = np.triu(np.ones((128, 128), np.float32))
    hsel = np.zeros((4, 512), np.float32)
    for k in range(4):
        hsel[k, k * 128:(k + 1) * 128] = 1.0
    onef = np.ones((1, 128), np.float32)
    in_maps = []
    for core in range(NCORE):
        b, hg = divmod(core, 2)
        tok = slice(b * S, (b + 1) * S)
        hs = slice(hg * 4096, (hg + 1) * 4096)
        gs = slice(hg * 512, (hg + 1) * 512)
        Bfm = np.ascontiguousarray(Bf[gs, tok].reshape(4, 128, S))
        Cfm = np.ascontiguousarray(Cf[gs, tok].reshape(4, 128, S))
        Btm = np.ascontiguousarray(Bf[gs, tok].T)
        in_maps.append({
            "hid": np.ascontiguousarray(hid_f[hs, tok].T),
            "Bfm": Bfm, "Cfm": Cfm, "Btm": Btm,
            "dt": np.ascontiguousarray(dtf[hg * 64:(hg + 1) * 64, tok]),
            "alog": np.ascontiguousarray(A_log[hg * 64:(hg + 1) * 64].astype(np.float32).reshape(64, 1)),
            "dpx": np.ascontiguousarray(np.broadcast_to(
                np.repeat(Dp[hg * 64:(hg + 1) * 64].astype(np.float32), 64), (128, 4096))),
            "idf": iden_f, "tri4": tri4, "hsel": hsel, "onef": onef,
        })
    res = run_bass_kernel_spmd(nc, in_maps, core_ids=list(range(NCORE)))
    y = np.zeros((B, S, 8192), BF)
    for core in range(NCORE):
        b, hg = divmod(core, 2)
        y[b, :, hg * 4096:(hg + 1) * 4096] = np.asarray(res.results[core]["o_y"])
    return y.reshape(B * S, 8192)


def build_l6a_gate():
    nc = _mknc()
    y = nc.dram_tensor("y", [TC, 8192], BF16, kind="ExternalInput").ap()
    gt = nc.dram_tensor("gt", [TC, 8192], BF16, kind="ExternalInput").ap()
    gnw = nc.dram_tensor("gnw", [128, 8192], F32, kind="ExternalInput").ap()
    o_gn = nc.dram_tensor("o_gn", [TC, 8192], BF16, kind="ExternalOutput").ap()
    NT = TC // 128
    es = ExitStack()
    sbuf = lambda n, s_, d=F32: es.enter_context(nc.sbuf_tensor(n, s_, d))
    sem = lambda n: es.enter_context(nc.semaphore(n))
    with es:
        ys = sbuf("ys", [128, 2 * 8192], BF16)
        gs = sbuf("gs", [128, 8192], BF16)
        gn = sbuf("gn", [128, 8192])
        tg = sbuf("tg", [128, 8192], BF16)
        yg = sbuf("yg", [128, 8192])
        sq = sbuf("sq", [128, 8192])
        ob = sbuf("ob", [128, 2 * 8192], BF16)
        ms = sbuf("ms", [128, 16])
        junk = sbuf("junk", [128, 4])
        s_cld = sem("s_cld")
        s_ld = [sem("s_ld0"), sem("s_ld1")]
        s_sg = sem("s_sg")
        s_yg = sem("s_yg")
        s_ms = sem("s_ms")
        s_gn = sem("s_gn")
        s_st = [sem("s_st0"), sem("s_st1")]
        s_ss = sem("s_ss")
        with nc.Block() as blk:
            @blk.sync
            def _(e):
                e.dma_start(out=gn[:], in_=gnw).then_inc(s_cld, 16)
                for m in range(NT):
                    sl = m % 2
                    if m >= 2:
                        e.wait_ge(s_gn, m - 1)
                    e.dma_start(out=ys[:, sl * 8192:(sl + 1) * 8192],
                                in_=y[m * 128:(m + 1) * 128, :]).then_inc(s_ld[sl], 16)
                    if m >= 1:
                        e.wait_ge(s_sg, m)
                    e.dma_start(out=gs[:],
                                in_=gt[m * 128:(m + 1) * 128, :]).then_inc(s_ld[sl], 16)

            @blk.scalar
            def _(e):
                ssync = SelfSync(s_ss)
                for m in range(NT):
                    sl = m % 2
                    e.wait_ge(s_ld[sl], 32 * (m // 2 + 1))
                    if m >= 1:
                        e.wait_ge(s_yg, m)
                    e.activation(out=tg[:], in_=gs[:], func=AF.Silu)
                    e.memzero(junk[:, 0:1]).then_inc(s_sg, 1)
                    e.wait_ge(s_yg, m + 1)
                    e.activation(out=ms[:, 8:16], in_=ms[:, 0:8], func=AF.Copy,
                                 scale=1.0 / 1024.0, bias=EPS)
                    e.activation(out=ms[:, 8:16], in_=ms[:, 8:16], func=AF.Ln)
                    ins = e.activation(out=ms[:, 8:16], in_=ms[:, 8:16], func=AF.Exp, scale=-0.5)
                    ssync.barrier(e, ins)
                    e.memzero(junk[:, 0:1]).then_inc(s_ms, 1)

            @blk.vector
            def _(e):
                e.wait_ge(s_cld, 16)
                for m in range(NT):
                    sl = m % 2
                    e.wait_ge(s_sg, m + 1)
                    e.tensor_tensor(yg[:], ys[:, sl * 8192:(sl + 1) * 8192], tg[:], ALU.mult)
                    e.tensor_tensor(sq[:], yg[:], yg[:], ALU.mult)
                    e.reduce_sum(ms[:, 0:8], sq[:].rearrange("p (g n) -> p g n", g=8),
                                 axis=AX.X)
                    e.memset(junk[:, 2:3], 0.0).then_inc(s_yg, 1)
                    e.wait_ge(s_ms, m + 1)
                    if m >= 2:
                        e.wait_ge(s_st[sl], 16 * (m // 2))
                    for g in range(8):
                        e.scalar_tensor_tensor(
                            ob[:, sl * 8192 + g * 1024: sl * 8192 + (g + 1) * 1024],
                            yg[:, g * 1024:(g + 1) * 1024], ms[:, 8 + g:9 + g],
                            gn[:, g * 1024:(g + 1) * 1024], ALU.mult, ALU.mult)
                    e.memset(junk[:, 2:3], 0.0).then_inc(s_gn, 1)

            @blk.gpsimd
            def _(e):
                for m in range(NT):
                    sl = m % 2
                    e.wait_ge(s_gn, m + 1)
                    e.dma_start(out=o_gn[m * 128:(m + 1) * 128, :],
                                in_=ob[:, sl * 8192:(sl + 1) * 8192]).then_inc(s_st[sl], 16)
                e.wait_ge(s_st[0], 64)
                e.wait_ge(s_st[1], 64)
    return nc


def run_l6a_gate(y, gate_fm, gnorm_w):
    nc = cached("l6a_gate", build_l6a_gate)
    gnwb = np.ascontiguousarray(np.broadcast_to(gnorm_w.astype(np.float32), (128, 8192)))
    in_maps = []
    for c in range(NCORE):
        sl = slice(c * TC, (c + 1) * TC)
        in_maps.append({
            "y": np.ascontiguousarray(y[sl]),
            "gt": np.ascontiguousarray(gate_fm[:, sl].T),
            "gnw": gnwb,
        })
    res = run_bass_kernel_spmd(nc, in_maps, core_ids=list(range(NCORE)))
    return np.concatenate([np.asarray(r["o_gn"]) for r in res.results], 0)


def build_l6b_oproj():
    nc = _mknc()
    gnT = nc.dram_tensor("gnT", [8192, TC], BF16, kind="ExternalInput").ap()
    opwT = nc.dram_tensor("opwT", [8192, D], F32, kind="ExternalInput").ap()
    xn1 = nc.dram_tensor("xn1", [TC, D], BF16, kind="ExternalInput").ap()
    xn1T = nc.dram_tensor("xn1T", [D, TC], BF16, kind="ExternalInput").ap()
    w1T = nc.dram_tensor("w1T", [D, D], F32, kind="ExternalInput").ap()
    b1 = nc.dram_tensor("b1", [1, D], F32, kind="ExternalInput").ap()
    a1 = nc.dram_tensor("a1", [D, 32], F32, kind="ExternalInput").ap()
    liT = nc.dram_tensor("liT", [32, 8], F32, kind="ExternalInput").ap()
    parf = nc.dram_tensor("parf", [8, TC], F32, kind="ExternalInput").ap()
    o_s = nc.dram_tensor("o_s", [TC, D], F32, kind="ExternalOutput").ap()
    o_H = nc.dram_tensor("o_H", [TC, D], BF16, kind="ExternalOutput").ap()
    o_zfm = nc.dram_tensor("o_zfm", [32, TC], F32, kind="ExternalOutput").ap()
    o_lm = nc.dram_tensor("o_lm", [1, 2], F32, kind="ExternalOutput").ap()
    lm_dram = nc.dram_tensor("lm_dram", [16], F32, kind="Internal").ap()

    NT = TC // 128
    NO = D // 512
    NKG = 8192 // 128
    NKH = D // 128
    NGW = NO * NKG + NO * NKH

    es = ExitStack()
    sbuf = lambda n, s_, d=F32: es.enter_context(nc.sbuf_tensor(n, s_, d))
    sem = lambda n: es.enter_context(nc.semaphore(n))
    with es:
        gts = sbuf("gts", [128, 2 * TC], BF16)
        x1ts = sbuf("x1ts", [128, NKH * TC], BF16)
        x1s = sbuf("x1s", [128, 4 * 512], BF16)
        wb = sbuf("wb", [128, 2 * 512])
        wbb = sbuf("wbb", [128, 2 * 512], BF16)
        a1b = sbuf("a1b", [128, NKH * 32], BF16)
        w8b = sbuf("w8b", [128, NKH * 8], BF16)
        brow = sbuf("brow", [1, D])
        browb = sbuf("browb", [1, D], BF16)
        b8f = sbuf("b8f", [1, 8])
        b8r = sbuf("b8r", [1, 8], BF16)
        onesw = sbuf("onesw", [1, 512], BF16)
        liTf = sbuf("liTf", [32, 8])
        onesr = sbuf("onesr", [1, 128], BF16)
        liTs = sbuf("liTs", [32, 8], BF16)
        pars = sbuf("pars", [8, TC])
        otile = sbuf("otile", [128, 4 * 512])
        htile = sbuf("htile", [128, 4 * 512], BF16)
        zfm = sbuf("zfm", [32, TC], BF16)
        zff = sbuf("zff", [32, TC])
        hsf = sbuf("hsf", [8, TC])
        dsf = sbuf("dsf", [8, TC])
        lmx = sbuf("lmx", [8, 4])
        lrow = sbuf("lrow", [1, 16])
        junk = sbuf("junk", [128, 4])
        psm = es.enter_context(nc.psum_tensor("psm", [128, 8 * 512], F32))

        s_cld = sem("s_cld")
        s_cldg = sem("s_cldg")
        s_gl = [sem("s_gl0"), sem("s_gl1")]
        s_wl = [sem("s_wl0"), sem("s_wl1")]
        s_x1l = [sem(f"s_x1l{j}") for j in range(4)]
        s_wc = sem("s_wc")
        s_pre = sem("s_pre")
        s_pe2 = sem("s_pe2")
        s_lm = sem("s_lm")
        s_mm = sem("s_mm")
        s_ev = sem("s_ev")
        s_he = sem("s_he")
        s_st = [sem(f"s_st{j}") for j in range(4)]
        s_sth = [sem(f"s_sth{j}") for j in range(4)]
        s_zo = sem("s_zo")
        s_bb = sem("s_bb")

        with nc.Block() as blk:
            @blk.sync
            def _(e):
                e.dma_start(out=brow[:], in_=b1).then_inc(s_cld, 16)
                e.dma_start(out=b8f[:], in_=b1[0:1, 0:8]).then_inc(s_cld, 16)
                e.dma_start(out=liTf[:], in_=liT).then_inc(s_cld, 16)
                e.dma_start(out=pars[:], in_=parf).then_inc(s_cld, 16)
                e.dma_start(out=x1ts[:].rearrange("p (c t) -> p c t", t=TC),
                            in_=xn1T.rearrange("(c p) t -> p c t", p=128)).then_inc(s_cld, 16)
                for g in range(NGW):
                    oc_, k_ = (g // NKG, g % NKG) if g < NO * NKG else \
                        ((g - NO * NKG) // NKH, (g - NO * NKG) % NKH)
                    srcw = opwT if g < NO * NKG else w1T
                    if g >= 2:
                        e.wait_ge(s_wc, g - 1)
                    e.dma_start(out=wb[:, (g % 2) * 512:(g % 2 + 1) * 512],
                                in_=srcw[k_ * 128:(k_ + 1) * 128, oc_ * 512:(oc_ + 1) * 512]
                                ).then_inc(s_wl[g % 2], 16)
                    if g < NO * NKG:
                        if g >= 2:
                            e.wait_ge(s_mm, g - 1)
                        e.dma_start(out=gts[:, (k_ % 2) * TC:(k_ % 2 + 1) * TC],
                                    in_=gnT[k_ * 128:(k_ + 1) * 128, :]).then_inc(s_gl[k_ % 2], 16)

            @blk.vector
            def _(e):
                e.wait_ge(s_cld, 80)
                e.wait_ge(s_cldg, 32)
                e.tensor_copy(browb[0:1, :], brow[0:1, :])
                e.memset(onesr[0:1, :].bitcast(mybir.dt.uint16), 0x3F80)
                e.memset(onesw[0:1, :].bitcast(mybir.dt.uint16), 0x3F80)
                e.tensor_copy(liTs[:], liTf[:])
                e.tensor_copy(b8r[0:1, :], b8f[0:1, :])
                e.memset(junk[:, 2:3], 0.0).then_inc(s_cld, 1)
                e.wait_ge(s_pre, 1)
                e.tensor_copy(zff[:, 0:512], psm[0:32, 0:512])
                e.tensor_copy(zff[:, 512:1024], psm[0:32, 512:1024])
                e.tensor_copy(zfm[:, 0:512], psm[0:32, 0:512])
                e.tensor_copy(zfm[:, 512:1024], psm[0:32, 512:1024])
                e.memset(junk[:, 2:3], 0.0).then_inc(s_pe2, 1)
                e.wait_ge(s_pre, 2)
                e.tensor_copy(hsf[:, 0:512], psm[0:8, 2 * 512:3 * 512])
                e.tensor_copy(hsf[:, 512:1024], psm[0:8, 3 * 512:4 * 512])
                e.memset(junk[:, 2:3], 0.0).then_inc(s_pe2, 1)
                e.wait_ge(s_pre, 3)
                e.tensor_copy(dsf[:, 0:512], psm[0:8, 4 * 512:5 * 512])
                e.tensor_copy(dsf[:, 512:1024], psm[0:8, 5 * 512:6 * 512])
                e.memset(junk[:, 2:3], 0.0).then_inc(s_pe2, 1)
                e.tensor_tensor(hsf[0:8, :], hsf[0:8, :], pars[0:8, :], ALU.mult)
                e.tensor_tensor(dsf[0:8, :], dsf[0:8, :], pars[0:8, :], ALU.mult)
                e.reduce_max(lmx[0:8, 0:1], hsf[0:8, :], axis=AX.X, apply_absolute_value=True)
                e.reduce_max(lmx[0:8, 1:2], dsf[0:8, :], axis=AX.X, apply_absolute_value=True)
                e.memset(junk[:, 2:3], 0.0).then_inc(s_lm, 1)
                e.wait_ge(s_bb, 32)
                e.reduce_max(o_lm_s := lrow[0:1, 0:1], strided(lrow[0:1, 0:16], 0, 2, 8),
                             axis=AX.X, apply_absolute_value=True)
                e.reduce_max(lrow[0:1, 1:2], strided(lrow[0:1, 0:16], 1, 2, 8),
                             axis=AX.X, apply_absolute_value=True)
                e.memset(junk[:, 2:3], 0.0).then_inc(s_lm, 1)
                def mmcount(j):
                    if j < NO * NKG:
                        return j + 1
                    oc2, k2 = (j - NO * NKG) // NKH, (j - NO * NKG) % NKH
                    return NO * NKG + 33 * oc2 + k2 + 1
                for g in range(NGW):
                    e.wait_ge(s_wl[g % 2], 16 * (g // 2 + 1))
                    if g >= 2:
                        e.wait_ge(s_mm, mmcount(g - 2))
                    e.tensor_copy(wbb[:, (g % 2) * 512:(g % 2 + 1) * 512],
                                  wb[:, (g % 2) * 512:(g % 2 + 1) * 512])
                    e.memset(junk[:, 2:3], 0.0).then_inc(s_wc, 1)
                    if g < NO * NKG and g % NKG == NKG - 1:
                        oc = g // NKG
                        e.wait_ge(s_mm, (oc + 1) * NKG)
                        for m in range(NT):
                            i = oc * NT + m
                            e.wait_ge(s_x1l[i % 4], 16 * (i // 4 + 1))
                            if i >= 4:
                                e.wait_ge(s_st[i % 4], 16 * ((i - 4) // 4 + 1))
                            e.tensor_tensor(otile[:, (i % 4) * 512:(i % 4 + 1) * 512],
                                            psm[:, m * 512:(m + 1) * 512],
                                            x1s[:, (i % 4) * 512:(i % 4 + 1) * 512],
                                            ALU.add)
                            e.memset(junk[:, 2:3], 0.0).then_inc(s_ev, 1)
                    elif g >= NO * NKG and (g - NO * NKG) % NKH == NKH - 1:
                        oc = (g - NO * NKG) // NKH
                        e.wait_ge(s_mm, NO * NKG + (oc + 1) * NKH + (oc + 1))
                        for m in range(NT):
                            i = oc * NT + m
                            if i >= 4:
                                e.wait_ge(s_sth[i % 4], 16 * ((i - 4) // 4 + 1))
                            e.tensor_copy(htile[:, (i % 4) * 512:(i % 4 + 1) * 512],
                                          psm[:, m * 512:(m + 1) * 512])
                            e.memset(junk[:, 2:3], 0.0).then_inc(s_he, 1)

            @blk.tensor
            def _(e):
                e.wait_ge(s_cld, 81)
                e.wait_ge(s_cldg, 32)
                for n in range(2):
                    for k in range(NKH):
                        mmi = e.matmul(psm[0:32, n * 512:(n + 1) * 512],
                                       a1b[:, k * 32:(k + 1) * 32],
                                       x1ts[:, k * TC + n * 512: k * TC + (n + 1) * 512],
                                       start=(k == 0), stop=(k == NKH - 1))
                mmi.then_inc(s_pre, 1)
                for n in range(2):
                    for k in range(NKH):
                        e.matmul(psm[0:8, (2 + n) * 512:(3 + n) * 512],
                                 w8b[:, k * 8:(k + 1) * 8],
                                 x1ts[:, k * TC + n * 512: k * TC + (n + 1) * 512],
                                 start=(k == 0), stop=False)
                    mmi = e.matmul(psm[0:8, (2 + n) * 512:(3 + n) * 512],
                                   b8r[0:1, 0:8], onesw[0:1, 0:512],
                                   start=False, stop=True)
                mmi.then_inc(s_pre, 1)
                e.wait_ge(s_pe2, 1)
                for n in range(2):
                    mmi = e.matmul(psm[0:8, (4 + n) * 512:(5 + n) * 512],
                                   liTs[0:32, 0:8],
                                   zfm[0:32, n * 512:(n + 1) * 512],
                                   start=True, stop=True)
                mmi.then_inc(s_pre, 1)
                for g in range(NGW):
                    e.wait_ge(s_wc, g + 1)
                    if g < NO * NKG:
                        oc, k = g // NKG, g % NKG
                        e.wait_ge(s_gl[k % 2], 16 * ((oc * NKG + k) // 2 + 1))
                        if k == 0:
                            if oc == 0:
                                e.wait_ge(s_pe2, 3)
                                e.wait_ge(s_lm, 1)
                            else:
                                e.wait_ge(s_ev, oc * NT)
                        for m in range(NT):
                            mmi = e.matmul(psm[:, m * 512:(m + 1) * 512],
                                           gts[:, (k % 2) * TC + m * 128:(k % 2) * TC + m * 128 + 128],
                                           wbb[:, (g % 2) * 512:(g % 2 + 1) * 512],
                                           start=(k == 0), stop=(k == NKG - 1))
                        mmi.then_inc(s_mm, 1)
                    else:
                        oc, k = (g - NO * NKG) // NKH, (g - NO * NKG) % NKH
                        if k == 0:
                            if oc == 0:
                                e.wait_ge(s_ev, NO * NT)
                            else:
                                e.wait_ge(s_he, oc * NT)
                        for m in range(NT):
                            mmi = e.matmul(psm[:, m * 512:(m + 1) * 512],
                                           x1ts[:, k * TC + m * 128: k * TC + m * 128 + 128],
                                           wbb[:, (g % 2) * 512:(g % 2 + 1) * 512],
                                           start=(k == 0), stop=False)
                        mmi.then_inc(s_mm, 1)
                        if k == NKH - 1:
                            for m in range(NT):
                                mmi = e.matmul(psm[:, m * 512:(m + 1) * 512],
                                               onesr[0:1, 0:128],
                                               browb[0:1, oc * 512:(oc + 1) * 512],
                                               start=False, stop=True)
                            mmi.then_inc(s_mm, 1)

            @blk.gpsimd
            def _(e):
                e.dma_start(out=a1b[:].rearrange("p (c o) -> p c o", o=32),
                            in_=a1.rearrange("(c p) o -> p c o", p=128)).then_inc(s_cldg, 16)
                e.dma_start(out=w8b[:].rearrange("p (c o) -> p c o", o=8),
                            in_=w1T.rearrange("(c p) o -> p c o", p=128)[:, :, 0:8]
                            ).then_inc(s_cldg, 16)
                def x1load(i):
                    oc, m = divmod(i, NT)
                    e.dma_start(out=x1s[:, (i % 4) * 512:(i % 4 + 1) * 512],
                                in_=xn1[m * 128:(m + 1) * 128, oc * 512:(oc + 1) * 512]
                                ).then_inc(s_x1l[i % 4], 16)
                for i in range(4):
                    x1load(i)
                e.wait_ge(s_lm, 1)
                e.dma_start(out=lm_dram.rearrange("(a b) -> a b", a=8), in_=lmx[0:8, 0:2]).then_inc(s_bb, 16)
                e.wait_ge(s_bb, 16)
                e.dma_start(out=lrow[0:1, :], in_=lm_dram.rearrange("(o k) -> o k", o=1)).then_inc(s_bb, 16)
                e.wait_ge(s_lm, 2)
                e.dma_start(out=o_lm, in_=lrow[0:1, 0:2]).then_inc(s_zo, 16)
                e.dma_start(out=o_zfm, in_=zff[0:32, :]).then_inc(s_zo, 16)
                for i in range(NO * NT):
                    oc, m = divmod(i, NT)
                    e.wait_ge(s_ev, i + 1)
                    e.dma_start(out=o_s[m * 128:(m + 1) * 128, oc * 512:(oc + 1) * 512],
                                in_=otile[:, (i % 4) * 512:(i % 4 + 1) * 512]
                                ).then_inc(s_st[i % 4], 16)
                    if i + 4 < NO * NT:
                        x1load(i + 4)
                for i in range(NO * NT):
                    oc, m = divmod(i, NT)
                    e.wait_ge(s_he, i + 1)
                    e.dma_start(out=o_H[m * 128:(m + 1) * 128, oc * 512:(oc + 1) * 512],
                                in_=htile[:, (i % 4) * 512:(i % 4 + 1) * 512]
                                ).then_inc(s_sth[i % 4], 16)
                for j in range(4):
                    e.wait_ge(s_st[j], 16 * 16)
                    e.wait_ge(s_sth[j], 16 * 16)
                e.wait_ge(s_zo, 32)
    return nc


def run_l6b_oproj(gn, out_proj_w, xn1, W1, b1v, A1, LiMEs):
    nc = cached("l6b_oproj", build_l6b_oproj)
    opwT = np.ascontiguousarray(out_proj_w.T.astype(np.float32))
    w1T = np.ascontiguousarray(W1.T.astype(np.float32))
    liT = np.ascontiguousarray(LiMEs.T.astype(np.float32))
    browv = np.ascontiguousarray(b1v.astype(np.float32).reshape(1, D))
    xn1T_full = np.ascontiguousarray(xn1.T)
    gnT_full = np.ascontiguousarray(gn.T)
    in_maps = []
    for c in range(NCORE):
        sl = slice(c * TC, (c + 1) * TC)
        par = np.zeros((8, TC), np.float32)
        par[:, 1::2] = 1.0
        in_maps.append({
            "gnT": np.ascontiguousarray(gnT_full[:, sl]),
            "opwT": opwT,
            "xn1": np.ascontiguousarray(xn1[sl]),
            "xn1T": np.ascontiguousarray(xn1T_full[:, sl]),
            "w1T": w1T, "b1": browv, "a1": A1.astype(np.float32),
            "liT": liT, "parf": par,
        })
    res = run_bass_kernel_spmd(nc, in_maps, core_ids=list(range(NCORE)))
    s = np.concatenate([np.asarray(r["o_s"]) for r in res.results], 0)
    H = np.concatenate([np.asarray(r["o_H"]) for r in res.results], 0)
    zfm = np.concatenate([np.asarray(r["o_zfm"]) for r in res.results], 1)
    lm = np.stack([np.asarray(r["o_lm"]).reshape(2) for r in res.results], 0)
    return s, H, zfm, lm


def build_l7_final():
    nc = _mknc()
    sdr = nc.dram_tensor("sdr", [TC, D], F32, kind="ExternalInput").ap()
    H = nc.dram_tensor("H", [TC, D], BF16, kind="ExternalInput").ap()
    zfm = nc.dram_tensor("zfm", [32, TC], F32, kind="ExternalInput").ap()
    lm16 = nc.dram_tensor("lm16", [1, 16], F32, kind="ExternalInput").ap()
    liT = nc.dram_tensor("liT", [32, 8], F32, kind="ExternalInput").ap()
    lim = nc.dram_tensor("lim", [8, 32], F32, kind="ExternalInput").ap()
    bmat = nc.dram_tensor("bmat", [32, D], F32, kind="ExternalInput").ap()
    x1 = nc.dram_tensor("x1", [TC, D], F32, kind="ExternalInput").ap()
    onef = nc.dram_tensor("onef", [1, 128], F32, kind="ExternalInput").ap()
    idb = nc.dram_tensor("idb", [128, 128], BF16, kind="ExternalInput").ap()
    o_out = nc.dram_tensor("o_out", [TC, D], F32, kind="ExternalOutput").ap()
    wn_dram = nc.dram_tensor("wn_dram", [TC, 8], F32, kind="Internal").ap()

    NT = TC // 128
    es = ExitStack()
    sbuf = lambda n, s_, d=F32: es.enter_context(nc.sbuf_tensor(n, s_, d))
    sem = lambda n: es.enter_context(nc.semaphore(n))
    with es:
        ss_ = sbuf("ss_", [128, 2 * D])
        x1s = sbuf("x1s", [128, 2 * D])
        hs = sbuf("hs", [128, 2 * D], BF16)
        hf = sbuf("hf", [128, D])
        zs = sbuf("zs", [32, TC])
        zbt = sbuf("zbt", [32, 128], BF16)
        liTb = sbuf("liTb", [32, 8], BF16)
        liTf = sbuf("liTf", [32, 8])
        limb = sbuf("limb", [8, 32], BF16)
        limf = sbuf("limf", [8, 32])
        bmb = sbuf("bmb", [32, D], BF16)
        lrow = sbuf("lrow", [1, 16])
        sclc = sbuf("sclc", [128, 8])
        onc = sbuf("onc", [1, 128])
        sid = sbuf("sid", [128, 128], BF16)
        rt = sbuf("rt", [128, 8 * 8])
        wnt = sbuf("wnt", [128, 8])
        wxp = sbuf("wxp", [128, 8], BF16)
        wfT = sbuf("wfT", [8, 128], BF16)
        pmx = sbuf("pmx", [32, 128])
        gsl = sbuf("gsl", [128, D])
        ot = sbuf("ot", [128, 2 * D])
        junk = sbuf("junk", [128, 4])
        psm = es.enter_context(nc.psum_tensor("psm", [128, 7 * 512], F32))
        psb = es.enter_context(nc.psum_tensor("psb", [128, 1024], BF16))

        s_cld = sem("s_cld")
        s_ld = [sem("s_ld0"), sem("s_ld1")]
        s_scl = sem("s_scl")
        s_dsm = sem("s_dsm")
        s_rt = sem("s_rt")
        s_rs = sem("s_rs")
        s_vb = sem("s_vb")
        s_bw = sem("s_bw")
        s_tw = sem("s_tw")
        s_twe = sem("s_twe")
        s_pm = sem("s_pm")
        s_pme = sem("s_pme")
        s_dl = sem("s_dl")
        s_vd = sem("s_vd")
        s_ge = sem("s_ge")
        s_hf = sem("s_hf")
        s_st = [sem("s_st0"), sem("s_st1")]

        with nc.Block() as blk:
            @blk.sync
            def _(e):
                e.dma_start(out=lrow[0:1, :], in_=lm16).then_inc(s_cld, 16)
                e.dma_start(out=liTf[:], in_=liT).then_inc(s_cld, 16)
                e.dma_start(out=limf[:], in_=lim).then_inc(s_cld, 16)
                e.dma_start(out=onc[:], in_=onef).then_inc(s_cld, 16)
                e.dma_start(out=sid[:], in_=idb).then_inc(s_cld, 16)
                e.dma_start(out=zs[:], in_=zfm).then_inc(s_cld, 16)
                for m in range(NT):
                    sl = m % 2
                    if m >= 2:
                        e.wait_ge(s_ge, 2 * (m - 1))
                        e.wait_ge(s_hf, m - 1)
                    e.dma_start(out=ss_[:, sl * D:(sl + 1) * D],
                                in_=sdr[m * 128:(m + 1) * 128, :]).then_inc(s_ld[sl], 16)
                    e.dma_start(out=x1s[:, sl * D:(sl + 1) * D],
                                in_=x1[m * 128:(m + 1) * 128, :]).then_inc(s_ld[sl], 16)
                    e.dma_start(out=hs[:, sl * D:(sl + 1) * D],
                                in_=H[m * 128:(m + 1) * 128, :]).then_inc(s_ld[sl], 16)

            @blk.gpsimd
            def _(e):
                e.dma_start(out=bmb[:], in_=bmat).then_inc(s_cld, 16)
                for m in range(NT):
                    e.wait_ge(s_rt, 5 * m + 5)
                    e.dma_start(out=wn_dram[m * 128:(m + 1) * 128, :],
                                in_=wnt[:]).then_inc(s_bw, 16)
                    e.wait_ge(s_bw, 32 * m + 16)
                    if m >= 1:
                        e.wait_ge(s_tw, m)
                    e.dma_start(out=wxp[:],
                                in_=bass.AP(wn_dram.tensor, wn_dram.offset + m * 1024 + 8,
                                            [[16, 64], [0, 2], [1, 8]])).then_inc(s_bw, 16)
                    e.wait_ge(s_ge, 2 * (m + 1))
                    e.dma_start(out=o_out[m * 128:(m + 1) * 128, :],
                                in_=ot[:, (m % 2) * D:(m % 2 + 1) * D]).then_inc(s_st[m % 2], 16)
                e.wait_ge(s_st[0], 64)
                e.wait_ge(s_st[1], 64)

            @blk.vector
            def _(e):
                vb = [0]

                def vbar():
                    e.memset(junk[:, 2:3], 0.0).then_inc(s_vb, 1)
                    vb[0] += 1
                    e.wait_ge(s_vb, vb[0])
                e.wait_ge(s_cld, 112)
                e.tensor_copy(liTb[:], liTf[:])
                e.tensor_copy(limb[:], limf[:])
                e.reduce_max(lrow[0:1, 2:3], strided(lrow[0:1, 0:16], 0, 2, 8), axis=AX.X)
                e.reduce_max(lrow[0:1, 3:4], strided(lrow[0:1, 0:16], 1, 2, 8), axis=AX.X)
                e.tensor_scalar(lrow[0:1, 2:4], lrow[0:1, 2:4], 1e-6, None, ALU.max)
                e.memset(junk[0:1, 0:1], 0.0).then_inc(s_scl, 1)
                e.wait_ge(s_scl, 2)
                e.reciprocal(sclc[:, 0:2], psm[:, 6 * 512:6 * 512 + 2])
                e.tensor_scalar(sclc[:, 0:2], sclc[:, 0:2], 0.5, None, ALU.mult)
                vbar()
                e.memset(junk[0:1, 0:1], 0.0).then_inc(s_scl, 1)
                for m in range(NT):
                    e.wait_ge(s_hf, m + 1)
                    e.wait_ge(s_dsm, m + 1)
                    e.tensor_scalar(rt[:, 0:8], hf[:, 0:8], sclc[:, 0:1], None, ALU.mult)
                    e.scalar_tensor_tensor(rt[:, 0:8], psm[:, 0:8], sclc[:, 1:2],
                                           rt[:, 0:8], ALU.mult, ALU.add)
                    e.reduce_max(rt[:, 8:9], rt[:, 0:8], axis=AX.X, negate=True)
                    e.memset(junk[:, 2:3], 0.0).then_inc(s_rt, 1)
                    e.wait_ge(s_rs, 4 * m + 1)
                    e.reciprocal(rt[:, 25:26], rt[:, 24:25])
                    vbar()
                    e.tensor_scalar(rt[:, 16:24], rt[:, 16:24], rt[:, 25:26], None, ALU.mult)
                    e.reduce_max(rt[:, 26:27], rt[:, 16:24], axis=AX.X)
                    vbar()
                    e.tensor_scalar(rt[:, 32:40], rt[:, 16:24], rt[:, 26:27], None, ALU.is_equal)
                    e.scalar_tensor_tensor(rt[:, 32:40], rt[:, 32:40], -1e9,
                                           rt[:, 16:24], ALU.mult, ALU.add)
                    e.reduce_max(rt[:, 27:28], rt[:, 32:40], axis=AX.X)
                    vbar()
                    e.tensor_scalar(rt[:, 28:29], rt[:, 27:28], -2.0, None, ALU.mult)
                    e.memset(junk[:, 2:3], 0.0).then_inc(s_rt, 1)
                    e.wait_ge(s_rs, 4 * m + 2)
                    if m >= 1:
                        e.wait_ge(s_bw, 32 * (m - 1) + 16)
                    e.tensor_tensor(rt[:, 40:48], rt[:, 16:24], rt[:, 48:56], ALU.mult)
                    e.reduce_sum(rt[:, 29:30], rt[:, 40:48], axis=AX.X)
                    vbar()
                    e.tensor_scalar(rt[:, 29:30], rt[:, 29:30], 1e-9, None, ALU.add)
                    e.reciprocal(rt[:, 30:31], rt[:, 29:30])
                    vbar()
                    e.tensor_scalar(wnt[:], rt[:, 40:48], rt[:, 30:31], None, ALU.mult)
                    e.memset(junk[:, 2:3], 0.0).then_inc(s_rt, 3)
                    e.wait_ge(s_tw, m + 1)
                    e.tensor_copy(wfT[0:8, :], psb[0:8, 0:128])
                    e.memset(junk[:, 2:3], 0.0).then_inc(s_twe, 1)
                    e.wait_ge(s_pm, m + 1)
                    e.tensor_tensor(zbt[0:32, :], psm[0:32, 512:512 + 128],
                                    zs[0:32, m * 128:(m + 1) * 128], ALU.mult)
                    e.memset(junk[:, 2:3], 0.0).then_inc(s_pme, 1)
                    for hh in range(2):
                        e.wait_ge(s_dl, 2 * m + hh + 1)
                        for nt in range(4):
                            e.scalar_tensor_tensor(
                                gsl[:, (hh * 4 + nt) * 512:(hh * 4 + nt + 1) * 512],
                                psm[:, (2 + nt) * 512:(3 + nt) * 512], 0.5,
                                hf[:, (hh * 4 + nt) * 512:(hh * 4 + nt + 1) * 512],
                                ALU.mult, ALU.add)
                        e.memset(junk[:, 2:3], 0.0).then_inc(s_vd, 1)
                        e.wait_ge(s_rs, 4 * m + 3 + hh)
                        if m >= 2 and hh == 0:
                            e.wait_ge(s_st[m % 2], 16 * (m // 2))
                        sl = m % 2
                        hv = gsl[:, hh * 2048:(hh + 1) * 2048]
                        e.tensor_tensor(ot[:, sl * D + hh * 2048: sl * D + (hh + 1) * 2048],
                                        ss_[:, sl * D + hh * 2048: sl * D + (hh + 1) * 2048],
                                        hv, ALU.mult)
                        e.tensor_add(ot[:, sl * D + hh * 2048: sl * D + (hh + 1) * 2048],
                                     ot[:, sl * D + hh * 2048: sl * D + (hh + 1) * 2048],
                                     x1s[:, sl * D + hh * 2048: sl * D + (hh + 1) * 2048])
                        e.memset(junk[:, 2:3], 0.0).then_inc(s_ge, 1)

            @blk.scalar
            def _(e):
                for m in range(NT):
                    sl = m % 2
                    e.wait_ge(s_ld[sl], 48 * (m // 2 + 1))
                    if m >= 1:
                        e.wait_ge(s_vd, 2 * m)
                    e.activation(out=hf[:], in_=hs[:, sl * D:(sl + 1) * D], func=AF.Copy)
                    e.memzero(junk[:, 1:2]).then_inc(s_hf, 1)
                    e.wait_ge(s_rt, 5 * m + 1)
                    e.activation(out=rt[:, 16:24], in_=rt[:, 0:8], func=AF.Exp,
                                 bias=rt[:, 8:9], accum_out=rt[:, 24:25])
                    e.memzero(junk[:, 1:2]).then_inc(s_rs, 1)
                    e.wait_ge(s_rt, 5 * m + 2)
                    e.activation(out=rt[:, 48:56], in_=rt[:, 16:24], func=AF.Sigmoid,
                                 scale=2.0, bias=rt[:, 28:29])
                    e.memzero(junk[:, 1:2]).then_inc(s_rs, 1)
                    for hh in range(2):
                        e.wait_ge(s_vd, 2 * m + hh + 1)
                        e.activation(out=gsl[:, hh * 2048:(hh + 1) * 2048],
                                     in_=gsl[:, hh * 2048:(hh + 1) * 2048], func=AF.Silu)
                        e.memzero(junk[:, 1:2]).then_inc(s_rs, 1)

            @blk.tensor
            def _(e):
                e.wait_ge(s_cld, 112)
                e.wait_ge(s_scl, 1)
                e.matmul(psm[:, 6 * 512:6 * 512 + 2], onc[0:1, :], lrow[0:1, 2:4],
                         start=True, stop=True).then_inc(s_scl, 1)
                for m in range(NT):
                    e.wait_ge(s_scl, 3)
                    if m >= 1:
                        e.wait_ge(s_rt, 5 * (m - 1) + 1)
                    e.matmul(psm[:, 0:8], zs[0:32, m * 128:(m + 1) * 128],
                             liTf[0:32, :], start=True, stop=True).then_inc(s_dsm, 1)
                    e.wait_ge(s_bw, 32 * m + 32)
                    if m >= 1:
                        e.wait_ge(s_twe, m)
                    e.transpose(psb[0:8, 0:128], wxp[:], sid[:]).then_inc(s_tw, 1)
                    e.wait_ge(s_twe, m + 1)
                    if m >= 1:
                        e.wait_ge(s_pme, m)
                    e.matmul(psm[0:32, 512:512 + 128], limb[0:8, :], wfT[0:8, :],
                             start=True, stop=True).then_inc(s_pm, 1)
                    e.wait_ge(s_pme, m + 1)
                    for hh in range(2):
                        if 2 * m + hh >= 1:
                            e.wait_ge(s_vd, 2 * m + hh)
                        mmi = None
                        for nt in range(4):
                            mmi = e.matmul(psm[:, (2 + nt) * 512:(3 + nt) * 512],
                                           zbt[0:32, :],
                                           bmb[0:32, (hh * 4 + nt) * 512:(hh * 4 + nt + 1) * 512],
                                           start=True, stop=True)
                        mmi.then_inc(s_dl, 1)
    return nc


def run_l7_final(s, H, zfm, lm, x1, LiMEs, Bmat):
    nc = cached("l7_final", build_l7_final)
    lm16 = np.ascontiguousarray(lm.reshape(1, 16).astype(np.float32))
    liT = np.ascontiguousarray(LiMEs.T.astype(np.float32))
    limf = np.ascontiguousarray(LiMEs.astype(np.float32))
    bm = np.ascontiguousarray(Bmat.astype(np.float32))
    onef = np.ones((1, 128), np.float32)
    idb = np.eye(128, dtype=BF)
    in_maps = []
    for c in range(NCORE):
        sl = slice(c * TC, (c + 1) * TC)
        in_maps.append({
            "sdr": np.ascontiguousarray(s[sl]),
            "H": np.ascontiguousarray(H[sl]),
            "zfm": np.ascontiguousarray(zfm[:, sl]),
            "lm16": lm16, "liT": liT, "lim": limf, "bmat": bm,
            "x1": np.ascontiguousarray(x1[sl]),
            "onef": onef, "idb": idb,
        })
    res = run_bass_kernel_spmd(nc, in_maps, core_ids=list(range(NCORE)))
    return np.concatenate([np.asarray(r["o_out"]) for r in res.results], 0)


def run_l1_qkv(x_flat, wq, wk, wv, qweight, wscale, wzp):
    nc = cached("l1_qkv", build_l1_qkv)
    wT = np.ascontiguousarray(np.concatenate([wq, wk, wv], 0).T)
    sctab = sincos_tables()
    zp2 = np.ascontiguousarray(np.concatenate([wzp, wscale], 1).astype(np.float32))
    iden = np.eye(128, dtype=np.float32)
    in_maps = []
    for c in range(NCORE):
        half = c % 2
        in_maps.append({
            "x": np.ascontiguousarray(x_flat[c * TC:(c + 1) * TC]),
            "wT": wT,
            "sc": np.ascontiguousarray(sctab[half * TC:(half + 1) * TC]),
            "qw": np.ascontiguousarray(qweight[c * 512:(c + 1) * 512]),
            "zp": np.ascontiguousarray(zp2[c * 512:(c + 1) * 512]),
            "iden": iden,
        })
    res = run_bass_kernel_spmd(nc, in_maps, core_ids=list(range(NCORE)))
    qkv = np.concatenate([np.asarray(r["o_qkv"]) for r in res.results], 0)
    xn = np.concatenate([np.asarray(r["o_xn"]) for r in res.results], 0)
    wdeq = np.concatenate([np.asarray(r["o_wd"]) for r in res.results], 0)
    return qkv, xn, wdeq


def kernel(**inputs):
    x = np.asarray(inputs["L0_attn_hidden_states"], np.float32)
    x_flat = np.ascontiguousarray(x.reshape(B * S, D))
    qkv, xn, wdeq = run_l1_qkv(
        x_flat, inputs["L0_attn_wq"], inputs["L0_attn_wk"], inputs["L0_attn_wv"],
        inputs["L0_ffn_qweight"], inputs["L0_ffn_weight_scale"],
        inputs["L0_ffn_weight_zero_point"])
    ao = run_l2_attn(qkv)
    x1 = run_l3_mix(x_flat, ao, xn, wdeq, inputs["L0_attn_wo"], inputs["L0_ffn_bias"])
    xn1, gate, hbc, dt = run_l4_proj(
        x1, inputs["L1_attn_in_proj_w"], inputs["L1_attn_norm_w"],
        inputs["L1_attn_conv_w"], inputs["L1_attn_conv_b"], inputs["L1_attn_dt_bias"])
    y = run_l5_ssd(hbc, dt, inputs["L1_attn_A_log"], inputs["L1_attn_Dp"])
    gn = run_l6a_gate(y, gate, inputs["L1_attn_gnorm_w"])
    s, H, zfm, lm = run_l6b_oproj(gn, inputs["L1_attn_out_proj_w"], xn1,
                                  inputs["L1_ffn_W"], inputs["L1_ffn_b"],
                                  inputs["L1_ffn_A"], inputs["L1_ffn_LiMEs"])
    out = run_l7_final(s, H, zfm, lm, x1, inputs["L1_ffn_LiMEs"], inputs["L1_ffn_Bmat"])
    return out.reshape(B, S, D).astype(np.float32)
